# version 02
# baseline (speedup 1.0000x reference)
import sys
sys.path.insert(0, "/opt/trn_rl_repo")
import math
import numpy as np
import ml_dtypes
import concourse.bass as bass
import concourse.mybir as mybir
from concourse.bass_utils import run_bass_kernel_spmd
from concourse.bass import IndirectOffsetOnAxis
from contextlib import ExitStack

F32 = mybir.dt.float32
BF16 = mybir.dt.bfloat16
I32 = mybir.dt.int32
U8 = mybir.dt.uint8
AF = mybir.ActivationFunctionType
ALU = mybir.AluOpType
AX = mybir.AxisListType
POOL = mybir.EngineType.Pool

B, S, D = 4, 8192, 1024
NCORES = 8
OWN = 4096
DCH = D // 128
LN_EPS = 1e-5

bf16 = ml_dtypes.bfloat16


def own_rows(slot):
    if slot == 0:
        return np.concatenate([np.arange(0, 2048), np.arange(6144, 8192)])
    return np.arange(2048, 6144)


class Prog:

    def __init__(self, name="prog"):
        self.nc = bass.Bass("TRN2", target_bir_lowering=False)
        self.es = ExitStack()
        self._semcnt = {}
        self._sems = {}
        self.block = None
        self._secs = {}

    def sem(self, name):
        if name not in self._sems:
            self._sems[name] = self.es.enter_context(self.nc.semaphore(name))
            self._semcnt[name] = 0
        return self._sems[name]

    def bump(self, name, n):
        self.sem(name)
        self._semcnt[name] += n
        return self._semcnt[name]

    def cnt(self, name):
        return self._semcnt.get(name, 0)

    def sb(self, name, shape, dtype):
        return self.es.enter_context(self.nc.sbuf_tensor(name, list(shape), dtype))

    def sbs(self, scope, name, shape, dtype):
        return scope.enter_context(self.nc.sbuf_tensor(name, list(shape), dtype))

    def ps(self, name, shape, dtype=F32):
        return self.es.enter_context(self.nc.psum_tensor(name, list(shape), dtype))

    def pss(self, scope, name, shape, dtype=F32):
        return scope.enter_context(self.nc.psum_tensor(name, list(shape), dtype))

    def din(self, name, shape, dtype):
        return self.nc.dram_tensor(name, list(shape), dtype, kind="ExternalInput").ap()

    def dout(self, name, shape, dtype):
        return self.nc.dram_tensor(name, list(shape), dtype, kind="ExternalOutput").ap()

    def dint(self, name, shape, dtype, shared=False):
        return self.nc.dram_tensor(
            name, list(shape), dtype, kind="Internal",
            addr_space="Shared" if shared else "Local").ap()

    def start_block(self):
        self.block = self.es.enter_context(self.nc.Block())

    def dma(self, eng, out, in_, sem):
        eng.dma_start(out=out, in_=in_).then_inc(self._sems[sem], 16)
        return self.bump(sem, 16)

    def idma(self, gp, out, out_idx, in_, in_idx, sem):
        kw = {}
        if out_idx is not None:
            kw["out_offset"] = IndirectOffsetOnAxis(ap=out_idx, axis=0)
            kw["in_offset"] = None
        else:
            kw["out_offset"] = None
            kw["in_offset"] = IndirectOffsetOnAxis(ap=in_idx, axis=0)
        gp.indirect_dma_start(out=out, in_=in_, oob_is_err=False, **kw).then_inc(
            self._sems[sem], 16)
        return self.bump(sem, 16)

    def inc(self, inst, sem, n=1):
        inst.then_inc(self._sems[sem], n)
        return self.bump(sem, n)



def np_bf16(x):
    return np.asarray(x).astype(bf16)


def identity_bf16():
    return np.eye(128, dtype=bf16)


def identity_f32():
    return np.eye(128, dtype=np.float32)




_join_counter = [0]


def emit_join(p, blk, jn=None, waits=None):
    SM = p._sems
    if jn is None:
        _join_counter[0] += 1
        jn = f"jn{_join_counter[0]}"
    p.sem(jn)
    wmap = {k: v for k, v in (waits or [])}

    def mk(engname, engsec):
        def body(eng):
            for sm, cnt in wmap.get(engname, []):
                eng.wait_ge(SM[sm], cnt)
            eng.nop().then_inc(SM[jn], 1)
        return body

    for engname in ("sync", "gpsimd", "scalar", "tensor", "vector"):
        sec = getattr(blk, engname)
        sec(mk(engname, None))
        p.bump(jn, 1)
    return (jn, p.cnt(jn))


"""Phase 1: stage-1 = LN + L0_attn + postLN + residual + ln2/norm/scores, TWO batches.

io keys: xs1 [2*S,D]f32, Wk1,Wq1,Wk2,Wproj [D,D]f32,
bk1c,bq1c,bk2c,bprojc [128,8]f32 (col j = bias[j*128:(j+1)*128]),
centTc [128,64]f32 (col j*8+e = centroids[e, j*128+p]), gB,bB [128,D]f32,
identb, identf, ln1h/oh [128, DCH*S]bf16 scratch,
x1out [S,D]f32, normout [S,D]bf16, scout [128,1024]f32 (row b*64+tile).
"""

NT1 = S // 128
NCT = S // 512
C2 = 6.25e-4
C2H = C2 / 2.0
CH = 6
PL = 2


def emit_p1(p, io):
    nc = p.nc
    blk = p.block

    identb = p.sb("p1_identb", [128, 128], BF16)
    identf = p.sb("p1_identf", [128, 128], F32)
    biases = p.sb("p1_biases", [128, 32], F32)
    epsc = p.sb("p1_epsc", [128, 1], F32)
    centT = p.sb("p1_centT", [128, 64], F32)
    wtmp = p.sb("p1_wtmp", [128, D], BF16)
    zcol = p.sb("p1_zcol", [128, 1], F32)
    onescF = p.sb("p1_onescF", [128, 1], F32)
    onescB = p.sb("p1_onescB", [128, 1], BF16)

    psTb = [p.ps(f"p1_psTb{i}", [128, 1024], BF16) for i in range(2)]
    psA = [p.ps(f"p1_psA{i}", [128, 512]) for i in range(4)]
    psR = p.ps("p1_psR", [128, 512])

    sems = ["ini", "wp", "wtr0", "wev0",
            "xt0", "xt1", "pv1", "ps1", "pv2", "ptl", "ple", "lns0", "lns1",
            "wldA", "wldB", "wtr", "wev", "lnr0", "lnr1", "lnc", "prj", "evS", "evV",
            "scK2", "scS", "scn", "chn", "pool", "ost",
            "old0", "old1", "pm", "evC", "sqC", "pstA", "mvA", "pstB", "vvC",
            "sq2C", "mvrC", "eT", "eV", "xe0", "xe1", "x1v", "x1s0", "x1s1",
            "lnq", "sq3", "nrm", "ntv", "scm", "sce", "scs", "nst", "ini2"]
    for s_ in sems:
        p.sem(s_)
    SM = p._sems

    ln1h_3d = io["ln1h"].rearrange("p (j t) -> p j t", t=S)
    oh_3d = io["oh"].rearrange("p (j t) -> p j t", t=S)

    @blk.sync
    def _(sync):
        for dst, src in ((identb[:], io["identb"]), (identf[:], io["identf"]),
                         (biases[:, 0:8], io["bk1c"]), (biases[:, 8:16], io["bq1c"]),
                         (biases[:, 16:24], io["bk2c"]), (biases[:, 24:32], io["bprojc"]),
                         (centT[:], io["centTc"])):
            sync.dma_start(out=dst, in_=src[:, :]).then_inc(SM["ini"], 16)
    p.bump("ini", 7 * 16)

    @blk.gpsimd
    def _(gp):
        gp.memset(zcol[:], 0.0)
        gp.memset(epsc[:], 1e-6)
        gp.memset(onescF[:], 1.0)
        gp.memset(onescB[:], 1.0).then_inc(SM["ini"], 1)
    p.bump("ini", 1)
    INI = p.cnt("ini")

    @blk.vector
    def _(ve):
        ve.wait_ge(SM["ini"], INI)
        ve.tensor_scalar(biases[:, 8:16], biases[:, 8:16], 0.5, None, ALU.mult)
        ve.tensor_scalar(biases[:, 16:24], biases[:, 16:24], C2H, None, ALU.mult)
        ve.drain()
        ve.nop().then_inc(SM["wev0"], 1)
    p.bump("wev0", 1)

    JN = emit_join(p, blk, waits=[("sync", [("ini", INI)]),
                                  ("vector", [("wev0", p.cnt("wev0"))])])

    for b in range(2):
        xsrc = io["xs1"][b * S:(b + 1) * S, :]

        scopeA = ExitStack()
        x0t = [p.sbs(scopeA, f"p1_x0t{i}_{b}", [128, D], F32) for i in range(2)]
        lnt = [p.sbs(scopeA, f"p1_lnt{i}_{b}", [128, D], BF16) for i in range(2)]
        bnsc = [p.sbs(scopeA, f"p1_bn{i}_{b}", [128, 16], F32) for i in range(2)]
        ln1s = [p.sbs(scopeA, f"p1_ln1s{i}_{b}", [128, DCH * 128], BF16) for i in range(2)]

        A = dict(ps1=p.cnt("ps1"), pv1=p.cnt("pv1"), pv2=p.cnt("pv2"),
                 ptl=p.cnt("ptl"), ple=p.cnt("ple"),
                 xt=[p.cnt("xt0"), p.cnt("xt1")],
                 lns=[p.cnt("lns0"), p.cnt("lns1")],
                 wev=p.cnt("wev"), wev0=p.cnt("wev0"), eV=p.cnt("eV"))

        @blk.scalar
        def _(sc, b=b, A=A, xsrc=xsrc, JN=JN):
            sc.wait_ge(SM[JN[0]], JN[1])
            for i in range(NT1):
                if i >= 2:
                    sc.wait_ge(SM["pv2"], A["pv2"] + i - 1)
                sc.dma_start(out=x0t[i % 2][:], in_=xsrc[i * 128:(i + 1) * 128, :]
                             ).then_inc(SM["xt0" if i % 2 == 0 else "xt1"], 16)
                sc.wait_ge(SM["pv1"], A["pv1"] + i + 1)
                sc.activation(out=bnsc[i % 2][:, 15:16], in_=bnsc[i % 2][:, 14:15],
                              func=AF.Sqrt).then_inc(SM["ps1"], 1)
                sc.wait_ge(SM["ple"], A["ple"] + i + 1)
                sc.dma_start(out=ln1h_3d[:, :, i * 128:(i + 1) * 128],
                             in_=ln1s[i % 2].ap().rearrange("p (j c) -> p j c", c=128)
                             ).then_inc(SM["lns0" if i % 2 == 0 else "lns1"], 16)

        @blk.tensor
        def _(te, A=A, JN=JN):
            te.wait_ge(SM[JN[0]], JN[1])
            for i in range(NT1):
                te.wait_ge(SM["pv2"], A["pv2"] + i + 1)
                if i >= 2:
                    te.wait_ge(SM["ple"], A["ple"] + i - 1)
                for j in range(DCH):
                    inst = te.transpose(psTb[i % 2][:, j * 128:(j + 1) * 128],
                                        lnt[i % 2][:, j * 128:(j + 1) * 128], identb[:])
                inst.then_inc(SM["ptl"], 1)

        @blk.vector
        def _(ve, A=A, JN=JN):
            ve.wait_ge(SM[JN[0]], JN[1])
            for i in range(NT1):
                ve.wait_ge(SM["xt0" if i % 2 == 0 else "xt1"],
                           A["xt"][i % 2] + 16 * (i // 2 + 1))
                sc_ = bnsc[i % 2]
                ve.bn_stats(sc_[:, 0:6], x0t[i % 2][:, 0:512])
                ve.bn_stats(sc_[:, 6:12], x0t[i % 2][:, 512:1024])
                ve.drain()
                ve.bn_aggr(sc_[:, 12:14], sc_[:, 0:12])
                ve.drain()
                ve.tensor_scalar(sc_[:, 14:15], sc_[:, 13:14], float(LN_EPS), None,
                                 ALU.add).then_inc(SM["pv1"], 1)
                ve.wait_ge(SM["ps1"], A["ps1"] + i + 1)
                ve.reciprocal(sc_[:, 14:15], sc_[:, 15:16])
                ve.drain()
                if i >= 2:
                    ve.wait_ge(SM["ptl"], A["ptl"] + i - 1)
                ve.tensor_scalar(lnt[i % 2][:], x0t[i % 2][:], sc_[:, 12:13],
                                 sc_[:, 14:15], ALU.subtract, ALU.mult
                                 ).then_inc(SM["pv2"], 1)
                ve.wait_ge(SM["ptl"], A["ptl"] + i + 1)
                if i >= 2:
                    ve.wait_ge(SM["lns0" if i % 2 == 0 else "lns1"],
                               A["lns"][i % 2] + 16 * (i // 2))
                ve.tensor_copy(ln1s[i % 2][:], psTb[i % 2][:, 0:DCH * 128]
                               ).then_inc(SM["ple"], 1)
            ve.drain()

        for s_, n_ in (("ps1", NT1), ("pv1", NT1), ("pv2", NT1), ("ptl", NT1),
                       ("ple", NT1), ("xt0", 16 * NT1 // 2), ("xt1", 16 * NT1 // 2),
                       ("lns0", 16 * NT1 // 2), ("lns1", 16 * NT1 // 2)):
            p.bump(s_, n_)

        JN = emit_join(p, blk, waits=[
            ("scalar", [("xt0", p.cnt("xt0")), ("xt1", p.cnt("xt1")),
                        ("lns0", p.cnt("lns0")), ("lns1", p.cnt("lns1"))]),
            ("vector", [("ple", p.cnt("ple"))]),
            ("tensor", [("ptl", p.cnt("ptl"))])])
        scopeA.close()

        scopeB = ExitStack()
        b1 = p.sbs(scopeB, f"p1_b1_{b}", [128, S], BF16)
        b2 = p.sbs(scopeB, f"p1_b2_{b}", [128, S], BF16)
        b3 = p.sbs(scopeB, f"p1_b3_{b}", [128, S], BF16)
        b4 = p.sbs(scopeB, f"p1_b4_{b}", [128, S], BF16)
        b5 = p.sbs(scopeB, f"p1_b5_{b}", [128, S], BF16)
        b6 = p.sbs(scopeB, f"p1_b6_{b}", [128, S], BF16)
        f1 = p.sbs(scopeB, f"p1_f1_{b}", [128, S], F32)
        f2 = p.sbs(scopeB, f"p1_f2_{b}", [128, S], F32)
        wk = [p.sbs(scopeB, f"p1_wk{i}_{b}", [128, D], BF16) for i in range(2)]
        wkT = [p.sbs(scopeB, f"p1_wkT{i}_{b}", [128, DCH * 128], BF16) for i in range(3)]
        ln1r = [p.sbs(scopeB, f"p1_ln1r{i}_{b}", [128, DCH * 512], BF16) for i in range(2)]

        C = dict(lnc=p.cnt("lnc"), lnr=[p.cnt("lnr0"), p.cnt("lnr1")],
                 wld=[p.cnt("wldA"), p.cnt("wldB")], wtr=p.cnt("wtr"), wev=p.cnt("wev"),
                 prj=p.cnt("prj"), evS=p.cnt("evS"), evV=p.cnt("evV"),
                 scK2=p.cnt("scK2"), scS=p.cnt("scS"), scn=p.cnt("scn"),
                 chn=p.cnt("chn"), pool=p.cnt("pool"), ost=p.cnt("ost"),
                 lns=[p.cnt("lns0"), p.cnt("lns1")], ple=p.cnt("ple"))

        @blk.sync
        def _(sync, C=C, JN=JN):
            sync.wait_ge(SM[JN[0]], JN[1])
            for e in range(DCH):
                if e >= 1:
                    sync.wait_ge(SM["pool"], C["pool"] + e * PL)
                sync.dma_start(out=b4[:], in_=ln1h_3d[:, e, :]).then_inc(SM["lnc"], 16)
                for ct in range(NCT):
                    g = e * NCT + ct
                    if g >= 2:
                        sync.wait_ge(SM["prj"], C["prj"] + (g - 1) * 3)
                    sync.dma_start(
                        out=ln1r[g % 2].ap().rearrange("p (j t) -> p j t", t=512),
                        in_=ln1h_3d[:, :, ct * 512:(ct + 1) * 512],
                    ).then_inc(SM["lnr0" if g % 2 == 0 else "lnr1"], 16)
                sync.wait_ge(SM["chn"], C["chn"] + e * CH + CH)
                sync.dma_start(out=oh_3d[:, e, :], in_=b1[:]).then_inc(SM["ost"], 16)

        @blk.gpsimd
        def _(gp, C=C, JN=JN):
            gp.wait_ge(SM[JN[0]], JN[1])
            for e in range(DCH):
                for wi, wnm in enumerate(("Wk1", "Wq1", "Wk2")):
                    gi = e * 3 + wi
                    if gi >= 2:
                        gp.wait_ge(SM["wtr"], C["wtr"] + gi - 1)
                    gp.dma_start(out=wk[gi % 2][:], in_=io[wnm][e * 128:(e + 1) * 128, :]
                                 ).then_inc(SM["wldA" if gi % 2 == 0 else "wldB"], 16)
                gp.wait_ge(SM["evS"], C["evS"] + e * 32 + 32)
                gp.wait_ge(SM["lnc"], C["lnc"] + 16 * (e + 1))
                if e >= 1:
                    gp.wait_ge(SM["scn"], C["scn"] + (e - 1) * 3 + 2)
                gp.tensor_tensor(b5[:], b1[:], b4[:], ALU.mult).then_inc(SM["pool"], 1)
                gp.wait_ge(SM["chn"], C["chn"] + e * CH + 5)
                gp.tensor_tensor(b4[:], b5[:], b1[:], ALU.mult).then_inc(SM["pool"], 1)

        @blk.tensor
        def _(te, C=C, JN=JN):
            te.wait_ge(SM[JN[0]], JN[1])
            for e in range(DCH):
                for wi in range(3):
                    gi = e * 3 + wi
                    te.wait_ge(SM["wldA" if gi % 2 == 0 else "wldB"],
                               C["wld"][gi % 2] + 16 * (gi // 2 + 1))
                    if gi >= 2:
                        te.wait_ge(SM["wev"], C["wev"] + gi - 1)
                    elif b == 0 and e == 0:
                        te.wait_ge(SM["ple"], C["ple"])
                    for j in range(DCH):
                        inst = te.transpose(psTb[gi % 2][:, j * 128:(j + 1) * 128],
                                            wk[gi % 2][:, j * 128:(j + 1) * 128], identb[:])
                    inst.then_inc(SM["wtr"], 1)
                for ct in range(NCT):
                    g = e * NCT + ct
                    te.wait_ge(SM["lnr0" if g % 2 == 0 else "lnr1"],
                               C["lnr"][g % 2] + 16 * (g // 2 + 1))
                    for wi in range(3):
                        gpi = e * 48 + ct * 3 + wi
                        te.wait_ge(SM["wev"], C["wev"] + e * 3 + wi + 1)
                        if gpi >= 4:
                            pg = gpi - 4
                            pwi = pg % 3
                            if pwi == 2:
                                te.wait_ge(SM["evV"], C["evV"] + pg // 3 + 1)
                            else:
                                te.wait_ge(SM["evS"],
                                           C["evS"] + 2 * (pg // 3) + pwi + 1)
                        for dj in range(DCH):
                            inst = te.matmul(
                                psA[gpi % 4][:],
                                wkT[wi][:, dj * 128:(dj + 1) * 128],
                                ln1r[g % 2][:, dj * 512:(dj + 1) * 512],
                                start=(dj == 0), stop=(dj == DCH - 1))
                        inst.then_inc(SM["prj"], 1)

        @blk.scalar
        def _(sc, C=C, JN=JN):
            sc.wait_ge(SM[JN[0]], JN[1])
            for e in range(DCH):
                if e >= 1:
                    sc.wait_ge(SM["ost"], C["ost"] + 16 * e)
                for ct in range(NCT):
                    base_p = C["prj"] + e * 48 + ct * 3
                    sc.wait_ge(SM["prj"], base_p + 1)
                    sc.activation(out=b1[:, ct * 512:(ct + 1) * 512],
                                  in_=psA[(ct * 3) % 4][:], func=AF.Exp,
                                  bias=biases[:, e:e + 1]).then_inc(SM["evS"], 1)
                    sc.wait_ge(SM["prj"], base_p + 2)
                    sc.activation(out=b2[:, ct * 512:(ct + 1) * 512],
                                  in_=psA[(ct * 3 + 1) % 4][:], func=AF.Tanh,
                                  scale=0.5, bias=biases[:, 8 + e:9 + e]
                                  ).then_inc(SM["evS"], 1)
                sc.wait_ge(SM["evV"], C["evV"] + e * 16 + 16)
                sc.wait_ge(SM["scn"], C["scn"] + e * 3 + 2)
                sc.wait_ge(SM["pool"], C["pool"] + e * PL + 1)
                sc.activation(out=b5[:, 1:S], in_=b3[:, 0:S - 1], func=AF.Tanh,
                              scale=C2H, bias=biases[:, 16 + e:17 + e])
                sc.drain()
                sc.activation(out=b5[:, 0:1], in_=zcol[:], func=AF.Tanh,
                              scale=C2H, bias=biases[:, 16 + e:17 + e]
                              ).then_inc(SM["scK2"], 1)
                sc.wait_ge(SM["scn"], C["scn"] + e * 3 + 1)
                sc.activation(out=f1[:], in_=f1[:], func=AF.Ln, bias=epsc[:, 0:1])
                sc.drain()
                sc.activation(out=f1[:], in_=f1[:], func=AF.Exp, scale=-1.0
                              ).then_inc(SM["scS"], 1)

        @blk.vector
        def _(ve, C=C, JN=JN):
            ve.wait_ge(SM[JN[0]], JN[1])
            for e in range(DCH):
                for wi in range(3):
                    gi = e * 3 + wi
                    ve.wait_ge(SM["wtr"], C["wtr"] + gi + 1)
                    ve.tensor_copy(wkT[wi][:], psTb[gi % 2][:, 0:DCH * 128]
                                   ).then_inc(SM["wev"], 1)
                ve.drain()
                for ct in range(NCT):
                    base_p = C["prj"] + e * 48 + ct * 3
                    ve.wait_ge(SM["prj"], base_p + 3)
                    if e >= 1 and ct == 0:
                        ve.wait_ge(SM["chn"], C["chn"] + (e - 1) * CH + 4)
                    ve.tensor_copy(b3[:, ct * 512:(ct + 1) * 512],
                                   psA[(ct * 3 + 2) % 4][:]).then_inc(SM["evV"], 1)
                ve.wait_ge(SM["evS"], C["evS"] + e * 32 + 32)
                ve.tensor_scalar(b2[:], b2[:], 0.5, 0.5, ALU.mult, ALU.add
                                 ).then_inc(SM["chn"], 1)
                ve.drain()
                if e >= 1:
                    ve.wait_ge(SM["chn"], C["chn"] + (e - 1) * CH + 3)
                ve.tensor_tensor_scan(f1[:], b1[:], b1[:], 0.0, ALU.add, ALU.bypass
                                      ).then_inc(SM["scn"], 1)
                ve.drain()
                ve.wait_ge(SM["pool"], C["pool"] + e * PL + 1)
                ve.tensor_tensor_scan(f2[:], b5[:], b5[:], 0.0, ALU.add, ALU.bypass
                                      ).then_inc(SM["scn"], 1)
                ve.drain()
                ve.wait_ge(SM["scK2"], C["scK2"] + e + 1)
                ve.tensor_scalar(b5[:], b5[:], 0.5, 0.5, ALU.mult, ALU.add
                                 ).then_inc(SM["chn"], 1)
                ve.drain()
                ve.wait_ge(SM["scS"], C["scS"] + e + 1)
                ve.tensor_tensor(b3[:], f2[:], f1[:], ALU.mult).then_inc(SM["chn"], 1)
                ve.drain()
                ve.tensor_tensor(b6[:], b3[:], b2[:], ALU.mult).then_inc(SM["chn"], 1)
                ve.drain()
                ve.memset(b1[:, 0:1], 0.0)
                ve.drain()
                ve.tensor_tensor(b1[:, 1:S], b4[:, 1:S], b6[:, 0:S - 1], ALU.subtract
                                 ).then_inc(SM["chn"], 1)
                ve.drain()
                ve.wait_ge(SM["pool"], C["pool"] + e * PL + 2)
                ve.tensor_tensor_scan(b5[:], b4[:], b4[:], 0.0, ALU.add, ALU.bypass
                                      ).then_inc(SM["scn"], 1)
                ve.drain()
                ve.memset(b1[:, 0:1], 0.0)
                ve.drain()
                ve.scalar_tensor_tensor(b1[:, 1:S], b5[:, 1:S], C2, b2[:, 0:S - 1],
                                        ALU.mult, ALU.mult)
                ve.drain()
                ve.tensor_tensor(b1[:], b1[:], b6[:], ALU.add).then_inc(SM["chn"], 1)
                ve.drain()

        for s_, n_ in (("lnc", 16 * DCH), ("lnr0", 16 * 64), ("lnr1", 16 * 64),
                       ("wldA", 16 * 12), ("wldB", 16 * 12), ("wtr", 24), ("wev", 24), ("prj", 3 * 128),
                       ("evS", 32 * DCH), ("evV", 16 * DCH), ("scK2", DCH),
                       ("scS", DCH), ("scn", 3 * DCH), ("chn", CH * DCH),
                       ("pool", PL * DCH), ("ost", 16 * DCH)):
            p.bump(s_, n_)

        JN = emit_join(p, blk, waits=[
            ("sync", [("lnc", p.cnt("lnc")), ("lnr0", p.cnt("lnr0")),
                      ("lnr1", p.cnt("lnr1")), ("ost", p.cnt("ost"))]),
            ("gpsimd", [("wldA", p.cnt("wldA")), ("wldB", p.cnt("wldB"))]),
            ("vector", [("chn", p.cnt("chn"))])])
        scopeB.close()

        scopeC = ExitStack()
        WpT = p.sbs(scopeC, f"p1_WpT_{b}", [128, 64 * 128], BF16)
        gB = p.sbs(scopeC, f"p1_gB_{b}", [128, D], F32)
        bB = p.sbs(scopeC, f"p1_bB_{b}", [128, D], F32)
        oct_ = [p.sbs(scopeC, f"p1_oct{i}_{b}", [128, DCH * 512], BF16) for i in range(2)]
        attn = p.sbs(scopeC, f"p1_attn_{b}", [128, DCH * 512], F32)
        atsq = p.sbs(scopeC, f"p1_atsq_{b}", [128, DCH * 512], BF16)
        murow = p.sbs(scopeC, f"p1_murow_{b}", [1, 512], F32)
        varow = p.sbs(scopeC, f"p1_varow_{b}", [1, 3 * 512], F32)
        mucol = p.sbs(scopeC, f"p1_mucol_{b}", [128, 4], F32)
        rstcol = p.sbs(scopeC, f"p1_rstcol_{b}", [128, 4], F32)
        ytm = p.sbs(scopeC, f"p1_ytm_{b}", [128, D], F32)
        x0e = [p.sbs(scopeC, f"p1_x0e{i}_{b}", [128, D], F32) for i in range(2)]
        x1t = [p.sbs(scopeC, f"p1_x1t{i}_{b}", [128, D], F32) for i in range(2)]
        ln2f = p.sbs(scopeC, f"p1_ln2f_{b}", [128, D], F32)
        bnsc2 = p.sbs(scopeC, f"p1_bn2_{b}", [128, 16], F32)
        normf = p.sbs(scopeC, f"p1_normf_{b}", [128, D], F32)
        nfm = p.sbs(scopeC, f"p1_nfm_{b}", [128, D], F32)
        scrow = p.sbs(scopeC, f"p1_scrow_{b}", [128, 8], F32)

        W_ = dict(wp=p.cnt("wp"), wtr0=p.cnt("wtr0"), wev0=p.cnt("wev0"))

        @blk.sync
        def _(sync, JN=JN):
            sync.wait_ge(SM[JN[0]], JN[1])
            sync.dma_start(out=gB[:], in_=io["gB"][:, :]).then_inc(SM["ini2"], 16)
            sync.dma_start(out=bB[:], in_=io["bB"][:, :]).then_inc(SM["ini2"], 16)
        p.bump("ini2", 32)
        INI2 = p.cnt("ini2")

        @blk.gpsimd
        def _(gp, JN=JN, W_=W_):
            gp.wait_ge(SM[JN[0]], JN[1])
            for i in range(8):
                if i >= 1:
                    gp.wait_ge(SM["wtr0"], W_["wtr0"] + i)
                gp.dma_start(out=wtmp[:], in_=io["Wproj"][i * 128:(i + 1) * 128, :]
                             ).then_inc(SM["wp"], 16)

        @blk.tensor
        def _(te, JN=JN, W_=W_):
            te.wait_ge(SM[JN[0]], JN[1])
            for i in range(8):
                te.wait_ge(SM["wp"], W_["wp"] + 16 * (i + 1))
                if i >= 2:
                    te.wait_ge(SM["wev0"], W_["wev0"] + i - 1)
                for j in range(DCH):
                    inst = te.transpose(psTb[i % 2][:, j * 128:(j + 1) * 128],
                                        wtmp[:, j * 128:(j + 1) * 128], identb[:])
                inst.then_inc(SM["wtr0"], 1)

        @blk.vector
        def _(ve, JN=JN, W_=W_):
            ve.wait_ge(SM[JN[0]], JN[1])
            for i in range(8):
                ve.wait_ge(SM["wtr0"], W_["wtr0"] + i + 1)
                ve.tensor_copy(
                    WpT.ap().rearrange("p (j i c) -> p j i c", i=8, c=128)[:, :, i, :],
                    psTb[i % 2].ap().rearrange("p (j c) -> p j c", c=128),
                ).then_inc(SM["wev0"], 1)
            ve.drain()
        p.bump("wp", 8 * 16)
        p.bump("wtr0", 8)
        p.bump("wev0", 8)

        E = dict(old=[p.cnt("old0"), p.cnt("old1")], pm=p.cnt("pm"),
                 evC=p.cnt("evC"), sqC=p.cnt("sqC"), pstA=p.cnt("pstA"),
                 mvA=p.cnt("mvA"), pstB=p.cnt("pstB"), vvC=p.cnt("vvC"),
                 sq2C=p.cnt("sq2C"), mvrC=p.cnt("mvrC"), eT=p.cnt("eT"),
                 eV=p.cnt("eV"), xe=[p.cnt("xe0"), p.cnt("xe1")],
                 x1v=p.cnt("x1v"), x1s=[p.cnt("x1s0"), p.cnt("x1s1")],
                 lnq=p.cnt("lnq"), sq3=p.cnt("sq3"), nrm=p.cnt("nrm"),
                 ntv=p.cnt("ntv"), scm=p.cnt("scm"),
                 sce=p.cnt("sce"), scs=p.cnt("scs"), nst=p.cnt("nst"),
                 ost=p.cnt("ost"), prj=p.cnt("prj"))

        @blk.sync
        def _(sync, b=b, E=E, JN=JN):
            sync.wait_ge(SM[JN[0]], JN[1])
            for ct in range(NCT):
                if ct >= 2:
                    sync.wait_ge(SM["pm"], E["pm"] + (ct - 1) * 8)
                sync.dma_start(
                    out=oct_[ct % 2].ap().rearrange("p (j t) -> p j t", t=512),
                    in_=oh_3d[:, :, ct * 512:(ct + 1) * 512],
                ).then_inc(SM["old0" if ct % 2 == 0 else "old1"], 16)
                if b == 0:
                    for q in range(4):
                        gq = ct * 4 + q
                        sync.wait_ge(SM["x1v"], E["x1v"] + gq + 1)
                        sync.dma_start(
                            out=io["x1out"][ct * 512 + q * 128: ct * 512 + (q + 1) * 128, :],
                            in_=x1t[gq % 2][:],
                        ).then_inc(SM["x1s0" if gq % 2 == 0 else "x1s1"], 16)

        @blk.scalar
        def _(sc, b=b, E=E, xsrc=xsrc, JN=JN):
            sc.wait_ge(SM[JN[0]], JN[1])
            for ct in range(NCT):
                sc.wait_ge(SM["evC"], E["evC"] + ct * 8 + 8)
                sc.activation(out=atsq[:], in_=attn[:], func=AF.Square
                              ).then_inc(SM["sqC"], 1)
                sc.wait_ge(SM["vvC"], E["vvC"] + ct + 1)
                sc.activation(out=varow[0:1, 512:1024], in_=varow[0:1, 0:512],
                              func=AF.Sqrt).then_inc(SM["sq2C"], 1)
                for q in range(4):
                    gq = ct * 4 + q
                    if gq >= 2:
                        sc.wait_ge(SM["x1v"], E["x1v"] + gq - 1)
                    sc.dma_start(out=x0e[gq % 2][:],
                                 in_=xsrc[ct * 512 + q * 128: ct * 512 + (q + 1) * 128, :]
                                 ).then_inc(SM["xe0" if gq % 2 == 0 else "xe1"], 16)
                    sc.wait_ge(SM["lnq"], E["lnq"] + gq + 1)
                    sc.activation(out=bnsc2[:, 15:16], in_=bnsc2[:, 14:15],
                                  func=AF.Sqrt).then_inc(SM["sq3"], 1)

        @blk.gpsimd
        def _(gp, b=b, E=E, JN=JN):
            gp.wait_ge(SM[JN[0]], JN[1])
            for ct in range(NCT):
                for q in range(4):
                    gq = ct * 4 + q
                    if b == 0:
                        gp.wait_ge(SM["nrm"], E["nrm"] + gq + 1)
                        gp.dma_start(
                            out=io["normout"][ct * 512 + q * 128: ct * 512 + (q + 1) * 128, :],
                            in_=normf[:]).then_inc(SM["nst"], 16)
                    gp.wait_ge(SM["sce"], E["sce"] + gq + 1)
                    gp.dma_start(
                        out=io["scout"][b * 64 + gq: b * 64 + gq + 1, :]
                        .rearrange("r (t e) -> (r t) e", e=8),
                        in_=scrow[:],
                    ).then_inc(SM["scs"], 16)

        @blk.tensor
        def _(te, E=E, JN=JN):
            te.wait_ge(SM[JN[0]], JN[1])
            te.wait_ge(SM["wev0"], p.cnt("wev0"))
            for ct in range(NCT):
                te.wait_ge(SM["old0" if ct % 2 == 0 else "old1"],
                           E["old"][ct % 2] + 16 * (ct // 2 + 1))
                for ech in range(8):
                    gm = ct * 8 + ech
                    if gm >= 2:
                        te.wait_ge(SM["evC"], E["evC"] + gm - 1)
                    for dj in range(DCH):
                        inst = te.matmul(
                            psA[gm % 2][:],
                            WpT[:, (dj * 8 + ech) * 128:(dj * 8 + ech + 1) * 128],
                            oct_[ct % 2][:, dj * 512:(dj + 1) * 512],
                            start=(dj == 0), stop=(dj == DCH - 1))
                    inst.then_inc(SM["pm"], 1)
                te.wait_ge(SM["evC"], E["evC"] + ct * 8 + 8)
                if ct >= 1:
                    te.wait_ge(SM["sce"], E["sce"] + ct * 4)
                for dj in range(DCH):
                    inst = te.matmul(psR[0:1, :], onescF[:, 0:1],
                                     attn[:, dj * 512:(dj + 1) * 512],
                                     start=(dj == 0), stop=(dj == DCH - 1))
                inst.then_inc(SM["pstA"], 1)
                te.wait_ge(SM["mvA"], E["mvA"] + ct + 1)
                te.wait_ge(SM["sqC"], E["sqC"] + ct + 1)
                for dj in range(DCH):
                    inst = te.matmul(psR[0:1, :], onescB[:, 0:1],
                                     atsq[:, dj * 512:(dj + 1) * 512],
                                     start=(dj == 0), stop=(dj == DCH - 1))
                inst.then_inc(SM["pstB"], 1)
                te.wait_ge(SM["mvrC"], E["mvrC"] + ct + 1)
                for k in range(8):
                    j = ct * 72 + k
                    bnk = psA[2 + (j % 2)]
                    if j >= 2:
                        te.wait_ge(SM["eV"], E["eV"] + j - 1)
                    q = k // 2
                    if k % 2 == 0:
                        src = murow[0:1, q * 128:(q + 1) * 128]
                    else:
                        src = varow[0:1, 1024 + q * 128: 1024 + (q + 1) * 128]
                    te.transpose(bnk[:, 0:1], src, identf[0:1, 0:1]
                                 ).then_inc(SM["eT"], 1)
                for q in range(4):
                    gq = ct * 4 + q
                    for i in range(8):
                        j = ct * 72 + 8 + 16 * q + i
                        bnk = psA[2 + (j % 2)]
                        te.wait_ge(SM["eV"], E["eV"] + j - 1)
                        te.transpose(bnk[:, 0:128],
                                     attn[:, i * 512 + q * 128: i * 512 + (q + 1) * 128],
                                     identf[:]).then_inc(SM["eT"], 1)
                    te.wait_ge(SM["nrm"], E["nrm"] + gq + 1)
                    for dj in range(DCH):
                        j = ct * 72 + 16 + 16 * q + dj
                        bnk = psA[2 + (j % 2)]
                        te.wait_ge(SM["eV"], E["eV"] + j - 1)
                        te.transpose(bnk[:, 0:128],
                                     normf[:, dj * 128:(dj + 1) * 128], identf[:]
                                     ).then_inc(SM["eT"], 1)
                    te.wait_ge(SM["eV"], E["eV"] + ct * 72 + 16 * q + 24)
                    if gq >= 1:
                        te.wait_ge(SM["sce"], E["sce"] + gq)
                    for dj in range(DCH):
                        inst = te.matmul(psR[0:128, 0:8], nfm[:, dj * 128:(dj + 1) * 128],
                                         centT[:, dj * 8:(dj + 1) * 8],
                                         start=(dj == 0), stop=(dj == DCH - 1))
                    inst.then_inc(SM["scm"], 1)

        @blk.vector
        def _(ve, b=b, E=E, JN=JN, INI2=INI2):
            ve.wait_ge(SM[JN[0]], JN[1])
            ve.wait_ge(SM["ini2"], INI2)
            for ct in range(NCT):
                for ech in range(8):
                    gm = ct * 8 + ech
                    ve.wait_ge(SM["pm"], E["pm"] + gm + 1)
                    if ct >= 1 and ech == 0:
                        ve.wait_ge(SM["eT"], E["eT"] + (ct - 1) * 72 + 64)
                        ve.wait_ge(SM["pstB"], E["pstB"] + ct)
                        ve.wait_ge(SM["sqC"], E["sqC"] + ct)
                    ve.tensor_scalar(attn[:, ech * 512:(ech + 1) * 512],
                                     psA[gm % 2][:], biases[:, 24 + ech:25 + ech],
                                     None, ALU.add).then_inc(SM["evC"], 1)
                ve.drain()
                ve.wait_ge(SM["pstA"], E["pstA"] + ct + 1)
                ve.tensor_scalar(murow[0:1, :], psR[0:1, :], 1.0 / D, None, ALU.mult
                                 ).then_inc(SM["mvA"], 1)
                ve.drain()
                ve.wait_ge(SM["pstB"], E["pstB"] + ct + 1)
                ve.tensor_scalar(varow[0:1, 1024:1536], psR[0:1, :], 1.0 / D, None,
                                 ALU.mult)
                ve.drain()
                ve.tensor_tensor(varow[0:1, 0:512], murow[0:1, :], murow[0:1, :],
                                 ALU.mult)
                ve.drain()
                ve.tensor_tensor(varow[0:1, 0:512], varow[0:1, 1024:1536],
                                 varow[0:1, 0:512], ALU.subtract)
                ve.drain()
                ve.tensor_scalar(varow[0:1, 0:512], varow[0:1, 0:512], float(LN_EPS),
                                 None, ALU.add).then_inc(SM["vvC"], 1)
                ve.wait_ge(SM["sq2C"], E["sq2C"] + ct + 1)
                ve.reciprocal(varow[0:1, 1024:1536], varow[0:1, 512:1024]
                              ).then_inc(SM["mvrC"], 1)
                ve.drain()
                for k in range(8):
                    j = ct * 72 + k
                    ve.wait_ge(SM["eT"], E["eT"] + j + 1)
                    bnk = psA[2 + (j % 2)]
                    q = k // 2
                    dst = mucol if k % 2 == 0 else rstcol
                    ve.tensor_copy(dst[:, q:q + 1], bnk[:, 0:1]).then_inc(SM["eV"], 1)
                ve.drain()
                for q in range(4):
                    gq = ct * 4 + q
                    for i in range(8):
                        j = ct * 72 + 8 + 16 * q + i
                        ve.wait_ge(SM["eT"], E["eT"] + j + 1)
                        bnk = psA[2 + (j % 2)]
                        ve.tensor_scalar(ytm[:, i * 128:(i + 1) * 128], bnk[:, 0:128],
                                         mucol[:, q:q + 1], rstcol[:, q:q + 1],
                                         ALU.subtract, ALU.mult).then_inc(SM["eV"], 1)
                    ve.drain()
                    ve.wait_ge(SM["xe0" if gq % 2 == 0 else "xe1"],
                               E["xe"][gq % 2] + 16 * (gq // 2 + 1))
                    if b == 0 and gq >= 2:
                        ve.wait_ge(SM["x1s0" if gq % 2 == 0 else "x1s1"],
                                   E["x1s"][gq % 2] + 16 * (gq // 2))
                    ve.tensor_tensor(x1t[gq % 2][:], ytm[:], x0e[gq % 2][:],
                                     ALU.add).then_inc(SM["x1v"], 1)
                    ve.drain()
                    ve.bn_stats(bnsc2[:, 0:6], x1t[gq % 2][:, 0:512])
                    ve.bn_stats(bnsc2[:, 6:12], x1t[gq % 2][:, 512:1024])
                    ve.drain()
                    ve.bn_aggr(bnsc2[:, 12:14], bnsc2[:, 0:12])
                    ve.drain()
                    ve.tensor_scalar(bnsc2[:, 14:15], bnsc2[:, 13:14],
                                     float(LN_EPS), None, ALU.add
                                     ).then_inc(SM["lnq"], 1)
                    ve.wait_ge(SM["sq3"], E["sq3"] + gq + 1)
                    ve.reciprocal(bnsc2[:, 14:15], bnsc2[:, 15:16])
                    ve.drain()
                    ve.tensor_scalar(ln2f[:], x1t[gq % 2][:], bnsc2[:, 12:13],
                                     bnsc2[:, 14:15], ALU.subtract, ALU.mult)
                    ve.drain()
                    if b == 0 and gq >= 1:
                        ve.wait_ge(SM["nst"], E["nst"] + 16 * gq)
                    if q >= 1:
                        ve.wait_ge(SM["eT"], E["eT"] + ct * 72 + 16 * q + 8)
                    elif ct >= 1:
                        ve.wait_ge(SM["eT"], E["eT"] + ct * 72)
                    ve.tensor_tensor(normf[:], ln2f[:], gB[:], ALU.mult)
                    ve.drain()
                    ve.tensor_tensor(normf[:], normf[:], bB[:], ALU.add
                                     ).then_inc(SM["nrm"], 1)
                    ve.drain()
                    for dj in range(DCH):
                        j = ct * 72 + 16 + 16 * q + dj
                        ve.wait_ge(SM["eT"], E["eT"] + j + 1)
                        bnk = psA[2 + (j % 2)]
                        ve.tensor_copy(nfm[:, dj * 128:(dj + 1) * 128],
                                       bnk[:, 0:128]).then_inc(SM["eV"], 1)
                    ve.drain()
                    ve.wait_ge(SM["scm"], E["scm"] + gq + 1)
                    if gq >= 1:
                        ve.wait_ge(SM["scs"], E["scs"] + 16 * gq)
                    ve.tensor_copy(scrow[:], psR[0:128, 0:8]).then_inc(SM["sce"], 1)
                    ve.drain()

        for s_, n_ in (("old0", 16 * 8), ("old1", 16 * 8), ("pm", 8 * NCT),
                       ("evC", 8 * NCT), ("sqC", NCT), ("pstA", NCT), ("mvA", NCT),
                       ("pstB", NCT), ("vvC", NCT), ("sq2C", NCT), ("mvrC", NCT),
                       ("eT", 72 * NCT), ("eV", 72 * NCT),
                       ("xe0", 16 * 32), ("xe1", 16 * 32), ("x1v", 64),
                       ("lnq", 64), ("sq3", 64), ("nrm", 64),
                       ("scm", 64), ("sce", 64), ("scs", 16 * 64)):
            p.bump(s_, n_)
        if b == 0:
            p.bump("x1s0", 16 * 32)
            p.bump("x1s1", 16 * 32)
            p.bump("nst", 16 * 64)

        JN = emit_join(p, blk, waits=[
            ("sync", [("old0", p.cnt("old0")), ("old1", p.cnt("old1")),
                      ("x1s0", p.cnt("x1s0")), ("x1s1", p.cnt("x1s1"))]),
            ("scalar", [("xe0", p.cnt("xe0")), ("xe1", p.cnt("xe1"))]),
            ("gpsimd", [("nst", p.cnt("nst")), ("scs", p.cnt("scs"))])])
        scopeC.close()


"""Phase 2: L0_ffn routing (replicated) + my-4-experts FFN + sandwich epilogue.

Routing: greedy balanced assignment over 32768 scores (pair-local batch order,
my batch = big-rows 0..63). Bisection on threshold per expert (22 iters,
DVE + gpsimd partition_all_reduce), exact-capacity band fixup, pos[t] slots.
FFN: capacity 1280/(expert,batch), tiles (512,512,256); w1my/w2my resident bf16.

io: scoresF [512,1024]f32, normH [S,D]bf16, x1own [OWN,D]f32,
w1my [4*D,4096]f32, w2my [4*4096,D]f32, b1c [128,128]f32, b2c [128,32]f32,
gidx [128,40]i32 (=4*slot*1280 + arange(5120) host), ownrows [128,32]i32,
tokid [128,64]i32, identb/identf, S_h [10240,1]i32, posH [S,1]i32,
yH [10240,D]bf16, x2out [OWN,D]f32.
"""

CAP = 2560
CAPC = CAP // 128
TT_E = (512,) * (CAP // 512)
CGLOB = 4096.0
BISECT = 22
BIGF = 10240.0
GELC = 0.044715
GELS = 0.7978845608028654


def emit_p2(p, io, JN=None, nexp=4, dbg=None):
    nc = p.nc
    blk = p.block
    if dbg is not None:
        p.sem("dbgv")
        p.sem("dbgs")

    identb = p.sb("p2_identb", [128, 128], BF16)
    identf = p.sb("p2_identf", [128, 128], F32)
    onescF = p.sb("p2_onescF", [128, 1], F32)
    ones1f = p.sb("p2_ones1f", [1, 128], F32)
    onescB = p.sb("p2_onescB", [128, 1], BF16)

    psA = [p.ps(f"p2_psA{i}", [128, 512]) for i in range(4)]
    psR = p.ps("p2_psR", [128, 512])
    psT = p.ps("p2_psT", [128, 1024], BF16)
    psTf = p.ps("p2_psTf", [128, 512])

    sems = ["ini", "cntv", "rpe", "cv2", "rpt", "rpv", "pcast", "sbig", "spos", "phv",
            "tld", "gld", "gno", "ntr", "nte", "w1l", "w2l", "bld", "pm1", "ev1",
            "gel", "pm2", "ev2", "ytr", "yte", "yst0", "yst1",
            "opg", "xld0", "xld1", "ygt0", "ygt1", "fv", "sv", "x2v",
            "x2s0", "x2s1", "lnp", "sqp", "lnp2", "sqp2"]
    for s_ in sems:
        p.sem(s_)
    SM = p._sems

    scR = ExitStack()
    score = p.sbs(scR, "p2_score", [128, 2048], F32)
    avail = p.sbs(scR, "p2_avail", [128, 256], F32)
    gwork = p.sbs(scR, "p2_gwork", [128, 256], F32)
    band = p.sbs(scR, "p2_band", [128, 256], F32)
    base_ = p.sbs(scR, "p2_base", [128, 256], F32)
    pscanF = p.sbs(scR, "p2_pscanF", [128, 256], F32)
    selmy = p.sbs(scR, "p2_selmy", [128, 64], F32)
    pscanM = p.sbs(scR, "p2_pscanM", [128, 64], F32)
    posf = p.sbs(scR, "p2_posf", [128, 64], F32)
    posi = p.sbs(scR, "p2_posi", [128, 64], I32)
    toki = p.sbs(scR, "p2_toki", [128, 64], I32)
    sbig = p.sbs(scR, "p2_sbig", [128, (8 * CAP + 128) // 128], I32)
    cntc = p.sbs(scR, "p2_cntc", [128, 4], F32)
    cntR = p.sbs(scR, "p2_cntR", [128, 4], F32)
    masku = p.sbs(scR, "p2_masku", [128, 64], U8)
    sca = p.sbs(scR, "p2_sca", [128, 16], F32)
    predu = p.sbs(scR, "p2_predu", [128, 2], U8)
    rowsc = p.sbs(scR, "p2_rowsc", [1, 256], F32)
    if dbg is not None:
        dbg["bandpre"] = p.sbs(scR, "p2_bandpre", [128, 256], F32)
        dbg["gworkc"] = p.sbs(scR, "p2_gworkc", [128, 256], F32)

    if JN is not None:
        pass

    @blk.sync
    def _(sync):
        if JN is not None:
            sync.wait_ge(SM[JN[0]], JN[1])
        sync.dma_start(out=identb[:], in_=io["identb"][:, :]).then_inc(SM["ini"], 16)
        sync.dma_start(out=identf[:], in_=io["identf"][:, :]).then_inc(SM["ini"], 16)
        sync.dma_start(out=score.ap().rearrange("p (r e) -> p r e", e=8),
                       in_=io["scoresF"].rearrange("r (p e) -> p r e", e=8)
                       ).then_inc(SM["ini"], 16)
        sync.dma_start(out=toki[:], in_=io["tokid"][:, :]).then_inc(SM["ini"], 16)
    p.bump("ini", 4 * 16)

    @blk.gpsimd
    def _(gp):
        if JN is not None:
            gp.wait_ge(SM[JN[0]], JN[1])
        gp.memset(avail[:], 1.0)
        gp.memset(posf[:], BIGF)
        gp.memset(onescF[:], 1.0)
        gp.memset(onescB[:], 1.0)
        gp.memset(sbig[:], 8192)
        gp.memset(ones1f[:], 1.0)
        gp.drain()
        gp.engine_nop().then_inc(SM["ini"], 1)
        gp.dma_start(out=io["S_h"].rearrange("(a b) one -> a (b one)", a=128),
                     in_=sbig[:]).then_inc(SM["sbig"], 16)
        gp.wait_ge(SM["pcast"], 1)
        gp.wait_ge(SM["sbig"], 16)
        for c in range(64):
            if c >= 1:
                gp.wait_ge(SM["spos"], 16 * c)
            gp.indirect_dma_start(
                out=io["S_h"][:, :],
                out_offset=IndirectOffsetOnAxis(ap=posi[:, c:c + 1], axis=0),
                in_=toki[:, c:c + 1], in_offset=None, oob_is_err=False,
            ).then_inc(SM["spos"], 16)
    p.bump("ini", 1)
    p.bump("sbig", 16)
    p.bump("spos", 16 * 64)
    INI = p.cnt("ini")

    @blk.tensor
    def _(te):
        te.wait_ge(SM["ini"], INI)
        for e in range(8):
            cbase = e * (BISECT + 2)
            for it in range(BISECT + 2):
                src = cntc[:, 0:1] if it < BISECT else (
                    cntc[:, 1:2] if it == BISECT else cntc[:, 2:3])
                te.wait_ge(SM["cntv"], cbase + it + 1)
                te.matmul(psR[0:1, 0:1], src, onescF[:, 0:1],
                          start=True, stop=True).then_inc(SM["rpe"], 1)
                te.wait_ge(SM["cv2"], cbase + it + 1)
                te.matmul(psR[0:128, 4:5], ones1f[0:1, :], sca[0:1, 3:4],
                          start=True, stop=True).then_inc(SM["rpe"], 1)
            for d in range(2):
                rb = e * 4 + d * 2
                te.wait_ge(SM["rpv"], rb + 1)
                te.transpose(psTf[0:1, 0:128], sca[:, 9:10], identf[:]
                             ).then_inc(SM["rpt"], 1)
                te.wait_ge(SM["rpv"], rb + 2)
                te.transpose(psTf[0:128, 128:129], rowsc[0:1, 0:128],
                             identf[0:1, 0:1]).then_inc(SM["rpt"], 1)
    p.bump("rpt", 32)
    p.bump("rpe", 8 * 2 * (BISECT + 2))

    @blk.vector
    def _(ve):
        ve.wait_ge(SM["ini"], INI)
        if dbg is not None:
            ve.memset(sca[:], 0.0)
            ve.drain()
        for e in range(8):
            ve.memset(sca[:, 0:1], -64.0)
            ve.memset(sca[:, 1:2], 64.0)
            ve.memset(sca[:, 2:3], 0.0)
            ve.drain()
            sc_e = score.ap().rearrange("p (r e) -> p r e", e=8)[:, :, e]
            pbase = e * (BISECT + 2)
            rbase = e * 2 * (BISECT + 2)
            for it in range(BISECT):
                ve.scalar_tensor_tensor(gwork[:], sc_e, sca[:, 2:3], avail[:],
                                        ALU.is_ge, ALU.mult,
                                        accum_out=cntc[:, 0:1]).then_inc(SM["cntv"], 1)
                ve.wait_ge(SM["rpe"], rbase + 2 * it + 1)
                ve.tensor_copy(sca[0:1, 3:4], psR[0:1, 0:1])
                ve.drain()
                ve.nop().then_inc(SM["cv2"], 1)
                ve.wait_ge(SM["rpe"], rbase + 2 * it + 2)
                ve.tensor_copy(sca[:, 10:11], psR[:, 4:5])
                ve.drain()
                ve.tensor_scalar(predu[:, 0:1], sca[:, 10:11], CGLOB, None, ALU.is_ge)
                ve.drain()
                ve.select(sca[:, 13:14], predu[:, 0:1], sca[:, 2:3], sca[:, 0:1], add_drain=True)
                ve.select(sca[:, 14:15], predu[:, 0:1], sca[:, 1:2], sca[:, 2:3], add_drain=True)
                ve.drain()
                ve.tensor_copy(sca[:, 0:1], sca[:, 13:14])
                ve.tensor_copy(sca[:, 1:2], sca[:, 14:15])
                ve.drain()
                ve.tensor_tensor(sca[:, 2:3], sca[:, 0:1], sca[:, 1:2], ALU.add)
                ve.drain()
                ve.tensor_scalar(sca[:, 2:3], sca[:, 2:3], 0.5, None, ALU.mult)
                ve.drain()
            ve.scalar_tensor_tensor(base_[:], sc_e, sca[:, 0:1], avail[:],
                                    ALU.is_ge, ALU.mult,
                                    accum_out=cntc[:, 1:2]).then_inc(SM["cntv"], 1)
            ve.wait_ge(SM["rpe"], rbase + 2 * BISECT + 1)
            ve.tensor_copy(sca[0:1, 3:4], psR[0:1, 0:1])
            ve.drain()
            ve.nop().then_inc(SM["cv2"], 1)
            ve.wait_ge(SM["rpe"], rbase + 2 * BISECT + 2)
            ve.tensor_copy(sca[:, 11:12], psR[:, 4:5])
            ve.drain()
            ve.scalar_tensor_tensor(band[:], sc_e, sca[:, 1:2], base_[:],
                                    ALU.is_lt, ALU.mult,
                                    accum_out=cntc[:, 2:3]).then_inc(SM["cntv"], 1)
            ve.wait_ge(SM["rpe"], rbase + 2 * BISECT + 3)
            ve.tensor_copy(sca[0:1, 3:4], psR[0:1, 0:1])
            ve.drain()
            ve.nop().then_inc(SM["cv2"], 1)
            ve.wait_ge(SM["rpe"], rbase + 2 * BISECT + 4)
            ve.tensor_copy(sca[:, 12:13], psR[:, 4:5])
            ve.drain()
            ve.tensor_tensor(base_[:], base_[:], band[:], ALU.subtract)
            ve.drain()
            ve.tensor_scalar(sca[:, 5:6], sca[:, 11:12], -1.0, CGLOB, ALU.mult, ALU.add)
            ve.drain()
            ve.tensor_tensor(sca[:, 5:6], sca[:, 5:6], sca[:, 12:13], ALU.add)
            ve.drain()
            ve.tensor_tensor_scan(pscanF[:], band[:], band[:], 0.0, ALU.add, ALU.bypass)
            ve.drain()
            ve.tensor_copy(sca[:, 9:10], pscanF[:, 255:256])
            ve.drain()
            ve.nop().then_inc(SM["rpv"], 1)
            ve.wait_ge(SM["rpt"], e * 4 + 1)
            ve.tensor_copy(rowsc[0:1, 0:128], psTf[0:1, 0:128])
            ve.drain()
            ve.tensor_tensor_scan(rowsc[0:1, 128:256], rowsc[0:1, 0:128],
                                  rowsc[0:1, 0:128], 0.0, ALU.add, ALU.bypass)
            ve.drain()
            ve.tensor_tensor(rowsc[0:1, 0:128], rowsc[0:1, 128:256],
                             rowsc[0:1, 0:128], ALU.subtract)
            ve.drain()
            ve.nop().then_inc(SM["rpv"], 1)
            ve.wait_ge(SM["rpt"], e * 4 + 2)
            ve.tensor_copy(sca[:, 6:7], psTf[:, 128:129])
            ve.drain()
            ve.tensor_scalar(gwork[:], pscanF[:], sca[:, 6:7], None, ALU.add)
            ve.drain()
            ve.tensor_scalar(gwork[:], gwork[:], sca[:, 5:6], None, ALU.is_le)
            ve.drain()
            if dbg is not None:
                ve.tensor_copy(dbg["bandpre"][:], band[:])
                ve.drain()
                ve.tensor_copy(dbg["gworkc"][:], gwork[:])
                ve.drain()
            ve.tensor_tensor(band[:], band[:], gwork[:], ALU.mult)
            ve.drain()
            ve.tensor_tensor(base_[:], base_[:], band[:], ALU.add)
            ve.drain()
            ve.tensor_tensor(avail[:], avail[:], base_[:], ALU.subtract)
            ve.drain()
            ve.tensor_copy(selmy[:], base_[:, 0:64])
            ve.drain()
            ve.tensor_tensor_scan(pscanM[:], selmy[:], selmy[:], 0.0, ALU.add,
                                  ALU.bypass)
            ve.drain()
            ve.tensor_copy(sca[:, 9:10], pscanM[:, 63:64])
            ve.drain()
            ve.nop().then_inc(SM["rpv"], 1)
            ve.wait_ge(SM["rpt"], e * 4 + 3)
            ve.tensor_copy(rowsc[0:1, 0:128], psTf[0:1, 0:128])
            ve.drain()
            ve.tensor_tensor_scan(rowsc[0:1, 128:256], rowsc[0:1, 0:128],
                                  rowsc[0:1, 0:128], 0.0, ALU.add, ALU.bypass)
            ve.drain()
            ve.tensor_tensor(rowsc[0:1, 0:128], rowsc[0:1, 128:256],
                             rowsc[0:1, 0:128], ALU.subtract)
            ve.drain()
            ve.nop().then_inc(SM["rpv"], 1)
            ve.wait_ge(SM["rpt"], e * 4 + 4)
            ve.tensor_copy(sca[:, 7:8], psTf[:, 128:129])
            ve.drain()
            ve.tensor_scalar(pscanM[:], pscanM[:], sca[:, 7:8],
                             -1.0, ALU.add, ALU.add)
            ve.drain()
            ve.tensor_scalar(gwork[:, 0:64], pscanM[:], float(CAP), None,
                             ALU.is_ge)
            ve.drain()
            ve.tensor_scalar(gwork[:, 0:64], gwork[:, 0:64], float(16 * CAP),
                             None, ALU.mult)
            ve.drain()
            ve.tensor_tensor(pscanM[:], pscanM[:], gwork[:, 0:64], ALU.add)
            ve.drain()
            ve.tensor_scalar(pscanM[:], pscanM[:], float(e * CAP), None,
                             ALU.add)
            ve.drain()
            ve.tensor_scalar(pscanM[:], pscanM[:], float(8 * CAP), None,
                             ALU.min)
            ve.drain()
            ve.tensor_scalar(masku[:], selmy[:], 0.5, None, ALU.is_ge)
            ve.drain()
            ve.select(posf[:], masku[:], pscanM[:], posf[:], add_drain=True)
            ve.drain()
            if dbg is not None:
                ve.nop().then_inc(SM["dbgv"], 1)
                ve.wait_ge(SM["dbgs"], (e + 1) * 9 * 16)
        ve.tensor_copy(posi[:], posf[:]).then_inc(SM["pcast"], 1)
        ve.drain()
    p.bump("cntv", 8 * (BISECT + 2))
    p.bump("cv2", 8 * (BISECT + 2))
    p.bump("rpv", 32)
    p.bump("pcast", 1)

    if dbg is not None:
        @blk.sync
        def _(sync):
            for e in range(8):
                sync.wait_ge(SM["dbgv"], e + 1)
                sync.dma_start(out=dbg["dbgA"][e * 128:(e + 1) * 128, :],
                               in_=avail[:]).then_inc(SM["dbgs"], 16)
                sync.dma_start(out=dbg["dbgS"][e * 128:(e + 1) * 128, :],
                               in_=base_[:]).then_inc(SM["dbgs"], 16)
                sync.dma_start(out=dbg["dbgC"][e * 128:(e + 1) * 128, :],
                               in_=sca[:]).then_inc(SM["dbgs"], 16)
                sync.dma_start(out=dbg["dbgPF"][e * 128:(e + 1) * 128, :],
                               in_=pscanF[:]).then_inc(SM["dbgs"], 16)
                sync.dma_start(out=dbg["dbgPM"][e * 128:(e + 1) * 128, :],
                               in_=pscanM[:]).then_inc(SM["dbgs"], 16)
                sync.dma_start(out=dbg["dbgPO"][e * 128:(e + 1) * 128, :],
                               in_=posf[:]).then_inc(SM["dbgs"], 16)
                sync.dma_start(out=dbg["dbgBP"][e * 128:(e + 1) * 128, :],
                               in_=dbg["bandpre"][:]).then_inc(SM["dbgs"], 16)
                sync.dma_start(out=dbg["dbgGW"][e * 128:(e + 1) * 128, :],
                               in_=dbg["gworkc"][:]).then_inc(SM["dbgs"], 16)
                sync.dma_start(out=dbg["dbgBX"][e * 128:(e + 1) * 128, :],
                               in_=band[:]).then_inc(SM["dbgs"], 16)
        p.bump("dbgv", 8)
        p.bump("dbgs", 8 * 9 * 16)

    JN2 = emit_join(p, blk, waits=[
        ("sync", [("ini", INI)]),
        ("gpsimd", [("sbig", 16), ("spos", p.cnt("spos"))])])
    scR.close()

    scF = ExitStack()
    tmy = p.sbs(scF, "p2_tmy", [128, CAPC * nexp], I32)
    gidxs = p.sbs(scF, "p2_gidxs", [128, CAPC * nexp], I32)
    w1j = p.sbs(scF, "p2_w1j", [128, DCH * 4096], BF16)
    w2j = p.sbs(scF, "p2_w2j", [128, 32 * D], BF16)
    b1s = p.sbs(scF, "p2_b1s", [128, 32 * nexp], F32)
    b2s = p.sbs(scF, "p2_b2s", [128, 8 * nexp], F32)
    normg = [p.sbs(scF, f"p2_normg{i}", [128, 4 * D], BF16) for i in range(2)]
    normfm = p.sbs(scF, "p2_normfm", [128, DCH * 512], BF16)
    hpre = [p.sbs(scF, f"p2_hpre{i}", [128, 512], F32) for i in range(2)]
    hsq = [p.sbs(scF, f"p2_hsq{i}", [128, 512], F32) for i in range(2)]
    hth = [p.sbs(scF, f"p2_hth{i}", [128, 512], BF16) for i in range(2)]
    hbuf = p.sbs(scF, "p2_h", [128, 32 * 512], BF16)
    ybuf = p.sbs(scF, "p2_y", [128, 8 * 512], BF16)
    ytm = p.sbs(scF, "p2_ytm", [128, D], BF16)

    TTOT = []
    for j in range(nexp):
        off = 0
        for t, tw in enumerate(TT_E):
            TTOT.append((j, t, tw, j * CAPC + off))
            off += tw // 128

    @blk.sync
    def _(sync, JN2=JN2):
        sync.wait_ge(SM[JN2[0]], JN2[1])
        sync.dma_start(out=b1s[:], in_=io["b1c"][:, :]).then_inc(SM["bld"], 16)
        sync.dma_start(out=b2s[:], in_=io["b2c"][:, :]).then_inc(SM["bld"], 16)
        sync.dma_start(out=gidxs[:], in_=io["gidx"][:, :]).then_inc(SM["gld"], 16)
    p.bump("bld", 32)
    p.bump("gld", 16)

    @blk.gpsimd
    def _(gp, JN2=JN2):
        gp.wait_ge(SM[JN2[0]], JN2[1])
        gp.memset(normg[0][:], 0.0)
        gp.memset(normg[1][:], 0.0)
        gp.drain()
        gp.wait_ge(SM["gld"], 16)
        for c in range(CAPC * nexp):
            gp.indirect_dma_start(
                out=tmy[:, c:c + 1], out_offset=None, in_=io["S_h"][:, :],
                in_offset=IndirectOffsetOnAxis(ap=gidxs[:, c:c + 1], axis=0),
            ).then_inc(SM["tld"], 16)
        for ti, (j, t, tw, toff) in enumerate(TTOT):
            if t == 0:
                if j >= 1:
                    gp.wait_ge(SM["pm1"], j * 96)
                    gp.wait_ge(SM["pm2"], j * 6)
                gp.dma_start(
                    out=w1j.ap().rearrange("p (dj f) -> p dj f", f=4096),
                    in_=io["w1my"][j * D:(j + 1) * D, :].rearrange("(dj p) f -> p dj f", p=128),
                ).then_inc(SM["w1l"], 16)
                gp.dma_start(
                    out=w2j.ap().rearrange("p (fc d) -> p fc d", d=D),
                    in_=io["w2my"][j * 4096:(j + 1) * 4096, :].rearrange("(fc p) d -> p fc d", p=128),
                ).then_inc(SM["w2l"], 16)
            nq = tw // 128
            gidx_t = CAPC * j + 4 * t
            gbase = sum(x[2] // 128 for x in TTOT[:ti])
            if ti == 0:
                gp.wait_ge(SM["tld"], 16 * CAPC * nexp)
            for q in range(nq):
                gi = gbase + q
                if gi >= 1:
                    gp.wait_ge(SM["gno"], 16 * gi)
                if gi >= 8:
                    gp.wait_ge(SM["ntr"], gi - 8 + 1)
                gp.indirect_dma_start(
                    out=normg[(gi // 4) % 2][:, (gi % 4) * D:(gi % 4 + 1) * D],
                    out_offset=None, in_=io["normH"][:, :],
                    in_offset=IndirectOffsetOnAxis(ap=tmy[:, toff + q:toff + q + 1], axis=0),
                ).then_inc(SM["gno"], 16)
            for q in range(nq):
                yq = gbase + q
                gp.wait_ge(SM["yte"], yq + 1)
                if yq >= 1:
                    gp.wait_ge(SM["yst0"], 16 * yq)
                gp.indirect_dma_start(
                    out=io["yH"][:, :],
                    out_offset=IndirectOffsetOnAxis(ap=tmy[:, toff + q:toff + q + 1], axis=0),
                    in_=ytm[:], in_offset=None,
                ).then_inc(SM["yst0"], 16)
    p.bump("tld", 16 * CAPC * nexp)
    ngtot = sum(tw // 128 for (_, _, tw, _) in TTOT)
    p.bump("gno", 16 * CAPC * nexp)
    p.bump("yst0", 16 * ngtot)

    @blk.tensor
    def _(te, JN2=JN2):
        te.wait_ge(SM[JN2[0]], JN2[1])
        ti = 0
        for (j, t, tw, toff) in TTOT:
            nq = tw // 128
            gbase = sum(x[2] // 128 for x in TTOT[:ti])
            te.wait_ge(SM["yte"], gbase)
            for q in range(nq):
                gi = gbase + q
                te.wait_ge(SM["gno"], 16 * (gi + 1))
                if gi >= 1:
                    te.wait_ge(SM["nte"], gi)
                for dj in range(DCH):
                    inst = te.transpose(
                        psT[:, dj * 128:(dj + 1) * 128],
                        normg[(gi // 4) % 2][:, (gi % 4) * D + dj * 128:(gi % 4) * D + (dj + 1) * 128],
                        identb[:])
                inst.then_inc(SM["ntr"], 1)
            te.wait_ge(SM["w1l"], 16 * (j + 1))
            te.wait_ge(SM["nte"], gbase + nq)
            for gb in range(16):
                gm = ti * 32 + gb * 2
                for fi in range(2):
                    fc = gb * 2 + fi
                    ps_ = psA[(gb % 2) * 2 + fi]
                    if gm + fi >= 4:
                        te.wait_ge(SM["ev1"], (ti * 32 + gb * 2 + fi) - 3)
                    for dj in range(DCH):
                        inst = te.matmul(
                            ps_[:, 0:tw],
                            w1j[:, dj * 4096 + fc * 128: dj * 4096 + (fc + 1) * 128],
                            normfm[:, dj * 512: dj * 512 + tw],
                            start=(dj == 0), stop=(dj == DCH - 1))
                    inst.then_inc(SM["pm1"], 1)
            te.wait_ge(SM["w2l"], 16 * (j + 1))
            te.wait_ge(SM["gel"], ti * 32 + 32)
            for half in range(2):
                hm = ti * 2 + half
                if hm >= 1:
                    te.wait_ge(SM["ev2"], hm * 4)
                for fc in range(32):
                    for oc in range(4):
                        inst = te.matmul(
                            psA[oc][:, 0:tw],
                            w2j[:, fc * D + half * 512 + oc * 128: fc * D + half * 512 + (oc + 1) * 128],
                            hbuf[:, fc * 512: fc * 512 + tw],
                            start=(fc == 0), stop=(fc == 31))
                        if fc == 31 and oc == 3:
                            inst.then_inc(SM["pm2"], 1)
            for q in range(nq):
                yq = gbase + q
                te.wait_ge(SM["ev2"], ti * 8 + 8)
                if yq >= 1:
                    te.wait_ge(SM["yte"], yq)
                for dj in range(DCH):
                    inst = te.transpose(psT[:, dj * 128:(dj + 1) * 128],
                                        ybuf[:, dj * 512 + q * 128: dj * 512 + (q + 1) * 128],
                                        identb[:])
                inst.then_inc(SM["ytr"], 1)
            ti += 1
    p.bump("ntr", ngtot)
    p.bump("pm1", 32 * len(TTOT))
    p.bump("pm2", 2 * len(TTOT))
    p.bump("ytr", ngtot)

    @blk.vector
    def _(ve, JN2=JN2):
        ve.wait_ge(SM[JN2[0]], JN2[1])
        ve.wait_ge(SM["bld"], 32)
        ti = 0
        for (j, t, tw, toff) in TTOT:
            nq = tw // 128
            gbase = sum(x[2] // 128 for x in TTOT[:ti])
            if ti >= 1:
                ve.wait_ge(SM["pm1"], ti * 32)
                ve.wait_ge(SM["pm2"], ti * 2)
                ve.wait_ge(SM["ytr"], gbase)
            for q in range(nq):
                gi = gbase + q
                ve.wait_ge(SM["ntr"], gi + 1)
                ve.tensor_copy(
                    normfm.ap().rearrange("p (dj c) -> p dj c", c=512)[:, :, q * 128:(q + 1) * 128],
                    psT.ap().rearrange("p (dj c) -> p dj c", c=128)
                ).then_inc(SM["nte"], 1)
            ve.drain()
            for fc in range(32):
                gm1 = ti * 32 + fc
                ve.wait_ge(SM["pm1"], gm1 + 1)
                if gm1 >= 2:
                    ve.wait_ge(SM["sqp"], gm1 - 1)
                ps_ = psA[((fc // 2) % 2) * 2 + (fc % 2)]
                ve.tensor_scalar(hpre[fc % 2][:, 0:tw], ps_[:, 0:tw],
                                 b1s[:, j * 32 + fc:j * 32 + fc + 1], None,
                                 ALU.add).then_inc(SM["ev1"], 1)
                ve.wait_ge(SM["sqp"], gm1 + 1)
                ve.tensor_scalar(hsq[fc % 2][:, 0:tw], hsq[fc % 2][:, 0:tw],
                                 GELC, 1.0, ALU.mult, ALU.add)
                ve.drain()
                ve.tensor_tensor(hsq[fc % 2][:, 0:tw], hsq[fc % 2][:, 0:tw],
                                 hpre[fc % 2][:, 0:tw], ALU.mult).then_inc(SM["fv"], 1)
                ve.drain()
                ve.wait_ge(SM["sqp2"], gm1 + 1)
                ve.scalar_tensor_tensor(hbuf[:, fc * 512:fc * 512 + tw],
                                        hth[fc % 2][:, 0:tw], 1.0,
                                        hpre[fc % 2][:, 0:tw], ALU.add, ALU.mult
                                        ).then_inc(SM["gel"], 1)
            for half in range(2):
                ve.wait_ge(SM["pm2"], ti * 2 + half + 1)
                for oc in range(4):
                    ve.tensor_scalar(ybuf[:, (half * 4 + oc) * 512:(half * 4 + oc) * 512 + tw],
                                     psA[oc][:, 0:tw], 0.5,
                                     b2s[:, j * 8 + half * 4 + oc:j * 8 + half * 4 + oc + 1],
                                     ALU.mult, ALU.add).then_inc(SM["ev2"], 1)
            ve.drain()
            for q in range(nq):
                yq = gbase + q
                ve.wait_ge(SM["ytr"], yq + 1)
                if yq >= 1:
                    ve.wait_ge(SM["yst0"], 16 * yq)
                ve.tensor_copy(ytm[:], psT[:, 0:1024]).then_inc(SM["yte"], 1)
                ve.drain()
            ti += 1
    p.bump("nte", ngtot)
    p.bump("ev1", 32 * len(TTOT))
    p.bump("gel", 32 * len(TTOT))
    p.bump("fv", 32 * len(TTOT))
    p.bump("ev2", 8 * len(TTOT))
    p.bump("yte", ngtot)

    @blk.scalar
    def _(sc, JN2=JN2):
        sc.wait_ge(SM[JN2[0]], JN2[1])
        ti = 0
        for (j, t, tw, toff) in TTOT:
            for fc in range(32):
                gm1 = ti * 32 + fc
                sc.wait_ge(SM["ev1"], gm1 + 1)
                if gm1 >= 2:
                    sc.wait_ge(SM["fv"], gm1 - 1)
                    sc.wait_ge(SM["gel"], gm1 - 1)
                sc.activation(out=hsq[fc % 2][:, 0:tw], in_=hpre[fc % 2][:, 0:tw],
                              func=AF.Square).then_inc(SM["sqp"], 1)
                sc.wait_ge(SM["fv"], gm1 + 1)
                sc.activation(out=hth[fc % 2][:, 0:tw], in_=hsq[fc % 2][:, 0:tw],
                              func=AF.Tanh, scale=GELS).then_inc(SM["sqp2"], 1)
            ti += 1
    p.bump("sqp", 32 * len(TTOT))
    p.bump("sqp2", 32 * len(TTOT))

    JN3 = emit_join(p, blk, waits=[
        ("gpsimd", [("gno", p.cnt("gno")),
                    ("yst0", p.cnt("yst0")), ("tld", p.cnt("tld")), ("gld", p.cnt("gld"))]),
        ("sync", [("bld", p.cnt("bld"))])])
    scF.close()

    scO = ExitStack()
    ownr = p.sbs(scO, "p2_ownr", [128, 32], I32)
    x1t = [p.sbs(scO, f"p2_x1t{i}", [128, D], F32) for i in range(2)]
    yg = [p.sbs(scO, f"p2_yg{i}", [128, D], BF16) for i in range(2)]
    feat = p.sbs(scO, "p2_feat", [128, D], F32)
    sfull = p.sbs(scO, "p2_sfull", [128, D], F32)
    x2t = [p.sbs(scO, f"p2_x2t{i}", [128, D], F32) for i in range(2)]
    bnp = p.sbs(scO, "p2_bnp", [128, 16], F32)
    bnp2 = p.sbs(scO, "p2_bnp2", [128, 16], F32)

    @blk.sync
    def _(sync, JN3=JN3):
        sync.wait_ge(SM[JN3[0]], JN3[1])
        sync.dma_start(out=ownr[:], in_=io["ownrows"][:, :]).then_inc(SM["opg"], 16)
        for i in range(32):
            sync.wait_ge(SM["x2v"], i + 1)
            sync.dma_start(out=io["x2out"][i * 128:(i + 1) * 128, :],
                           in_=x2t[i % 2][:]
                           ).then_inc(SM["x2s0" if i % 2 == 0 else "x2s1"], 16)
        sync.wait_ge(SM["x2s0"], p.cnt("x2s0") + 16 * 16)
        sync.wait_ge(SM["x2s1"], p.cnt("x2s1") + 16 * 16)
    p.bump("opg", 16)
    p.bump("x2s0", 16 * 16)
    p.bump("x2s1", 16 * 16)

    @blk.gpsimd
    def _(gp, JN3=JN3):
        gp.wait_ge(SM[JN3[0]], JN3[1])
        gp.wait_ge(SM["opg"], 16)
        for i in range(32):
            if i >= 2:
                gp.wait_ge(SM["sv"], i - 1)
            gp.indirect_dma_start(
                out=yg[i % 2][:], out_offset=None, in_=io["yH"][:, :],
                in_offset=IndirectOffsetOnAxis(ap=ownr[:, i:i + 1], axis=0),
            ).then_inc(SM["ygt0" if i % 2 == 0 else "ygt1"], 16)
    p.bump("ygt0", 16 * 16)
    p.bump("ygt1", 16 * 16)

    @blk.scalar
    def _(sc, JN3=JN3):
        sc.wait_ge(SM[JN3[0]], JN3[1])
        for i in range(32):
            if i >= 2:
                sc.wait_ge(SM["x2v"], i - 1)
            sc.dma_start(out=x1t[i % 2][:],
                         in_=io["x1own"][i * 128:(i + 1) * 128, :]
                         ).then_inc(SM["xld0" if i % 2 == 0 else "xld1"], 16)
            sc.wait_ge(SM["lnp"], i + 1)
            sc.activation(out=bnp[:, 15:16], in_=bnp[:, 14:15], func=AF.Sqrt
                          ).then_inc(SM["sqp"], 1)
            sc.wait_ge(SM["lnp2"], i + 1)
            sc.activation(out=bnp2[:, 15:16], in_=bnp2[:, 14:15], func=AF.Sqrt
                          ).then_inc(SM["sqp2"], 1)
    p.bump("xld0", 16 * 16)
    p.bump("xld1", 16 * 16)

    @blk.vector
    def _(ve, JN3=JN3, SQ=p.cnt("sqp"), SQ2=p.cnt("sqp2")):
        ve.wait_ge(SM[JN3[0]], JN3[1])
        for i in range(32):
            ve.wait_ge(SM["xld0" if i % 2 == 0 else "xld1"], 16 * (i // 2 + 1))
            ve.bn_stats(bnp[:, 0:6], x1t[i % 2][:, 0:512])
            ve.bn_stats(bnp[:, 6:12], x1t[i % 2][:, 512:1024])
            ve.drain()
            ve.bn_aggr(bnp[:, 12:14], bnp[:, 0:12])
            ve.drain()
            ve.tensor_scalar(bnp[:, 14:15], bnp[:, 13:14], float(LN_EPS), None,
                             ALU.add).then_inc(SM["lnp"], 1)
            ve.wait_ge(SM["sqp"], SQ + i + 1)
            ve.reciprocal(bnp[:, 14:15], bnp[:, 15:16])
            ve.drain()
            ve.tensor_scalar(feat[:], x1t[i % 2][:], bnp[:, 12:13], bnp[:, 14:15],
                             ALU.subtract, ALU.mult)
            ve.drain()
            ve.wait_ge(SM["ygt0" if i % 2 == 0 else "ygt1"], 16 * (i // 2 + 1))
            ve.tensor_tensor(sfull[:], feat[:], yg[i % 2][:], ALU.add
                             ).then_inc(SM["sv"], 1)
            ve.drain()
            ve.bn_stats(bnp2[:, 0:6], sfull[:, 0:512])
            ve.bn_stats(bnp2[:, 6:12], sfull[:, 512:1024])
            ve.drain()
            ve.bn_aggr(bnp2[:, 12:14], bnp2[:, 0:12])
            ve.drain()
            ve.tensor_scalar(bnp2[:, 14:15], bnp2[:, 13:14], float(LN_EPS), None,
                             ALU.add).then_inc(SM["lnp2"], 1)
            ve.wait_ge(SM["sqp2"], SQ2 + i + 1)
            ve.reciprocal(bnp2[:, 14:15], bnp2[:, 15:16])
            ve.drain()
            if i >= 2:
                ve.wait_ge(SM["x2s0" if i % 2 == 0 else "x2s1"],
                           16 * (i // 2))
            ve.tensor_scalar(sfull[:], sfull[:], bnp2[:, 12:13], bnp2[:, 14:15],
                             ALU.subtract, ALU.mult)
            ve.drain()
            ve.tensor_tensor(x2t[i % 2][:], x1t[i % 2][:], sfull[:], ALU.add
                             ).then_inc(SM["x2v"], 1)
            ve.drain()
    p.bump("lnp", 32)
    p.bump("lnp2", 32)
    p.bump("sqp", 32)
    p.bump("sqp2", 32)
    p.bump("sv", 32)
    p.bump("x2v", 32)

    JNF = emit_join(p, blk, waits=[
        ("sync", [("x2s0", p.cnt("x2s0")), ("x2s1", p.cnt("x2s1"))]),
        ("gpsimd", [("ygt0", p.cnt("ygt0")), ("ygt1", p.cnt("ygt1")),
                    ("opg", p.cnt("opg"))]),
        ("scalar", [("xld0", p.cnt("xld0")), ("xld1", p.cnt("xld1"))])])
    scO.close()
    return JNF


def emit_p2_owner(p, io, JN=None):
    blk = p.block
    for s_ in ["opg", "xld0", "xld1", "ygt0", "ygt1", "lnp", "lnp2",
               "sqp", "sqp2", "sv", "x2v", "x2s0", "x2s1"]:
        p.sem(s_)
    SM = p._sems

    scO = ExitStack()
    ownr = p.sbs(scO, "o_ownr", [128, 32], I32)
    x1t = [p.sbs(scO, f"o_x1t{i}", [128, D], F32) for i in range(2)]
    yg = [p.sbs(scO, f"o_yg{i}", [128, D], BF16) for i in range(2)]
    feat = p.sbs(scO, "o_feat", [128, D], F32)
    sfull = p.sbs(scO, "o_sfull", [128, D], F32)
    x2t = [p.sbs(scO, f"o_x2t{i}", [128, D], F32) for i in range(2)]
    bnp = p.sbs(scO, "o_bnp", [128, 16], F32)
    bnp2 = p.sbs(scO, "o_bnp2", [128, 16], F32)

    @blk.sync
    def _(sync):
        if JN is not None:
            sync.wait_ge(SM[JN[0]], JN[1])
        sync.dma_start(out=ownr[:], in_=io["ownrows"][:, :]).then_inc(SM["opg"], 16)
        for i in range(32):
            sync.wait_ge(SM["x2v"], i + 1)
            sync.dma_start(out=io["x2out"][i * 128:(i + 1) * 128, :],
                           in_=x2t[i % 2][:]
                           ).then_inc(SM["x2s0" if i % 2 == 0 else "x2s1"], 16)
        sync.wait_ge(SM["x2s0"], 16 * 16)
        sync.wait_ge(SM["x2s1"], 16 * 16)
    p.bump("opg", 16)
    p.bump("x2s0", 16 * 16)
    p.bump("x2s1", 16 * 16)

    @blk.gpsimd
    def _(gp):
        if JN is not None:
            gp.wait_ge(SM[JN[0]], JN[1])
        gp.wait_ge(SM["opg"], 16)
        for i in range(32):
            if i >= 2:
                gp.wait_ge(SM["sv"], i - 1)
            gp.indirect_dma_start(
                out=yg[i % 2][:], out_offset=None, in_=io["yH"][:, :],
                in_offset=IndirectOffsetOnAxis(ap=ownr[:, i:i + 1], axis=0),
            ).then_inc(SM["ygt0" if i % 2 == 0 else "ygt1"], 16)
    p.bump("ygt0", 16 * 16)
    p.bump("ygt1", 16 * 16)

    @blk.scalar
    def _(sc):
        if JN is not None:
            sc.wait_ge(SM[JN[0]], JN[1])
        for i in range(32):
            if i >= 2:
                sc.wait_ge(SM["x2v"], i - 1)
            sc.dma_start(out=x1t[i % 2][:],
                         in_=io["x1own"][i * 128:(i + 1) * 128, :]
                         ).then_inc(SM["xld0" if i % 2 == 0 else "xld1"], 16)
            sc.wait_ge(SM["lnp"], i + 1)
            sc.activation(out=bnp[:, 15:16], in_=bnp[:, 14:15], func=AF.Sqrt
                          ).then_inc(SM["sqp"], 1)
            sc.wait_ge(SM["lnp2"], i + 1)
            sc.activation(out=bnp2[:, 15:16], in_=bnp2[:, 14:15], func=AF.Sqrt
                          ).then_inc(SM["sqp2"], 1)
    p.bump("xld0", 16 * 16)
    p.bump("xld1", 16 * 16)

    @blk.vector
    def _(ve):
        if JN is not None:
            ve.wait_ge(SM[JN[0]], JN[1])
        for i in range(32):
            ve.wait_ge(SM["xld0" if i % 2 == 0 else "xld1"], 16 * (i // 2 + 1))
            ve.bn_stats(bnp[:, 0:6], x1t[i % 2][:, 0:512])
            ve.bn_stats(bnp[:, 6:12], x1t[i % 2][:, 512:1024])
            ve.drain()
            ve.bn_aggr(bnp[:, 12:14], bnp[:, 0:12])
            ve.drain()
            ve.tensor_scalar(bnp[:, 14:15], bnp[:, 13:14], float(LN_EPS), None,
                             ALU.add).then_inc(SM["lnp"], 1)
            ve.wait_ge(SM["sqp"], i + 1)
            ve.reciprocal(bnp[:, 14:15], bnp[:, 15:16])
            ve.drain()
            ve.tensor_scalar(feat[:], x1t[i % 2][:], bnp[:, 12:13], bnp[:, 14:15],
                             ALU.subtract, ALU.mult)
            ve.drain()
            ve.wait_ge(SM["ygt0" if i % 2 == 0 else "ygt1"], 16 * (i // 2 + 1))
            ve.tensor_tensor(sfull[:], feat[:], yg[i % 2][:], ALU.add
                             ).then_inc(SM["sv"], 1)
            ve.drain()
            ve.bn_stats(bnp2[:, 0:6], sfull[:, 0:512])
            ve.bn_stats(bnp2[:, 6:12], sfull[:, 512:1024])
            ve.drain()
            ve.bn_aggr(bnp2[:, 12:14], bnp2[:, 0:12])
            ve.drain()
            ve.tensor_scalar(bnp2[:, 14:15], bnp2[:, 13:14], float(LN_EPS), None,
                             ALU.add).then_inc(SM["lnp2"], 1)
            ve.wait_ge(SM["sqp2"], i + 1)
            ve.reciprocal(bnp2[:, 14:15], bnp2[:, 15:16])
            ve.drain()
            if i >= 2:
                ve.wait_ge(SM["x2s0" if i % 2 == 0 else "x2s1"], 16 * (i // 2))
            ve.tensor_scalar(sfull[:], sfull[:], bnp2[:, 12:13], bnp2[:, 14:15],
                             ALU.subtract, ALU.mult)
            ve.drain()
            ve.tensor_tensor(x2t[i % 2][:], x1t[i % 2][:], sfull[:], ALU.add
                             ).then_inc(SM["x2v"], 1)
            ve.drain()
    p.bump("lnp", 32)
    p.bump("lnp2", 32)
    p.bump("sqp", 32)
    p.bump("sqp2", 32)
    p.bump("sv", 32)
    p.bump("x2v", 32)

    ret = emit_join(p, blk, waits=[
        ("sync", [("x2s0", p.cnt("x2s0")), ("x2s1", p.cnt("x2s1"))]),
        ("gpsimd", [("ygt0", p.cnt("ygt0")), ("ygt1", p.cnt("ygt1"))]),
        ("scalar", [("xld0", p.cnt("xld0")), ("xld1", p.cnt("xld1"))])])
    scO.close()
    return ret


"""Phase 3: L1_attn — causal relu attention, 8 heads, one batch of 8192 tokens.

Per-core: own 4096 rows as 16 tiles of 256 (tile t' = global 256-block
2*t'+slot); keys = whole batch, segment s = keys [512s, 512s+512).
Causal schedule: group (s, t') exists for t' >= s, 4 key-chunks each;
masked (per-core maskTab) iff t' == s.  SPMD-uniform: slot enters only
through input data (x2own rows, maskTab contents).

out[n,v] = sum_h P_h @ V_h with P = relu(Q K^T)/8192 causal;
Q/K/V projections of ln3 = LN(x2) built per (head, seg) from an FM copy
of ln3 staged in HBM by the prologue.

io: x2own [4096,1024]f32, x2full [8192,1024]f32,
    WkTh/WqTh [128, 8*8*128]bf16  (col (h*8+dc)*128+e  = W[h,e,dc*128+p]),
    WvTh      [128, 8*8*1024]bf16 (col (h*8+dc)*1024+v = Wv[h,v,dc*128+p]),
    maskTab [128, 4*256]bf16, identb [128,128]bf16,
    ln3fH dint [128, 8*8192]bf16, ln3oH dint [128, 8*4096]bf16,
    x3own out [4096,1024]f32.
"""

NH = 8
NSEG = 16
TP = 16
ISC = 1.0 / 8192.0


def emit_p3(p, io, JN=None):
    nc = p.nc
    blk = p.block
    SM = p._sems

    identb = p.sb("p3_identb", [128, 128], BF16)
    outacc = p.sb("p3_outacc", [128, 32 * 1024], F32)

    sems = ["ini3", "xf0", "xf1", "pv1", "ps1", "pv2", "ptl", "ple",
            "lfs0", "lfs1",
            "wld", "mld", "qld0", "qld1", "qpe", "qev", "sld",
            "kpe", "kev", "vpe", "vev", "spe", "sse", "ssm",
            "g2", "oa",
            "xe0", "xe1", "pe1", "pe4", "est0", "est1"]
    for s_ in sems:
        p.sem(s_)

    @blk.sync
    def _(sync):
        if JN is not None:
            sync.wait_ge(SM[JN[0]], JN[1])
        sync.dma_start(out=identb[:], in_=io["identb"][:, :]).then_inc(SM["ini3"], 16)
    p.bump("ini3", 16)
    INI = p.cnt("ini3")

    ln3fH3 = io["ln3fH"].rearrange("p (j c) -> p j c", c=8192)
    ln3oH3 = io["ln3oH"].rearrange("p (j c) -> p j c", c=4096)

    scP = ExitStack()
    x2t = [p.sbs(scP, f"p3_x2t{i}", [128, D], F32) for i in range(2)]
    lnt = [p.sbs(scP, f"p3_lnt{i}", [128, D], BF16) for i in range(2)]
    bnsc = [p.sbs(scP, f"p3_bn{i}", [128, 16], F32) for i in range(2)]
    fs = [p.sbs(scP, f"p3_fs{i}", [128, DCH * 128], BF16) for i in range(2)]
    psTb = [p.pss(scP, f"p3_psTb{i}", [128, 1024], BF16) for i in range(2)]
    NTP = 64 + 32

    @blk.scalar
    def _(sc):
        if JN is not None:
            sc.wait_ge(SM[JN[0]], JN[1])
        for i in range(NTP):
            src = (io["x2full"][i * 128:(i + 1) * 128, :] if i < 64 else
                   io["x2own"][(i - 64) * 128:(i - 63) * 128, :])
            if i >= 2:
                sc.wait_ge(SM["pv2"], i - 1)
            sc.dma_start(out=x2t[i % 2][:], in_=src
                         ).then_inc(SM["xf0" if i % 2 == 0 else "xf1"], 16)
            sc.wait_ge(SM["pv1"], i + 1)
            sc.activation(out=bnsc[i % 2][:, 15:16], in_=bnsc[i % 2][:, 14:15],
                          func=AF.Sqrt).then_inc(SM["ps1"], 1)
            dst = (ln3fH3[:, :, i * 128:(i + 1) * 128] if i < 64 else
                   ln3oH3[:, :, (i - 64) * 128:(i - 63) * 128])
            sc.wait_ge(SM["ple"], i + 1)
            sc.dma_start(out=dst,
                         in_=fs[i % 2].ap().rearrange("p (j c) -> p j c", c=128)
                         ).then_inc(SM["lfs0" if i % 2 == 0 else "lfs1"], 16)

    @blk.tensor
    def _(te):
        te.wait_ge(SM["ini3"], INI)
        for i in range(NTP):
            te.wait_ge(SM["pv2"], i + 1)
            if i >= 2:
                te.wait_ge(SM["ple"], i - 1)
            for j in range(DCH):
                inst = te.transpose(psTb[i % 2][:, j * 128:(j + 1) * 128],
                                    lnt[i % 2][:, j * 128:(j + 1) * 128],
                                    identb[:])
            inst.then_inc(SM["ptl"], 1)

    @blk.vector
    def _(ve):
        for i in range(NTP):
            ve.wait_ge(SM["xf0" if i % 2 == 0 else "xf1"], 16 * (i // 2 + 1))
            sc_ = bnsc[i % 2]
            ve.bn_stats(sc_[:, 0:6], x2t[i % 2][:, 0:512])
            ve.bn_stats(sc_[:, 6:12], x2t[i % 2][:, 512:1024])
            ve.drain()
            ve.bn_aggr(sc_[:, 12:14], sc_[:, 0:12])
            ve.drain()
            ve.tensor_scalar(sc_[:, 14:15], sc_[:, 13:14], float(LN_EPS), None,
                             ALU.add).then_inc(SM["pv1"], 1)
            ve.wait_ge(SM["ps1"], i + 1)
            ve.reciprocal(sc_[:, 14:15], sc_[:, 15:16])
            ve.drain()
            if i >= 2:
                ve.wait_ge(SM["ptl"], i - 1)
            ve.tensor_scalar(lnt[i % 2][:], x2t[i % 2][:], sc_[:, 12:13],
                             sc_[:, 14:15], ALU.subtract, ALU.mult
                             ).then_inc(SM["pv2"], 1)
            ve.wait_ge(SM["ptl"], i + 1)
            if i >= 2:
                ve.wait_ge(SM["lfs0" if i % 2 == 0 else "lfs1"], 16 * (i // 2))
            ve.tensor_copy(fs[i % 2][:], psTb[i % 2][:, 0:DCH * 128]
                           ).then_inc(SM["ple"], 1)
        ve.drain()

    for s_, n_ in (("ps1", NTP), ("pv1", NTP), ("pv2", NTP), ("ptl", NTP),
                   ("ple", NTP), ("xf0", 16 * NTP // 2), ("xf1", 16 * NTP // 2),
                   ("lfs0", 16 * NTP // 2), ("lfs1", 16 * NTP // 2)):
        p.bump(s_, n_)

    JN2 = emit_join(p, blk, waits=[
        ("scalar", [("lfs0", p.cnt("lfs0")), ("lfs1", p.cnt("lfs1"))]),
        ("vector", [("ple", p.cnt("ple"))])])
    scP.close()

    scM = ExitStack()
    WvT = p.sbs(scM, "p3_WvT", [128, DCH * 1024], BF16)
    WkT = p.sbs(scM, "p3_WkT", [128, DCH * 128], BF16)
    WqT = p.sbs(scM, "p3_WqT", [128, DCH * 128], BF16)
    QhT = p.sbs(scM, "p3_QhT", [128, 4096], BF16)
    KhS = p.sbs(scM, "p3_KhS", [128, 512], BF16)
    VhS = p.sbs(scM, "p3_VhS", [128, 4 * 1024], BF16)
    stf = p.sbs(scM, "p3_stf", [128, DCH * 512], BF16)
    qtf = [p.sbs(scM, f"p3_qtf{i}", [128, DCH * 256], BF16) for i in range(2)]
    maskT = p.sbs(scM, "p3_maskT", [128, 4 * 256], BF16)
    STb = [p.sbs(scM, f"p3_STb{i}", [128, 256], BF16) for i in range(4)]
    STm = [p.sbs(scM, f"p3_STm{i}", [128, 256], BF16) for i in range(4)]
    ps2 = [p.pss(scM, f"p3_ps2_{i}", [128, 512]) for i in range(4)]
    psS = p.pss(scM, "p3_psS", [128, 512])
    psKQ = p.pss(scM, "p3_psKQ", [128, 512])
    psV = [p.pss(scM, f"p3_psV{i}", [128, 512]) for i in range(2)]

    GROUPS = [(s, tp) for s in range(NSEG) for tp in range(s, TP)]
    NG = len(GROUPS)

    @blk.sync
    def _(sync):
        sync.wait_ge(SM[JN2[0]], JN2[1])
        sync.dma_start(out=maskT[:], in_=io["maskTab"][:, :]).then_inc(SM["mld"], 16)
        for h in range(NH):
            if h >= 1:
                sync.wait_ge(SM["oa"], 136 * h)
            sync.dma_start(out=WvT[:],
                           in_=io["WvTh"][:, h * 8192:(h + 1) * 8192]
                           ).then_inc(SM["wld"], 16)
            sync.dma_start(out=WkT[:],
                           in_=io["WkTh"][:, h * 1024:(h + 1) * 1024]
                           ).then_inc(SM["wld"], 16)
            sync.dma_start(out=WqT[:],
                           in_=io["WqTh"][:, h * 1024:(h + 1) * 1024]
                           ).then_inc(SM["wld"], 16)
            for tp in range(TP):
                if h * TP + tp >= 2:
                    sync.wait_ge(SM["qpe"], h * TP + tp - 1)
                sync.dma_start(
                    out=qtf[tp % 2].ap().rearrange("p (j c) -> p j c", c=256),
                    in_=ln3oH3[:, :, tp * 256:(tp + 1) * 256]
                    ).then_inc(SM["qld0" if tp % 2 == 0 else "qld1"], 16)
            for s in range(NSEG):
                g = h * NSEG + s
                if g >= 1:
                    sync.wait_ge(SM["vpe"], 8 * g)
                sync.dma_start(
                    out=stf.ap().rearrange("p (j c) -> p j c", c=512),
                    in_=ln3fH3[:, :, s * 512:(s + 1) * 512]
                    ).then_inc(SM["sld"], 16)
    p.bump("wld", NH * 48)
    p.bump("mld", 16)
    p.bump("qld0", NH * TP // 2 * 16)
    p.bump("qld1", NH * TP // 2 * 16)
    p.bump("sld", NH * NSEG * 16)

    @blk.tensor
    def _(te):
        te.wait_ge(SM[JN2[0]], JN2[1])
        nchunk = 0
        ngrp = 0
        nvv = 0
        for h in range(NH):
            te.wait_ge(SM["wld"], (h + 1) * 48)
            if h >= 1:
                te.wait_ge(SM["kev"], h * NSEG)
            for tp in range(TP):
                it = h * TP + tp
                te.wait_ge(SM["qld0" if tp % 2 == 0 else "qld1"],
                           16 * (it // 2 + 1))
                if it >= 1:
                    te.wait_ge(SM["qev"], it)
                half = (tp % 2) * 256
                for dc in range(DCH):
                    inst = te.matmul(psKQ[:, half:half + 256],
                                     WqT[:, dc * 128:(dc + 1) * 128],
                                     qtf[tp % 2][:, dc * 256:(dc + 1) * 256],
                                     start=(dc == 0), stop=(dc == DCH - 1))
                inst.then_inc(SM["qpe"], 1)
            for s in range(NSEG):
                gseg = h * NSEG + s
                te.wait_ge(SM["sld"], 16 * (gseg + 1))
                te.wait_ge(SM["qev"], (h + 1) * TP)
                if gseg >= 1:
                    te.wait_ge(SM["kev"], gseg)
                for dc in range(DCH):
                    inst = te.matmul(psKQ[:, 0:512],
                                     WkT[:, dc * 128:(dc + 1) * 128],
                                     stf[:, dc * 512:(dc + 1) * 512],
                                     start=(dc == 0), stop=(dc == DCH - 1))
                inst.then_inc(SM["kpe"], 1)
                for mch in range(4):
                    for vh in range(2):
                        if nvv >= 2:
                            te.wait_ge(SM["vev"], nvv - 1)
                        for dc in range(DCH):
                            inst = te.matmul(
                                psV[nvv % 2][:, 0:512],
                                stf[:, dc * 512 + mch * 128:dc * 512 + mch * 128 + 128],
                                WvT[:, dc * 1024 + vh * 512:dc * 1024 + vh * 512 + 512],
                                start=(dc == 0), stop=(dc == DCH - 1))
                        nvv += 1
                        inst.then_inc(SM["vpe"], 1)
                te.wait_ge(SM["kev"], gseg + 1)
                te.wait_ge(SM["vev"], 8 * (gseg + 1))
                for tp in range(s, TP):
                    masked = (tp == s)
                    chunk_of = [nchunk + j for j in range(4)]
                    emitted_st = 0
                    for j in range(4):
                        while emitted_st <= j + 1 and emitted_st < 4:
                            jj = emitted_st
                            c = chunk_of[jj]
                            if c >= 2:
                                te.wait_ge(SM["sse"], c - 1)
                            half = (c % 2) * 256
                            te.matmul(psS[:, half:half + 256],
                                      KhS[:, jj * 128:(jj + 1) * 128],
                                      QhT[:, tp * 256:(tp + 1) * 256],
                                      start=True, stop=True
                                      ).then_inc(SM["spe"], 1)
                            emitted_st += 1
                        c = chunk_of[j]
                        if masked:
                            te.wait_ge(SM["ssm"], (h * NSEG + s) * 4 + j + 1)
                        else:
                            te.wait_ge(SM["sse"], c + 1)
                        if j == 0 and ngrp >= 1:
                            te.wait_ge(SM["oa"], ngrp)
                        src = STm[c % 4] if masked else STb[c % 4]
                        for nb in range(2):
                            for vh in range(2):
                                inst = te.matmul(
                                    ps2[nb * 2 + vh][:, 0:512],
                                    src[:, nb * 128:(nb + 1) * 128],
                                    VhS[:, j * 1024 + vh * 512:j * 1024 + vh * 512 + 512],
                                    start=(j == 0), stop=(j == 3))
                        if j == 3:
                            inst.then_inc(SM["g2"], 1)
                    nchunk += 4
                    ngrp += 1
    NCH = NG * 4 * NH
    p.bump("qpe", NH * TP)
    p.bump("kpe", NH * NSEG)
    p.bump("vpe", NH * NSEG * 8)
    p.bump("spe", NCH)
    p.bump("g2", NG * NH)

    @blk.scalar
    def _(sc):
        sc.wait_ge(SM[JN2[0]], JN2[1])
        nchunk = 0
        ngrp = 0
        for h in range(NH):
            for s in range(NSEG):
                for tp in range(s, TP):
                    for j in range(4):
                        c = nchunk + j
                        sc.wait_ge(SM["spe"], c + 1)
                        if ngrp >= 1:
                            sc.wait_ge(SM["g2"], ngrp)
                        half = (c % 2) * 256
                        sc.activation(out=STb[c % 4][:], in_=psS[:, half:half + 256],
                                      func=AF.Relu, scale=ISC
                                      ).then_inc(SM["sse"], 1)
                    nchunk += 4
                    ngrp += 1
    p.bump("sse", NCH)

    @blk.vector
    def _(ve):
        ve.wait_ge(SM[JN2[0]], JN2[1])
        ve.wait_ge(SM["mld"], 16)
        nchunk = 0
        ngrp = 0
        nvv = 0
        for h in range(NH):
            for tp in range(TP):
                it = h * TP + tp
                ve.wait_ge(SM["qpe"], it + 1)
                half = (tp % 2) * 256
                ve.tensor_copy(QhT[:, tp * 256:(tp + 1) * 256],
                               psKQ[:, half:half + 256]).then_inc(SM["qev"], 1)
            for s in range(NSEG):
                gseg = h * NSEG + s
                ve.wait_ge(SM["kpe"], gseg + 1)
                ve.tensor_copy(KhS[:], psKQ[:, 0:512]).then_inc(SM["kev"], 1)
                for i8 in range(8):
                    ve.wait_ge(SM["vpe"], nvv + 1)
                    mch, vh = i8 // 2, i8 % 2
                    ve.tensor_copy(
                        VhS[:, mch * 1024 + vh * 512:mch * 1024 + vh * 512 + 512],
                        psV[nvv % 2][:, 0:512]).then_inc(SM["vev"], 1)
                    nvv += 1
                for tp in range(s, TP):
                    masked = (tp == s)
                    if masked:
                        for j in range(4):
                            c = nchunk + j
                            ve.wait_ge(SM["sse"], c + 1)
                            if ngrp >= 1:
                                ve.wait_ge(SM["g2"], ngrp)
                            ve.tensor_tensor(STm[c % 4][:], STb[c % 4][:],
                                             maskT[:, j * 256:(j + 1) * 256],
                                             ALU.mult).then_inc(SM["ssm"], 1)
                    ve.wait_ge(SM["g2"], ngrp + 1)
                    first = (h == 0 and s == 0)
                    for nb in range(2):
                        for vh in range(2):
                            col = (tp * 2 + nb) * 1024 + vh * 512
                            if first:
                                inst = ve.tensor_copy(
                                    outacc[:, col:col + 512],
                                    ps2[nb * 2 + vh][:, 0:512])
                            else:
                                inst = ve.tensor_tensor(
                                    outacc[:, col:col + 512],
                                    ps2[nb * 2 + vh][:, 0:512],
                                    outacc[:, col:col + 512], ALU.add)
                    inst.then_inc(SM["oa"], 1)
                    nchunk += 4
                    ngrp += 1
        ve.drain()
    p.bump("qev", NH * TP)
    p.bump("kev", NH * NSEG)
    p.bump("vev", NH * NSEG * 8)
    p.bump("ssm", NH * NSEG * 4)
    p.bump("oa", NG * NH)

    JN3 = emit_join(p, blk, waits=[
        ("vector", [("oa", p.cnt("oa"))])])
    scM.close()

    scE = ExitStack()
    xe = [p.sbs(scE, f"p3_xe{i}", [128, D], F32) for i in range(2)]
    lnE = p.sbs(scE, "p3_lnE", [128, D], F32)
    aR = [p.sbs(scE, f"p3_aR{i}", [128, D], F32) for i in range(2)]
    x3t = [p.sbs(scE, f"p3_x3t{i}", [128, D], F32) for i in range(2)]
    bnE = [p.sbs(scE, f"p3_bnE{i}", [128, 16], F32) for i in range(2)]
    bnE2 = [p.sbs(scE, f"p3_bnE2{i}", [128, 16], F32) for i in range(2)]

    @blk.scalar
    def _(sc):
        sc.wait_ge(SM[JN3[0]], JN3[1])
        for i in range(32):
            if i >= 2:
                sc.wait_ge(SM["pe4"], i - 1)
            sc.dma_start(out=xe[i % 2][:],
                         in_=io["x2own"][i * 128:(i + 1) * 128, :]
                         ).then_inc(SM["xe0" if i % 2 == 0 else "xe1"], 16)
            sc.wait_ge(SM["pe1"], 2 * i + 1)
            sc.activation(out=bnE[i % 2][:, 15:16], in_=bnE[i % 2][:, 14:15],
                          func=AF.Sqrt).then_inc(SM["ps1"], 1)
            sc.wait_ge(SM["pe1"], 2 * i + 2)
            sc.activation(out=bnE2[i % 2][:, 15:16], in_=bnE2[i % 2][:, 14:15],
                          func=AF.Sqrt).then_inc(SM["ps1"], 1)
            sc.wait_ge(SM["pe4"], i + 1)
            sc.dma_start(out=io["x3own"][i * 128:(i + 1) * 128, :],
                         in_=x3t[i % 2][:]
                         ).then_inc(SM["est0" if i % 2 == 0 else "est1"], 16)
        sc.wait_ge(SM["est0"], 16 * 16)
        sc.wait_ge(SM["est1"], 16 * 16)

    @blk.vector
    def _(ve):
        ve.wait_ge(SM[JN3[0]], JN3[1])
        ps1base = p.cnt("ps1")
        for i in range(32):
            ve.wait_ge(SM["xe0" if i % 2 == 0 else "xe1"], 16 * (i // 2 + 1))
            b1 = bnE[i % 2]
            ve.bn_stats(b1[:, 0:6], xe[i % 2][:, 0:512])
            ve.bn_stats(b1[:, 6:12], xe[i % 2][:, 512:1024])
            ve.drain()
            ve.bn_aggr(b1[:, 12:14], b1[:, 0:12])
            ve.drain()
            ve.tensor_scalar(b1[:, 14:15], b1[:, 13:14], float(LN_EPS), None,
                             ALU.add).then_inc(SM["pe1"], 1)
            ve.wait_ge(SM["ps1"], ps1base + 2 * i + 1)
            ve.reciprocal(b1[:, 14:15], b1[:, 15:16])
            ve.drain()
            ve.tensor_scalar(lnE[:], xe[i % 2][:], b1[:, 12:13],
                             b1[:, 14:15], ALU.subtract, ALU.mult)
            ve.drain()
            if i >= 2:
                ve.wait_ge(SM["pe4"], i - 1)
            ve.tensor_tensor(aR[i % 2][:], lnE[:],
                             outacc[:, i * 1024:(i + 1) * 1024], ALU.add)
            ve.drain()
            b2 = bnE2[i % 2]
            ve.bn_stats(b2[:, 0:6], aR[i % 2][:, 0:512])
            ve.bn_stats(b2[:, 6:12], aR[i % 2][:, 512:1024])
            ve.drain()
            ve.bn_aggr(b2[:, 12:14], b2[:, 0:12])
            ve.drain()
            ve.tensor_scalar(b2[:, 14:15], b2[:, 13:14], float(LN_EPS), None,
                             ALU.add).then_inc(SM["pe1"], 1)
            ve.wait_ge(SM["ps1"], ps1base + 2 * i + 2)
            ve.reciprocal(b2[:, 14:15], b2[:, 15:16])
            ve.drain()
            ve.tensor_scalar(lnE[:], aR[i % 2][:], b2[:, 12:13],
                             b2[:, 14:15], ALU.subtract, ALU.mult)
            ve.drain()
            if i >= 2:
                ve.wait_ge(SM["est0" if i % 2 == 0 else "est1"], 16 * (i // 2))
            ve.tensor_tensor(x3t[i % 2][:], lnE[:], xe[i % 2][:], ALU.add
                             ).then_inc(SM["pe4"], 1)
        ve.drain()
    p.bump("pe1", 64)
    p.bump("ps1", 64)
    p.bump("pe4", 32)
    p.bump("xe0", 16 * 16)
    p.bump("xe1", 16 * 16)
    p.bump("est0", 16 * 16)
    p.bump("est1", 16 * 16)

    ret = emit_join(p, blk, waits=[
        ("scalar", [("est0", p.cnt("est0")), ("est1", p.cnt("est1"))]),
        ("vector", [("pe4", p.cnt("pe4"))])])
    scE.close()
    return ret


def p3_host_meta(slot):
    m = np.arange(128)[:, None]
    n = np.arange(256)[None, :]
    A = (m <= n).astype(np.float32)
    Bm = (m <= n - 128).astype(np.float32)
    one = np.ones((128, 256), np.float32)
    zero = np.zeros((128, 256), np.float32)
    if slot == 0:
        tab = np.concatenate([A, Bm, zero, zero], 1)
    else:
        tab = np.concatenate([one, one, A, Bm], 1)
    return tab.astype(bf16)


def p3_wk_layout(W):
    H = W.shape[0]
    blocks = []
    for h in range(H):
        WT = W[h].T
        for dc in range(8):
            blocks.append(WT[dc * 128:(dc + 1) * 128, :])
    return np.ascontiguousarray(np.concatenate(blocks, 1)).astype(bf16)


def p3_wv_layout(W):
    H = W.shape[0]
    blocks = []
    for h in range(H):
        WT = W[h].T
        for dc in range(8):
            blocks.append(WT[dc * 128:(dc + 1) * 128, :])
    return np.ascontiguousarray(np.concatenate(blocks, 1)).astype(bf16)


def p3_own_rows(slot):
    return np.concatenate([np.arange(512 * t + 256 * slot, 512 * t + 256 * slot + 256)
                           for t in range(16)])


"""Phase 4: L1_ffn sandwich stage.
x4 = x3 + LN( ffn(LN(x3)) ), ffn = 3-layer {0,1}-conn-dense net.
Per core: 4096 own token rows (local order), weights replicated.
"""

TW = 512
NTT = OWN // TW
NTILE = OWN // 128


def emit_p4(p, io):
    nc = p.nc
    blk = p.block

    ln4s = [p.sb(f"p4_ln4s{i}", [128, DCH * 128], BF16) for i in range(2)]
    ln4tt = [p.sb(f"p4_ln4tt{i}", [128, DCH * TW], BF16) for i in range(2)]
    x3t = [p.sb(f"p4_x3t{i}", [128, D], F32) for i in range(2)]
    lnt = [p.sb(f"p4_lnt{i}", [128, D], BF16) for i in range(2)]
    bnsc = [p.sb(f"p4_bn{i}", [128, 16], F32) for i in range(2)]
    k0t = [p.sb(f"p4_k0t{i}", [128, 8 * 256], U8) for i in range(2)]
    k0b = [p.sb(f"p4_k0b{i}", [128, 8 * 256], U8) for i in range(2)]
    k0d = [p.sb(f"p4_k0d{i}", [128, 8 * 256], BF16) for i in range(2)]
    k1b = [p.sb(f"p4_k1b{i}", [128, 32 * 256], BF16) for i in range(2)]
    h1 = p.sb("p4_h1", [128, 32 * TW], BF16)
    h2 = p.sb("p4_h2", [128, 32 * TW], BF16)
    y = p.sb("p4_y", [128, 8 * TW], F32)
    ysq = p.sb("p4_ysq", [128, 8 * TW], BF16)
    rs_f = p.sb("p4_rsf", [1, 3 * TW], F32)
    rs1h = p.sb("p4_rs1h", [1, 2 * TW], BF16)
    rs2h = p.sb("p4_rs2h", [1, 2 * TW], BF16)
    onescF = p.sb("p4_onescF", [128, 1], F32)
    ones1 = p.sb("p4_ones1", [1, 128], BF16)
    onescB = p.sb("p4_onescB", [128, 1], BF16)
    murow = p.sb("p4_murow", [1, TW], F32)
    varow = p.sb("p4_varow", [1, 3 * TW], F32)
    mucol = p.sb("p4_mucol", [128, 4], F32)
    rstcol = p.sb("p4_rstcol", [128, 4], F32)
    ytm = [p.sb("p4_ytm0", [128, D], F32)] * 2
    x3e = [p.sb(f"p4_x3e{i}", [128, D], F32) for i in range(2)]
    x4t = [p.sb(f"p4_x4t{i}", [128, D], F32) for i in range(2)]
    sv4 = p.sb("p4_sv4", [128, 3], F32)
    identb = p.sb("p4_identb", [128, 128], BF16)
    identf = p.sb("p4_identf", [128, 128], F32)

    psTb = [p.ps(f"p4_psTb{i}", [128, 1024], BF16) for i in range(2)]
    psA = [p.ps(f"p4_psA{i}", [128, TW]) for i in range(4)]
    psR = p.ps("p4_psR", [128, TW])

    for s in ["xtA", "xtB", "lk0A", "lk0B", "lk1A", "lk1B", "kd", "p1", "ev1",
              "pr1", "rs1", "p2", "ev2", "pr2", "rs2", "p3", "ev3", "sq8",
              "pstA", "mvA", "pstB", "vv", "sq2", "mvr", "eet", "eev", "x4v",
              "stA", "stB", "pv1", "ps1", "pv2", "ptl", "ple", "ini", "svr",
              "lnsA", "lnsB", "lktA", "lktB", "xeA", "xeB"]:
        p.sem(s)
    SM = p._sems

    def kidx(tt, i):
        return tt * 20 + i

    ln4h = io["ln4h"]


    @blk.sync
    def _(sync):
        def store(gq):
            tt, q = gq // 4, gq % 4
            sync.wait_ge(SM["x4v"], gq + 1)
            sync.dma_start(
                out=io["x4"][tt * TW + q * 128: tt * TW + (q + 1) * 128, :],
                in_=x4t[gq % 2][:],
            ).then_inc(SM["stA" if gq % 2 == 0 else "stB"], 16)

        nst = 0
        for tt in range(NTT):
            if tt >= 1:
                for q in range(4):
                    store((tt - 1) * 4 + q)
                    nst += 1
            sync.wait_ge(SM["lnsA"], 16 * (NTILE // 2))
            sync.wait_ge(SM["lnsB"], 16 * (NTILE // 2))
            if tt >= 2:
                sync.wait_ge(SM["p1"], tt * 16 - 16)
            sync.dma_start(
                out=ln4tt[tt % 2].ap().rearrange("p (j t) -> p j t", t=TW),
                in_=ln4h.rearrange("p (j t) -> p j t", t=OWN)[:, :, tt * TW:(tt + 1) * TW],
            ).then_inc(SM["lktA" if tt % 2 == 0 else "lktB"], 16)
            for i in range(16):
                g = tt * 16 + i
                if g >= 2:
                    sync.wait_ge(SM["p1"], g - 1)
                c0 = i * 256
                sync.dma_start(
                    out=k0t[g % 2].ap().rearrange("p (j c) -> p j c", c=256),
                    in_=io["k0"][0:1024, c0:c0 + 256].rearrange("(j p) c -> p j c", p=128),
                ).then_inc(SM["lk0A" if g % 2 == 0 else "lk0B"], 16)
                sync.dma_start(
                    out=k0b[g % 2].ap().rearrange("p (j c) -> p j c", c=256),
                    in_=io["k0"][1024:2048, c0:c0 + 256].rearrange("(j p) c -> p j c", p=128),
                ).then_inc(SM["lk0A" if g % 2 == 0 else "lk0B"], 16)
        for q in range(4):
            store((NTT - 1) * 4 + q)
            nst += 1
        sync.wait_ge(SM["stA"], 16 * (nst - nst // 2))
        sync.wait_ge(SM["stB"], 16 * (nst // 2))

    @blk.scalar
    def _(sc):
        sc.dma_start(out=identb[:], in_=io["identb"][:, :]).then_inc(SM["ini"], 16)
        sc.dma_start(out=identf[:], in_=io["identf"][:, :]).then_inc(SM["ini"], 16)
        sc.dma_start(out=sv4[:, 0:1], in_=io["s0v"][:, :]).then_inc(SM["ini"], 16)
        sc.dma_start(out=sv4[:, 1:2], in_=io["s1v"][:, :]).then_inc(SM["ini"], 16)
        sc.dma_start(out=sv4[:, 2:3], in_=io["s2v"][:, :]).then_inc(SM["ini"], 16)
        for i in range(NTILE):
            if i >= 2:
                sc.wait_ge(SM["pv2"], i - 1)
            sc.dma_start(out=x3t[i % 2][:],
                         in_=io["x3"][i * 128:(i + 1) * 128, :]
                         ).then_inc(SM["xtA" if i % 2 == 0 else "xtB"], 16)
            sc.wait_ge(SM["pv1"], i + 1)
            sc.activation(out=bnsc[i % 2][:, 15:16], in_=bnsc[i % 2][:, 14:15],
                          func=AF.Sqrt).then_inc(SM["ps1"], 1)
            sc.wait_ge(SM["ple"], i + 1)
            sc.dma_start(
                out=ln4h.rearrange("p (j t) -> p j t", t=OWN)[:, :, i * 128:(i + 1) * 128],
                in_=ln4s[i % 2].ap().rearrange("p (j c) -> p j c", c=128),
            ).then_inc(SM["lnsA" if i % 2 == 0 else "lnsB"], 16)
        sc.wait_ge(SM["svr"], 1)
        for tt in range(NTT):
            for i in range(16):
                g = tt * 16 + i
                sc.wait_ge(SM["p1"], g + 1)
                if tt >= 1 and i == 0:
                    sc.wait_ge(SM["p2"], tt * 16)
                pa = psA[0] if i % 2 == 0 else psA[2]
                pb = psA[1] if i % 2 == 0 else psA[3]
                sc.activation(out=h1[:, (2 * i) * TW:(2 * i + 1) * TW], in_=pa[:],
                              func=AF.Relu, scale=sv4[:, 0:1])
                sc.activation(out=h1[:, (2 * i + 1) * TW:(2 * i + 2) * TW], in_=pb[:],
                              func=AF.Relu, scale=sv4[:, 0:1]).then_inc(SM["ev1"], 1)
            for i in range(16):
                g = tt * 16 + i
                sc.wait_ge(SM["p2"], g + 1)
                if tt >= 1 and i == 0:
                    sc.wait_ge(SM["p3"], tt * 4)
                pa = psA[0] if i % 2 == 0 else psA[2]
                pb = psA[1] if i % 2 == 0 else psA[3]
                sc.activation(out=h2[:, (2 * i) * TW:(2 * i + 1) * TW], in_=pa[:],
                              func=AF.Relu, scale=sv4[:, 1:2])
                sc.activation(out=h2[:, (2 * i + 1) * TW:(2 * i + 2) * TW], in_=pb[:],
                              func=AF.Relu, scale=sv4[:, 1:2]).then_inc(SM["ev2"], 1)
            sc.wait_ge(SM["ev3"], tt * 4 + 4)
            sc.activation(out=ysq[:], in_=y[:], func=AF.Square).then_inc(SM["sq8"], 1)
            sc.wait_ge(SM["vv"], tt + 1)
            sc.activation(out=varow[0:1, TW:2 * TW], in_=varow[0:1, 0:TW],
                          func=AF.Sqrt).then_inc(SM["sq2"], 1)
            for q in range(4):
                gq = tt * 4 + q
                if gq >= 2:
                    sc.wait_ge(SM["x4v"], gq - 1)
                sc.dma_start(out=x3e[gq % 2][:],
                             in_=io["x3"][tt * TW + q * 128: tt * TW + (q + 1) * 128, :]
                             ).then_inc(SM["xeA" if gq % 2 == 0 else "xeB"], 16)

    @blk.gpsimd
    def _(gp):
        gp.memset(onescF[:], 1.0)
        gp.memset(ones1[:], 1.0)
        gp.memset(onescB[:], 1.0).then_inc(SM["ini"], 1)
        for tt in range(NTT):
            for i in range(16):
                g = tt * 16 + i
                gp.wait_ge(SM["lk0A" if g % 2 == 0 else "lk0B"], 32 * (g // 2 + 1))
                if g >= 2:
                    gp.wait_ge(SM["p1"], g - 1)
                gp.tensor_tensor(k0d[g % 2][:], k0t[g % 2][:], k0b[g % 2][:],
                                 ALU.subtract).then_inc(SM["kd"], 1)
            for i in range(20):
                g = kidx(tt, i)
                if g >= 2:
                    gp2 = g - 2
                    t2, i2 = gp2 // 20, gp2 % 20
                    if i2 < 16:
                        gp.wait_ge(SM["p2"], t2 * 16 + i2 + 1)
                    else:
                        gp.wait_ge(SM["p3"], t2 * 4 + (i2 - 16) + 1)
                if i < 16:
                    c0 = i * 256
                    src = io["k1"][:, c0:c0 + 256].rearrange("(j p) c -> p j c", p=128)
                else:
                    c0 = (i - 16) * 256
                    src = io["k2"][:, c0:c0 + 256].rearrange("(j p) c -> p j c", p=128)
                gp.dma_start(out=k1b[g % 2].ap().rearrange("p (j c) -> p j c", c=256),
                             in_=src).then_inc(SM["lk1A" if g % 2 == 0 else "lk1B"], 16)

    @blk.tensor
    def _(te):
        te.wait_ge(SM["ini"], 5 * 16 + 1)
        for i in range(NTILE):
            te.wait_ge(SM["pv2"], i + 1)
            if i >= 2:
                te.wait_ge(SM["ple"], i - 1)
            for j in range(DCH):
                inst = te.transpose(psTb[i % 2][:, j * 128:(j + 1) * 128],
                                    lnt[i % 2][:, j * 128:(j + 1) * 128], identb[:])
            inst.then_inc(SM["ptl"], 1)
        for tt in range(NTT):
            for i in range(16):
                g = tt * 16 + i
                te.wait_ge(SM["kd"], g + 1)
                if g >= 2:
                    te.wait_ge(SM["ev1"], g - 1)
                pa = psA[0] if i % 2 == 0 else psA[2]
                pb = psA[1] if i % 2 == 0 else psA[3]
                if i == 0:
                    te.wait_ge(SM["lktA" if tt % 2 == 0 else "lktB"], 16 * (tt // 2 + 1))
                for fi, ps_ in ((0, pa), (1, pb)):
                    for dj in range(DCH):
                        inst = te.matmul(
                            ps_[:],
                            k0d[g % 2][:, dj * 256 + fi * 128: dj * 256 + (fi + 1) * 128],
                            ln4tt[tt % 2][:, dj * TW: dj * TW + TW],
                            start=(dj == 0), stop=(dj == DCH - 1))
                inst.then_inc(SM["p1"], 1)
            te.wait_ge(SM["ev1"], tt * 16 + 16)
            if tt >= 1:
                te.wait_ge(SM["vv"], tt)
            for f in range(32):
                inst = te.matmul(psR[0:1, :], onescB[:, 0:1],
                                 h1[:, f * TW:(f + 1) * TW],
                                 start=(f == 0), stop=(f == 31))
            inst.then_inc(SM["pr1"], 1)
            for i in range(16):
                g = tt * 16 + i
                gk = kidx(tt, i)
                te.wait_ge(SM["lk1A" if gk % 2 == 0 else "lk1B"], 16 * (gk // 2 + 1))
                te.wait_ge(SM["rs1"], tt + 1)
                if g >= 2:
                    te.wait_ge(SM["ev2"], g - 1)
                pa = psA[0] if i % 2 == 0 else psA[2]
                pb = psA[1] if i % 2 == 0 else psA[3]
                for fi, ps_ in ((0, pa), (1, pb)):
                    for f in range(32):
                        te.matmul(
                            ps_[:],
                            k1b[gk % 2][:, f * 256 + fi * 128: f * 256 + (fi + 1) * 128],
                            h1[:, f * TW:(f + 1) * TW],
                            start=(f == 0), stop=False)
                    te.matmul(ps_[:], ones1[0:1, :], rs1h[0:1, 0:TW], start=False, stop=False)
                    inst = te.matmul(ps_[:], ones1[0:1, :], rs1h[0:1, TW:2 * TW], start=False, stop=True)
                inst.then_inc(SM["p2"], 1)
            te.wait_ge(SM["ev2"], tt * 16 + 16)
            te.wait_ge(SM["rs1"], tt + 1)
            for f in range(32):
                inst = te.matmul(psR[0:1, :], onescB[:, 0:1],
                                 h2[:, f * TW:(f + 1) * TW],
                                 start=(f == 0), stop=(f == 31))
            inst.then_inc(SM["pr2"], 1)
            for i in range(4):
                g = tt * 4 + i
                gk = kidx(tt, 16 + i)
                te.wait_ge(SM["lk1A" if gk % 2 == 0 else "lk1B"], 16 * (gk // 2 + 1))
                te.wait_ge(SM["rs2"], tt + 1)
                if g >= 2:
                    te.wait_ge(SM["ev3"], g - 1)
                pa = psA[0] if i % 2 == 0 else psA[2]
                pb = psA[1] if i % 2 == 0 else psA[3]
                for fi, ps_ in ((0, pa), (1, pb)):
                    for f in range(32):
                        te.matmul(
                            ps_[:],
                            k1b[gk % 2][:, f * 256 + fi * 128: f * 256 + (fi + 1) * 128],
                            h2[:, f * TW:(f + 1) * TW],
                            start=(f == 0), stop=False)
                    te.matmul(ps_[:], ones1[0:1, :], rs2h[0:1, 0:TW], start=False, stop=False)
                    inst = te.matmul(ps_[:], ones1[0:1, :], rs2h[0:1, TW:2 * TW], start=False, stop=True)
                inst.then_inc(SM["p3"], 1)
            te.wait_ge(SM["ev3"], tt * 4 + 4)
            te.wait_ge(SM["rs2"], tt + 1)
            for dj in range(DCH):
                inst = te.matmul(psR[0:1, :], onescF[:, 0:1], y[:, dj * TW:(dj + 1) * TW],
                                 start=(dj == 0), stop=(dj == DCH - 1))
            inst.then_inc(SM["pstA"], 1)
            te.wait_ge(SM["mvA"], tt + 1)
            te.wait_ge(SM["sq8"], tt + 1)
            for dj in range(DCH):
                inst = te.matmul(psR[0:1, :], onescB[:, 0:1], ysq[:, dj * TW:(dj + 1) * TW],
                                 start=(dj == 0), stop=(dj == DCH - 1))
            inst.then_inc(SM["pstB"], 1)
            te.wait_ge(SM["mvr"], tt + 1)
            for k in range(40):
                b = psA[k % 4]
                if k >= 4:
                    te.wait_ge(SM["eev"], 40 * tt + k - 3)
                if k < 8:
                    q = k // 2
                    if k % 2 == 0:
                        src = murow[0:1, q * 128:(q + 1) * 128]
                    else:
                        src = varow[0:1, 2 * TW + q * 128: 2 * TW + (q + 1) * 128]
                    te.transpose(b[:, 0:1], src, identf[0:1, 0:1]).then_inc(SM["eet"], 1)
                else:
                    q, dj = (k - 8) // 8, (k - 8) % 8
                    te.transpose(b[:, 0:128],
                                 y[:, dj * TW + q * 128: dj * TW + (q + 1) * 128],
                                 identf[:]).then_inc(SM["eet"], 1)

    @blk.vector
    def _(ve):
        ve.wait_ge(SM["ini"], 5 * 16 + 1)
        ve.tensor_scalar(sv4[:, 0:1], sv4[:, 0:1], 4.0, None, ALU.mult)
        ve.tensor_scalar(sv4[:, 1:2], sv4[:, 1:2], 4.0, None, ALU.mult)
        ve.tensor_scalar(sv4[:, 2:3], sv4[:, 2:3], 2.0, None, ALU.mult).then_inc(SM["svr"], 1)
        for i in range(NTILE):
            ve.wait_ge(SM["xtA" if i % 2 == 0 else "xtB"], 16 * (i // 2 + 1))
            sc_ = bnsc[i % 2]
            ve.bn_stats(sc_[:, 0:6], x3t[i % 2][:, 0:512])
            ve.bn_stats(sc_[:, 6:12], x3t[i % 2][:, 512:1024])
            ve.drain()
            ve.bn_aggr(sc_[:, 12:14], sc_[:, 0:12])
            ve.drain()
            ve.tensor_scalar(sc_[:, 14:15], sc_[:, 13:14], float(LN_EPS), None,
                             ALU.add).then_inc(SM["pv1"], 1)
            ve.wait_ge(SM["ps1"], i + 1)
            ve.reciprocal(sc_[:, 14:15], sc_[:, 15:16])
            ve.drain()
            if i >= 2:
                ve.wait_ge(SM["ptl"], i - 1)
            ve.tensor_scalar(lnt[i % 2][:], x3t[i % 2][:], sc_[:, 12:13], sc_[:, 14:15],
                             ALU.subtract, ALU.mult).then_inc(SM["pv2"], 1)
            ve.wait_ge(SM["ptl"], i + 1)
            if i >= 2:
                ve.wait_ge(SM["lnsA" if i % 2 == 0 else "lnsB"], 16 * (i // 2))
            ve.tensor_copy(ln4s[i % 2][:], psTb[i % 2][:, 0:DCH * 128]).then_inc(SM["ple"], 1)
        for tt in range(NTT):
            ve.wait_ge(SM["pr1"], tt + 1)
            ve.tensor_scalar(rs_f[0:1, 0:TW], psR[0:1, :], -0.5, None, ALU.mult)
            ve.drain()
            ve.tensor_copy(rs1h[0:1, 0:TW], rs_f[0:1, 0:TW])
            ve.drain()
            ve.tensor_tensor(rs_f[0:1, TW:2 * TW], rs_f[0:1, 0:TW], rs1h[0:1, 0:TW],
                             ALU.subtract)
            ve.drain()
            ve.tensor_copy(rs1h[0:1, TW:2 * TW], rs_f[0:1, TW:2 * TW]).then_inc(SM["rs1"], 1)
            ve.wait_ge(SM["pr2"], tt + 1)
            ve.tensor_scalar(rs_f[0:1, 0:TW], psR[0:1, :], -0.5, None, ALU.mult)
            ve.drain()
            ve.tensor_copy(rs2h[0:1, 0:TW], rs_f[0:1, 0:TW])
            ve.drain()
            ve.tensor_tensor(rs_f[0:1, TW:2 * TW], rs_f[0:1, 0:TW], rs2h[0:1, 0:TW],
                             ALU.subtract)
            ve.drain()
            ve.tensor_copy(rs2h[0:1, TW:2 * TW], rs_f[0:1, TW:2 * TW]).then_inc(SM["rs2"], 1)
            for i in range(4):
                ve.wait_ge(SM["p3"], tt * 4 + i + 1)
                if tt >= 1 and i == 0:
                    ve.wait_ge(SM["eet"], 40 * tt)
                pa = psA[0] if i % 2 == 0 else psA[2]
                pb = psA[1] if i % 2 == 0 else psA[3]
                ve.tensor_scalar(y[:, (2 * i) * TW:(2 * i + 1) * TW], pa[:],
                                 sv4[:, 2:3], None, ALU.mult)
                ve.tensor_scalar(y[:, (2 * i + 1) * TW:(2 * i + 2) * TW], pb[:],
                                 sv4[:, 2:3], None, ALU.mult).then_inc(SM["ev3"], 1)
            ve.wait_ge(SM["pstA"], tt + 1)
            ve.tensor_scalar(murow[0:1, :], psR[0:1, :], 1.0 / D, None,
                             ALU.mult).then_inc(SM["mvA"], 1)
            ve.wait_ge(SM["pstB"], tt + 1)
            ve.tensor_scalar(rs_f[0:1, 2 * TW:3 * TW], psR[0:1, :], 1.0 / D, None,
                             ALU.mult)
            ve.drain()
            ve.tensor_tensor(varow[0:1, 0:TW], murow[0:1, :], murow[0:1, :], ALU.mult)
            ve.drain()
            ve.tensor_tensor(varow[0:1, 0:TW], rs_f[0:1, 2 * TW:3 * TW],
                             varow[0:1, 0:TW], ALU.subtract)
            ve.drain()
            ve.tensor_scalar(varow[0:1, 0:TW], varow[0:1, 0:TW], float(LN_EPS), None,
                             ALU.add).then_inc(SM["vv"], 1)
            ve.wait_ge(SM["sq2"], tt + 1)
            ve.reciprocal(varow[0:1, 2 * TW:3 * TW], varow[0:1, TW:2 * TW]).then_inc(SM["mvr"], 1)
            for k in range(40):
                ve.wait_ge(SM["eet"], 40 * tt + k + 1)
                b = psA[k % 4]
                if k < 8:
                    q = k // 2
                    dst = mucol if k % 2 == 0 else rstcol
                    ve.tensor_copy(dst[:, q:q + 1], b[:, 0:1]).then_inc(SM["eev"], 1)
                    if k == 7:
                        ve.drain()
                else:
                    q, dj = (k - 8) // 8, (k - 8) % 8
                    gq = tt * 4 + q
                    ve.tensor_scalar(ytm[gq % 2][:, dj * 128:(dj + 1) * 128],
                                     b[:, 0:128], mucol[:, q:q + 1], rstcol[:, q:q + 1],
                                     ALU.subtract, ALU.mult).then_inc(SM["eev"], 1)
                if k >= 8 and (k - 8) % 8 == 7:
                    q = (k - 8) // 8
                    gq = tt * 4 + q
                    ve.drain()
                    ve.wait_ge(SM["xeA" if gq % 2 == 0 else "xeB"], 16 * (gq // 2 + 1))
                    if gq >= 2:
                        ve.wait_ge(SM["stA" if gq % 2 == 0 else "stB"], 16 * (gq // 2))
                    ve.tensor_tensor(x4t[gq % 2][:], ytm[gq % 2][:], x3e[gq % 2][:],
                                     ALU.add).then_inc(SM["x4v"], 1)
                    ve.drain()




def _own_idx(slot):
    return np.concatenate(
        [np.arange(512 * t + 256 * slot, 512 * t + 256 * slot + 256)
         for t in range(16)])


def _ownrows_tab(slot):
    r = _own_idx(slot).astype(np.int32)
    return np.ascontiguousarray(r.reshape(32, 128).T)


def kernel(_dump=None, **inputs):
    cc = np.ascontiguousarray
    f32 = np.float32
    x0 = np.asarray(inputs["L0_attn_x"], f32)
    identb_np = np.eye(128, dtype=bf16)
    identf_np = np.eye(128, dtype=f32)

    p = Prog()
    io = {}
    io["xs1"] = p.din("xs1", [2 * S, D], F32)
    for nm in ("Wk1", "Wq1", "Wk2", "Wproj"):
        io[nm] = p.din(nm, [D, D], F32)
    for nm in ("bk1c", "bq1c", "bk2c", "bprojc"):
        io[nm] = p.din(nm, [128, 8], F32)
    io["centTc"] = p.din("centTc", [128, 64], F32)
    io["gB"] = p.din("gB", [128, D], F32)
    io["bB"] = p.din("bB", [128, D], F32)
    io["identb"] = p.din("identb", [128, 128], BF16)
    io["identf"] = p.din("identf", [128, 128], F32)
    io["ln1h"] = p.dint("ln1h", [128, DCH * S], BF16)
    io["oh"] = p.dint("oh", [128, DCH * S], BF16)
    io["x1out"] = p.dout("x1out", [S, D], F32)
    io["normout"] = p.dout("normout", [S, D], BF16)
    io["scout"] = p.dout("scout", [128, 1024], F32)
    p.start_block()
    emit_p1(p, io)
    p.es.close()

    cent = np.asarray(inputs["L0_ffn_centroids"], f32)
    shared1 = {
        "Wk1": np.asarray(inputs["L0_attn_Wk1"], f32),
        "Wq1": np.asarray(inputs["L0_attn_Wq1"], f32),
        "Wk2": np.asarray(inputs["L0_attn_Wk2"], f32),
        "Wproj": np.asarray(inputs["L0_attn_Wproj"], f32),
        "bk1c": cc(np.asarray(inputs["L0_attn_bk1"], f32).reshape(8, 128).T),
        "bq1c": cc(np.asarray(inputs["L0_attn_bq1"], f32).reshape(8, 128).T),
        "bk2c": cc(np.asarray(inputs["L0_attn_bk2"], f32).reshape(8, 128).T),
        "bprojc": cc(np.asarray(inputs["L0_attn_bproj"], f32).reshape(8, 128).T),
        "centTc": cc(cent.T.reshape(8, 128, 8).transpose(1, 0, 2).reshape(128, 64)),
        "gB": cc(np.broadcast_to(np.asarray(inputs["L0_ffn_ln_g"], f32), (128, D))),
        "bB": cc(np.broadcast_to(np.asarray(inputs["L0_ffn_ln_b"], f32), (128, D))),
        "identb": identb_np, "identf": identf_np,
    }
    in_maps = []
    for c in range(NCORES):
        pair, slot = c // 2, c % 2
        bA, bB_ = ((pair, (pair + 1) % 4) if slot == 0 else
                   ((pair + 2) % 4, (pair + 3) % 4))
        im = dict(shared1)
        im["xs1"] = cc(np.concatenate([x0[bA], x0[bB_]], 0))
        in_maps.append(im)
    res1 = run_bass_kernel_spmd(p.nc, in_maps, core_ids=list(range(NCORES)))
    x1b, normb, scoresb = {}, {}, {}
    for b in range(4):
        r = res1.results[2 * b]
        x1b[b] = np.asarray(r["x1out"], f32)
        normb[b] = np.asarray(r["normout"])
        scoresb[b] = np.asarray(r["scout"], f32)[0:64].reshape(S, 8)

    nexp = 4
    p = Prog()
    io = {}
    io["scoresF"] = p.din("scoresF", [256, 1024], F32)
    io["normH"] = p.din("normH", [S + 128, D], BF16)
    io["x1own"] = p.din("x1own", [OWN, D], F32)
    io["w1my"] = p.din("w1my", [nexp * D, 4096], F32)
    io["w2my"] = p.din("w2my", [nexp * 4096, D], F32)
    io["b1c"] = p.din("b1c", [128, 32 * nexp], F32)
    io["b2c"] = p.din("b2c", [128, 8 * nexp], F32)
    io["gidx"] = p.din("gidx", [128, (CAP // 128) * nexp], I32)
    io["ownrows"] = p.din("ownrows", [128, 32], I32)
    io["tokid"] = p.din("tokid", [128, 64], I32)
    io["identb"] = p.din("identb", [128, 128], BF16)
    io["identf"] = p.din("identf", [128, 128], F32)
    io["S_h"] = p.dout("S_h", [8 * CAP + 128, 1], I32)
    io["yH"] = p.dout("yH", [S + 128, D], BF16)
    io["x2out"] = p.dout("x2out", [OWN, D], F32)
    p.start_block()
    emit_p2(p, io, nexp=nexp)
    p.es.close()

    w1 = np.asarray(inputs["L0_ffn_w1"], f32)
    w2 = np.asarray(inputs["L0_ffn_w2"], f32)
    b1 = np.asarray(inputs["L0_ffn_b1"], f32)
    b2 = np.asarray(inputs["L0_ffn_b2"], f32)
    tokid_np = (np.arange(64)[None, :] * 128
                + np.arange(128)[:, None]).astype(np.int32)
    in_maps = []
    for c in range(NCORES):
        pair, slot = c // 2, c % 2
        rot = [(pair + k) % 4 for k in range(4)]
        scoresL = np.concatenate([scoresb[b] for b in rot], 0)
        ee = range(4 * slot, 4 * slot + 4)
        im = {
            "scoresF": cc(scoresL.reshape(256, 1024)),
            "normH": cc(np.concatenate(
                [normb[pair], np.zeros((128, D), bf16)], 0)),
            "x1own": cc(x1b[pair][_own_idx(slot)]),
            "w1my": cc(w1[4 * slot:4 * slot + 4].reshape(nexp * D, 4096)),
            "w2my": cc(w2[4 * slot:4 * slot + 4].reshape(nexp * 4096, D)),
            "b1c": cc(np.concatenate([b1[j].reshape(32, 128).T for j in ee], 1)),
            "b2c": cc(np.concatenate([b2[j].reshape(8, 128).T for j in ee], 1)),
            "gidx": cc((slot * 4 * CAP
                        + np.arange(4 * CAP, dtype=np.int32)).reshape(-1, 128).T),
            "ownrows": _ownrows_tab(slot),
            "tokid": tokid_np,
            "identb": identb_np, "identf": identf_np,
        }
        in_maps.append(im)
    res2 = run_bass_kernel_spmd(p.nc, in_maps, core_ids=list(range(NCORES)))

    if _dump is not None:
        _dump["x1b"] = x1b; _dump["normb"] = normb; _dump["scoresb"] = scoresb
        _dump["S_h"] = {pr: np.asarray(res2.results[2 * pr]["S_h"]).reshape(-1)
                        for pr in range(4)}
    yHm = {}
    for pair in range(4):
        S0 = np.asarray(res2.results[2 * pair]["S_h"]).reshape(-1)
        y0 = np.asarray(res2.results[2 * pair]["yH"])
        y1 = np.asarray(res2.results[2 * pair + 1]["yH"])
        ym = np.zeros((S + 128, D), bf16)
        for e in range(8):
            tk = S0[e * CAP:(e + 1) * CAP]
            tk = tk[(tk >= 0) & (tk < S)]
            ym[tk] = (y0 if e < 4 else y1)[tk]
        yHm[pair] = ym

    p = Prog()
    io = {}
    io["ownrows"] = p.din("ownrows", [128, 32], I32)
    io["x1own"] = p.din("x1own", [OWN, D], F32)
    io["yH"] = p.din("yH", [S + 128, D], BF16)
    io["x2out"] = p.dout("x2out", [OWN, D], F32)
    p.start_block()
    emit_p2_owner(p, io)
    p.es.close()
    in_maps = []
    for c in range(NCORES):
        pair, slot = c // 2, c % 2
        in_maps.append({
            "ownrows": _ownrows_tab(slot),
            "x1own": cc(x1b[pair][_own_idx(slot)]),
            "yH": yHm[pair],
        })
    res3 = run_bass_kernel_spmd(p.nc, in_maps, core_ids=list(range(NCORES)))

    if _dump is not None:
        _dump["x2"] = {c: np.asarray(res3.results[c]["x2out"], f32)
                       for c in range(NCORES)}
        _dump["yHm"] = yHm
    p = Prog()
    io = {}
    io["x2own"] = p.din("x2own", [4096, D], F32)
    io["x2full"] = p.din("x2full", [S, D], F32)
    io["WkTh"] = p.din("WkTh", [128, 8 * 8 * 128], BF16)
    io["WqTh"] = p.din("WqTh", [128, 8 * 8 * 128], BF16)
    io["WvTh"] = p.din("WvTh", [128, 8 * 8 * 1024], BF16)
    io["maskTab"] = p.din("maskTab", [128, 4 * 256], BF16)
    io["identb"] = p.din("identb", [128, 128], BF16)
    io["ln3fH"] = p.dint("ln3fH", [128, DCH * S], BF16)
    io["ln3oH"] = p.dint("ln3oH", [128, DCH * 4096], BF16)
    io["x3own"] = p.dout("x3own", [4096, D], F32)
    p.start_block()
    emit_p3(p, io)
    p.es.close()

    WkTh = p3_wk_layout(np.asarray(inputs["L1_attn_Wk"], f32))
    WqTh = p3_wk_layout(np.asarray(inputs["L1_attn_Wq"], f32))
    WvTh = p3_wv_layout(np.asarray(inputs["L1_attn_Wv"], f32))
    x2full = {}
    for pair in range(4):
        xf = np.empty((S, D), f32)
        xf[_own_idx(0)] = np.asarray(res3.results[2 * pair]["x2out"], f32)
        xf[_own_idx(1)] = np.asarray(res3.results[2 * pair + 1]["x2out"], f32)
        x2full[pair] = xf
    in_maps = []
    for c in range(NCORES):
        pair, slot = c // 2, c % 2
        in_maps.append({
            "x2own": cc(np.asarray(res3.results[c]["x2out"], f32)),
            "x2full": x2full[pair],
            "WkTh": WkTh, "WqTh": WqTh, "WvTh": WvTh,
            "maskTab": p3_host_meta(slot),
            "identb": identb_np,
        })
    res4 = run_bass_kernel_spmd(p.nc, in_maps, core_ids=list(range(NCORES)))

    if _dump is not None:
        _dump["x3"] = {c: np.asarray(res4.results[c]["x3own"], f32)
                       for c in range(NCORES)}
    p = Prog()
    io = {}
    io["x3"] = p.din("x3", [OWN, D], F32)
    io["k0"] = p.din("k0", [2 * D, 4096], U8)
    io["k1"] = p.din("k1", [4096, 4096], U8)
    io["k2"] = p.din("k2", [4096, D], U8)
    for nm in ("s0v", "s1v", "s2v"):
        io[nm] = p.din(nm, [128, 1], F32)
    io["identb"] = p.din("identb", [128, 128], BF16)
    io["identf"] = p.din("identf", [128, 128], F32)
    io["x4"] = p.dout("x4", [OWN, D], F32)
    io["ln4h"] = p.dint("ln4h", [128, DCH * OWN], BF16)
    p.start_block()
    emit_p4(p, io)
    p.es.close()
    base5 = {
        "k0": np.asarray(inputs["L1_ffn_k0"]).astype(np.uint8),
        "k1": np.asarray(inputs["L1_ffn_k1"]).astype(np.uint8),
        "k2": np.asarray(inputs["L1_ffn_k2"]).astype(np.uint8),
        "s0v": np.full((128, 1), np.asarray(inputs["L1_ffn_s0"], f32), f32),
        "s1v": np.full((128, 1), np.asarray(inputs["L1_ffn_s1"], f32), f32),
        "s2v": np.full((128, 1), np.asarray(inputs["L1_ffn_s2"], f32), f32),
        "identb": identb_np, "identf": identf_np,
    }
    in_maps = [dict(base5, x3=cc(np.asarray(res4.results[c]["x3own"], f32)))
               for c in range(NCORES)]
    res5 = run_bass_kernel_spmd(p.nc, in_maps, core_ids=list(range(NCORES)))

    out = np.empty((4, S, D), f32)
    for c in range(NCORES):
        pair, slot = c // 2, c % 2
        out[pair][_own_idx(slot)] = np.asarray(res5.results[c]["x4"], f32)
    return out


# version 03
# speedup vs baseline: 1.0023x; 1.0023x over previous
import sys
sys.path.insert(0, "/opt/trn_rl_repo")
import math
import numpy as np
import ml_dtypes
import concourse.bass as bass
import concourse.mybir as mybir
from concourse.bass_utils import run_bass_kernel_spmd
from concourse.bass import IndirectOffsetOnAxis
from contextlib import ExitStack

F32 = mybir.dt.float32
BF16 = mybir.dt.bfloat16
I32 = mybir.dt.int32
U8 = mybir.dt.uint8
AF = mybir.ActivationFunctionType
ALU = mybir.AluOpType
AX = mybir.AxisListType
POOL = mybir.EngineType.Pool

B, S, D = 4, 8192, 1024
NCORES = 8
OWN = 4096
DCH = D // 128
LN_EPS = 1e-5

bf16 = ml_dtypes.bfloat16


def own_rows(slot):
    if slot == 0:
        return np.concatenate([np.arange(0, 2048), np.arange(6144, 8192)])
    return np.arange(2048, 6144)


class _SemMap(dict):

    def __init__(self, prog):
        super().__init__()
        self._prog = prog

    def __missing__(self, k):
        self._prog.sem(k)
        return self[k]


class Prog:

    def __init__(self, name="prog"):
        self.nc = bass.Bass("TRN2", target_bir_lowering=False)
        self.es = ExitStack()
        self._semcnt = {}
        self._sems = _SemMap(self)
        self._semcm = {}
        self._tens_cms = []
        self.block = None
        self._secs = {}

    def sem(self, name):
        if name not in self._sems:
            cm = self.nc.semaphore(name)
            self._sems[name] = cm.__enter__()
            self._semcm[name] = cm
            self._semcnt[name] = 0
        return self._sems[name]

    def free_sems(self, gp, names):
        inst = None
        for n in names:
            if self._semcnt[n] > 0:
                gp.wait_ge(self._sems[n], self._semcnt[n])
        for n in names:
            inst = gp.sem_clear(self._sems[n])
        for n in names:
            self._semcm.pop(n).__exit__(None, None, None)
            del self._sems[n]
            del self._semcnt[n]
        return inst

    def bump(self, name, n):
        self.sem(name)
        self._semcnt[name] += n
        return self._semcnt[name]

    def cnt(self, name):
        return self._semcnt.get(name, 0)

    def sb(self, name, shape, dtype):
        cm = self.nc.sbuf_tensor(name, list(shape), dtype)
        h = cm.__enter__()
        self._tens_cms.append(cm)
        return h

    def sbs(self, scope, name, shape, dtype):
        return scope.enter_context(self.nc.sbuf_tensor(name, list(shape), dtype))

    def ps(self, name, shape, dtype=F32):
        cm = self.nc.psum_tensor(name, list(shape), dtype)
        h = cm.__enter__()
        self._tens_cms.append(cm)
        return h

    def pss(self, scope, name, shape, dtype=F32):
        return scope.enter_context(self.nc.psum_tensor(name, list(shape), dtype))

    def reset_phase(self):
        self._sems.clear()
        self._semcnt.clear()
        self._semcm.clear()
        self._tens_cms.clear()
        self.nc.reset()

    def din(self, name, shape, dtype):
        return self.nc.dram_tensor(name, list(shape), dtype, kind="ExternalInput").ap()

    def dout(self, name, shape, dtype):
        return self.nc.dram_tensor(name, list(shape), dtype, kind="ExternalOutput").ap()

    def dint(self, name, shape, dtype, shared=False):
        return self.nc.dram_tensor(
            name, list(shape), dtype, kind="Internal",
            addr_space="Shared" if shared else "Local").ap()

    def start_block(self):
        self._block_cm = self.nc.Block()
        self.block = self._block_cm.__enter__()
        self.es.callback(self.end_block)

    def end_block(self):
        if self.block is not None:
            self._block_cm.__exit__(None, None, None)
            self.block = None

    def dma(self, eng, out, in_, sem):
        eng.dma_start(out=out, in_=in_).then_inc(self._sems[sem], 16)
        return self.bump(sem, 16)

    def idma(self, gp, out, out_idx, in_, in_idx, sem):
        kw = {}
        if out_idx is not None:
            kw["out_offset"] = IndirectOffsetOnAxis(ap=out_idx, axis=0)
            kw["in_offset"] = None
        else:
            kw["out_offset"] = None
            kw["in_offset"] = IndirectOffsetOnAxis(ap=in_idx, axis=0)
        gp.indirect_dma_start(out=out, in_=in_, oob_is_err=False, **kw).then_inc(
            self._sems[sem], 16)
        return self.bump(sem, 16)

    def inc(self, inst, sem, n=1):
        inst.then_inc(self._sems[sem], n)
        return self.bump(sem, n)



def np_bf16(x):
    return np.asarray(x).astype(bf16)


def identity_bf16():
    return np.eye(128, dtype=bf16)


def identity_f32():
    return np.eye(128, dtype=np.float32)




def emit_phase_boundary(p, blk, JN, barrier_fn=None):
    SM = p._sems

    @blk.gpsimd
    def _(gp):
        gp.wait_ge(SM[JN[0]], JN[1])
        for n in list(p._sems):
            if n != JN[0] and p._semcnt[n] > 0:
                gp.wait_ge(p._sems[n], p._semcnt[n])
        if barrier_fn is not None:
            barrier_fn(gp)

    p.end_block()
    p.reset_phase()
    p.start_block()
    return None


_join_counter = [0]


def emit_join(p, blk, jn=None, waits=None):
    SM = p._sems
    if jn is None:
        jn = "jn"
    p.sem(jn)
    wmap = {k: v for k, v in (waits or [])}
    prev = p.cnt(jn)

    def mk(engname, engsec):
        def body(eng):
            if prev > 0:
                eng.wait_ge(SM[jn], prev)
            for sm, cnt in wmap.get(engname, []):
                eng.wait_ge(SM[sm], cnt)
            eng.nop().then_inc(SM[jn], 1)
        return body

    for engname in ("sync", "gpsimd", "scalar", "tensor", "vector"):
        sec = getattr(blk, engname)
        sec(mk(engname, None))
        p.bump(jn, 1)
    return (jn, p.cnt(jn))


"""Phase 1: stage-1 = LN + L0_attn + postLN + residual + ln2/norm/scores, TWO batches.

io keys: xs1 [2*S,D]f32, Wk1,Wq1,Wk2,Wproj [D,D]f32,
bk1c,bq1c,bk2c,bprojc [128,8]f32 (col j = bias[j*128:(j+1)*128]),
centTc [128,64]f32 (col j*8+e = centroids[e, j*128+p]), gB,bB [128,D]f32,
identb, identf, ln1h/oh [128, DCH*S]bf16 scratch,
x1out [S,D]f32, normout [S,D]bf16, scout [128,1024]f32 (row b*64+tile).
"""

NT1 = S // 128
NCT = S // 512
C2 = 6.25e-4
C2H = C2 / 2.0
CH = 6
PL = 2


def emit_p1(p, io):
    nc = p.nc
    blk = p.block

    identb = p.sb("p1_identb", [128, 128], BF16)
    identf = p.sb("p1_identf", [128, 128], F32)
    biases = p.sb("p1_biases", [128, 32], F32)
    epsc = p.sb("p1_epsc", [128, 1], F32)
    centT = p.sb("p1_centT", [128, 64], F32)
    wtmp = p.sb("p1_wtmp", [128, D], BF16)
    zcol = p.sb("p1_zcol", [128, 1], F32)
    onescF = p.sb("p1_onescF", [128, 1], F32)
    onescB = p.sb("p1_onescB", [128, 1], BF16)

    psTb = [p.ps(f"p1_psTb{i}", [128, 1024], BF16) for i in range(2)]
    psA = [p.ps(f"p1_psA{i}", [128, 512]) for i in range(4)]
    psR = p.ps("p1_psR", [128, 512])

    SM = p._sems

    ln1h_3d = io["ln1h"].rearrange("p (j t) -> p j t", t=S)
    oh_3d = io["oh"].rearrange("p (j t) -> p j t", t=S)

    @blk.sync
    def _(sync):
        for dst, src in ((identb[:], io["identb"]), (identf[:], io["identf"]),
                         (biases[:, 0:8], io["bk1c"]), (biases[:, 8:16], io["bq1c"]),
                         (biases[:, 16:24], io["bk2c"]), (biases[:, 24:32], io["bprojc"]),
                         (centT[:], io["centTc"])):
            sync.dma_start(out=dst, in_=src[:, :]).then_inc(SM["ini"], 16)
    p.bump("ini", 7 * 16)

    @blk.gpsimd
    def _(gp):
        gp.memset(zcol[:], 0.0)
        gp.memset(epsc[:], 1e-6)
        gp.memset(onescF[:], 1.0)
        gp.memset(onescB[:], 1.0).then_inc(SM["ini"], 1)
    p.bump("ini", 1)
    INI = p.cnt("ini")

    @blk.vector
    def _(ve):
        ve.wait_ge(SM["ini"], INI)
        ve.tensor_scalar(biases[:, 8:16], biases[:, 8:16], 0.5, None, ALU.mult)
        ve.tensor_scalar(biases[:, 16:24], biases[:, 16:24], C2H, None, ALU.mult)
        ve.drain()
        ve.nop().then_inc(SM["wev0"], 1)
    p.bump("wev0", 1)

    JN = emit_join(p, blk, waits=[("sync", [("ini", INI)]),
                                  ("vector", [("wev0", p.cnt("wev0"))])])

    for b in range(2):
        xsrc = io["xs1"][b * S:(b + 1) * S, :]

        scopeA = ExitStack()
        x0t = [p.sbs(scopeA, f"p1_x0t{i}_{b}", [128, D], F32) for i in range(2)]
        lnt = [p.sbs(scopeA, f"p1_lnt{i}_{b}", [128, D], BF16) for i in range(2)]
        bnsc = [p.sbs(scopeA, f"p1_bn{i}_{b}", [128, 16], F32) for i in range(2)]
        ln1s = [p.sbs(scopeA, f"p1_ln1s{i}_{b}", [128, DCH * 128], BF16) for i in range(2)]

        A = dict(ps1=p.cnt("ps1"), pv1=p.cnt("pv1"), pv2=p.cnt("pv2"),
                 ptl=p.cnt("ptl"), ple=p.cnt("ple"),
                 xt=[p.cnt("xt0"), p.cnt("xt1")],
                 lns=[p.cnt("lns0"), p.cnt("lns1")],
                 wev=p.cnt("wev"), wev0=p.cnt("wev0"), eV=p.cnt("eV"))

        @blk.scalar
        def _(sc, b=b, A=A, xsrc=xsrc, JN=JN):
            sc.wait_ge(SM[JN[0]], JN[1])
            for i in range(NT1):
                if i >= 2:
                    sc.wait_ge(SM["pv2"], A["pv2"] + i - 1)
                sc.dma_start(out=x0t[i % 2][:], in_=xsrc[i * 128:(i + 1) * 128, :]
                             ).then_inc(SM["xt0" if i % 2 == 0 else "xt1"], 16)
                sc.wait_ge(SM["pv1"], A["pv1"] + i + 1)
                sc.activation(out=bnsc[i % 2][:, 15:16], in_=bnsc[i % 2][:, 14:15],
                              func=AF.Sqrt).then_inc(SM["ps1"], 1)
                sc.wait_ge(SM["ple"], A["ple"] + i + 1)
                sc.dma_start(out=ln1h_3d[:, :, i * 128:(i + 1) * 128],
                             in_=ln1s[i % 2].ap().rearrange("p (j c) -> p j c", c=128)
                             ).then_inc(SM["lns0" if i % 2 == 0 else "lns1"], 16)

        @blk.tensor
        def _(te, A=A, JN=JN):
            te.wait_ge(SM[JN[0]], JN[1])
            for i in range(NT1):
                te.wait_ge(SM["pv2"], A["pv2"] + i + 1)
                if i >= 2:
                    te.wait_ge(SM["ple"], A["ple"] + i - 1)
                for j in range(DCH):
                    inst = te.transpose(psTb[i % 2][:, j * 128:(j + 1) * 128],
                                        lnt[i % 2][:, j * 128:(j + 1) * 128], identb[:])
                inst.then_inc(SM["ptl"], 1)

        @blk.vector
        def _(ve, A=A, JN=JN):
            ve.wait_ge(SM[JN[0]], JN[1])
            for i in range(NT1):
                ve.wait_ge(SM["xt0" if i % 2 == 0 else "xt1"],
                           A["xt"][i % 2] + 16 * (i // 2 + 1))
                sc_ = bnsc[i % 2]
                ve.bn_stats(sc_[:, 0:6], x0t[i % 2][:, 0:512])
                ve.bn_stats(sc_[:, 6:12], x0t[i % 2][:, 512:1024])
                ve.drain()
                ve.bn_aggr(sc_[:, 12:14], sc_[:, 0:12])
                ve.drain()
                ve.tensor_scalar(sc_[:, 14:15], sc_[:, 13:14], float(LN_EPS), None,
                                 ALU.add).then_inc(SM["pv1"], 1)
                ve.wait_ge(SM["ps1"], A["ps1"] + i + 1)
                ve.reciprocal(sc_[:, 14:15], sc_[:, 15:16])
                ve.drain()
                if i >= 2:
                    ve.wait_ge(SM["ptl"], A["ptl"] + i - 1)
                ve.tensor_scalar(lnt[i % 2][:], x0t[i % 2][:], sc_[:, 12:13],
                                 sc_[:, 14:15], ALU.subtract, ALU.mult
                                 ).then_inc(SM["pv2"], 1)
                ve.wait_ge(SM["ptl"], A["ptl"] + i + 1)
                if i >= 2:
                    ve.wait_ge(SM["lns0" if i % 2 == 0 else "lns1"],
                               A["lns"][i % 2] + 16 * (i // 2))
                ve.tensor_copy(ln1s[i % 2][:], psTb[i % 2][:, 0:DCH * 128]
                               ).then_inc(SM["ple"], 1)
            ve.drain()

        for s_, n_ in (("ps1", NT1), ("pv1", NT1), ("pv2", NT1), ("ptl", NT1),
                       ("ple", NT1), ("xt0", 16 * NT1 // 2), ("xt1", 16 * NT1 // 2),
                       ("lns0", 16 * NT1 // 2), ("lns1", 16 * NT1 // 2)):
            p.bump(s_, n_)

        JN = emit_join(p, blk, waits=[
            ("scalar", [("xt0", p.cnt("xt0")), ("xt1", p.cnt("xt1")),
                        ("lns0", p.cnt("lns0")), ("lns1", p.cnt("lns1"))]),
            ("vector", [("ple", p.cnt("ple"))]),
            ("tensor", [("ptl", p.cnt("ptl"))])])
        scopeA.close()

        scopeB = ExitStack()
        b1 = p.sbs(scopeB, f"p1_b1_{b}", [128, S], BF16)
        b2 = p.sbs(scopeB, f"p1_b2_{b}", [128, S], BF16)
        b3 = p.sbs(scopeB, f"p1_b3_{b}", [128, S], BF16)
        b4 = p.sbs(scopeB, f"p1_b4_{b}", [128, S], BF16)
        b5 = p.sbs(scopeB, f"p1_b5_{b}", [128, S], BF16)
        b6 = p.sbs(scopeB, f"p1_b6_{b}", [128, S], BF16)
        f1 = p.sbs(scopeB, f"p1_f1_{b}", [128, S], F32)
        f2 = p.sbs(scopeB, f"p1_f2_{b}", [128, S], F32)
        wk = [p.sbs(scopeB, f"p1_wk{i}_{b}", [128, D], BF16) for i in range(2)]
        wkT = [p.sbs(scopeB, f"p1_wkT{i}_{b}", [128, DCH * 128], BF16) for i in range(3)]
        ln1r = [p.sbs(scopeB, f"p1_ln1r{i}_{b}", [128, DCH * 512], BF16) for i in range(2)]

        C = dict(lnc=p.cnt("lnc"), lnr=[p.cnt("lnr0"), p.cnt("lnr1")],
                 wld=[p.cnt("wldA"), p.cnt("wldB")], wtr=p.cnt("wtr"), wev=p.cnt("wev"),
                 prj=p.cnt("prj"), evS=p.cnt("evS"), evV=p.cnt("evV"),
                 scK2=p.cnt("scK2"), scS=p.cnt("scS"), scn=p.cnt("scn"),
                 chn=p.cnt("chn"), pool=p.cnt("pool"), ost=p.cnt("ost"),
                 lns=[p.cnt("lns0"), p.cnt("lns1")], ple=p.cnt("ple"))

        @blk.sync
        def _(sync, C=C, JN=JN):
            sync.wait_ge(SM[JN[0]], JN[1])
            for e in range(DCH):
                if e >= 1:
                    sync.wait_ge(SM["pool"], C["pool"] + e * PL)
                sync.dma_start(out=b4[:], in_=ln1h_3d[:, e, :]).then_inc(SM["lnc"], 16)
                for ct in range(NCT):
                    g = e * NCT + ct
                    if g >= 2:
                        sync.wait_ge(SM["prj"], C["prj"] + (g - 1) * 3)
                    sync.dma_start(
                        out=ln1r[g % 2].ap().rearrange("p (j t) -> p j t", t=512),
                        in_=ln1h_3d[:, :, ct * 512:(ct + 1) * 512],
                    ).then_inc(SM["lnr0" if g % 2 == 0 else "lnr1"], 16)
                sync.wait_ge(SM["chn"], C["chn"] + e * CH + CH)
                sync.dma_start(out=oh_3d[:, e, :], in_=b1[:]).then_inc(SM["ost"], 16)

        @blk.gpsimd
        def _(gp, C=C, JN=JN):
            gp.wait_ge(SM[JN[0]], JN[1])
            for e in range(DCH):
                for wi, wnm in enumerate(("Wk1", "Wq1", "Wk2")):
                    gi = e * 3 + wi
                    if gi >= 2:
                        gp.wait_ge(SM["wtr"], C["wtr"] + gi - 1)
                    gp.dma_start(out=wk[gi % 2][:], in_=io[wnm][e * 128:(e + 1) * 128, :]
                                 ).then_inc(SM["wldA" if gi % 2 == 0 else "wldB"], 16)
                gp.wait_ge(SM["evS"], C["evS"] + e * 32 + 32)
                gp.wait_ge(SM["lnc"], C["lnc"] + 16 * (e + 1))
                if e >= 1:
                    gp.wait_ge(SM["scn"], C["scn"] + (e - 1) * 3 + 2)
                gp.tensor_tensor(b5[:], b1[:], b4[:], ALU.mult).then_inc(SM["pool"], 1)
                gp.wait_ge(SM["chn"], C["chn"] + e * CH + 5)
                gp.tensor_tensor(b4[:], b5[:], b1[:], ALU.mult).then_inc(SM["pool"], 1)

        @blk.tensor
        def _(te, C=C, JN=JN):
            te.wait_ge(SM[JN[0]], JN[1])
            for e in range(DCH):
                for wi in range(3):
                    gi = e * 3 + wi
                    te.wait_ge(SM["wldA" if gi % 2 == 0 else "wldB"],
                               C["wld"][gi % 2] + 16 * (gi // 2 + 1))
                    if gi >= 2:
                        te.wait_ge(SM["wev"], C["wev"] + gi - 1)
                    elif b == 0 and e == 0:
                        te.wait_ge(SM["ple"], C["ple"])
                    for j in range(DCH):
                        inst = te.transpose(psTb[gi % 2][:, j * 128:(j + 1) * 128],
                                            wk[gi % 2][:, j * 128:(j + 1) * 128], identb[:])
                    inst.then_inc(SM["wtr"], 1)
                for ct in range(NCT):
                    g = e * NCT + ct
                    te.wait_ge(SM["lnr0" if g % 2 == 0 else "lnr1"],
                               C["lnr"][g % 2] + 16 * (g // 2 + 1))
                    for wi in range(3):
                        gpi = e * 48 + ct * 3 + wi
                        te.wait_ge(SM["wev"], C["wev"] + e * 3 + wi + 1)
                        if gpi >= 4:
                            pg = gpi - 4
                            pwi = pg % 3
                            if pwi == 2:
                                te.wait_ge(SM["evV"], C["evV"] + pg // 3 + 1)
                            else:
                                te.wait_ge(SM["evS"],
                                           C["evS"] + 2 * (pg // 3) + pwi + 1)
                        for dj in range(DCH):
                            inst = te.matmul(
                                psA[gpi % 4][:],
                                wkT[wi][:, dj * 128:(dj + 1) * 128],
                                ln1r[g % 2][:, dj * 512:(dj + 1) * 512],
                                start=(dj == 0), stop=(dj == DCH - 1))
                        inst.then_inc(SM["prj"], 1)

        @blk.scalar
        def _(sc, C=C, JN=JN):
            sc.wait_ge(SM[JN[0]], JN[1])
            for e in range(DCH):
                if e >= 1:
                    sc.wait_ge(SM["ost"], C["ost"] + 16 * e)
                for ct in range(NCT):
                    base_p = C["prj"] + e * 48 + ct * 3
                    sc.wait_ge(SM["prj"], base_p + 1)
                    sc.activation(out=b1[:, ct * 512:(ct + 1) * 512],
                                  in_=psA[(ct * 3) % 4][:], func=AF.Exp,
                                  bias=biases[:, e:e + 1]).then_inc(SM["evS"], 1)
                    sc.wait_ge(SM["prj"], base_p + 2)
                    sc.activation(out=b2[:, ct * 512:(ct + 1) * 512],
                                  in_=psA[(ct * 3 + 1) % 4][:], func=AF.Tanh,
                                  scale=0.5, bias=biases[:, 8 + e:9 + e]
                                  ).then_inc(SM["evS"], 1)
                sc.wait_ge(SM["evV"], C["evV"] + e * 16 + 16)
                sc.wait_ge(SM["scn"], C["scn"] + e * 3 + 2)
                sc.wait_ge(SM["pool"], C["pool"] + e * PL + 1)
                sc.activation(out=b5[:, 1:S], in_=b3[:, 0:S - 1], func=AF.Tanh,
                              scale=C2H, bias=biases[:, 16 + e:17 + e])
                sc.drain()
                sc.activation(out=b5[:, 0:1], in_=zcol[:], func=AF.Tanh,
                              scale=C2H, bias=biases[:, 16 + e:17 + e]
                              ).then_inc(SM["scK2"], 1)
                sc.wait_ge(SM["scn"], C["scn"] + e * 3 + 1)
                sc.activation(out=f1[:], in_=f1[:], func=AF.Ln, bias=epsc[:, 0:1])
                sc.drain()
                sc.activation(out=f1[:], in_=f1[:], func=AF.Exp, scale=-1.0
                              ).then_inc(SM["scS"], 1)

        @blk.vector
        def _(ve, C=C, JN=JN):
            ve.wait_ge(SM[JN[0]], JN[1])
            for e in range(DCH):
                for wi in range(3):
                    gi = e * 3 + wi
                    ve.wait_ge(SM["wtr"], C["wtr"] + gi + 1)
                    ve.tensor_copy(wkT[wi][:], psTb[gi % 2][:, 0:DCH * 128]
                                   ).then_inc(SM["wev"], 1)
                ve.drain()
                for ct in range(NCT):
                    base_p = C["prj"] + e * 48 + ct * 3
                    ve.wait_ge(SM["prj"], base_p + 3)
                    if e >= 1 and ct == 0:
                        ve.wait_ge(SM["chn"], C["chn"] + (e - 1) * CH + 4)
                    ve.tensor_copy(b3[:, ct * 512:(ct + 1) * 512],
                                   psA[(ct * 3 + 2) % 4][:]).then_inc(SM["evV"], 1)
                ve.wait_ge(SM["evS"], C["evS"] + e * 32 + 32)
                ve.tensor_scalar(b2[:], b2[:], 0.5, 0.5, ALU.mult, ALU.add
                                 ).then_inc(SM["chn"], 1)
                ve.drain()
                if e >= 1:
                    ve.wait_ge(SM["chn"], C["chn"] + (e - 1) * CH + 3)
                ve.tensor_tensor_scan(f1[:], b1[:], b1[:], 0.0, ALU.add, ALU.bypass
                                      ).then_inc(SM["scn"], 1)
                ve.drain()
                ve.wait_ge(SM["pool"], C["pool"] + e * PL + 1)
                ve.tensor_tensor_scan(f2[:], b5[:], b5[:], 0.0, ALU.add, ALU.bypass
                                      ).then_inc(SM["scn"], 1)
                ve.drain()
                ve.wait_ge(SM["scK2"], C["scK2"] + e + 1)
                ve.tensor_scalar(b5[:], b5[:], 0.5, 0.5, ALU.mult, ALU.add
                                 ).then_inc(SM["chn"], 1)
                ve.drain()
                ve.wait_ge(SM["scS"], C["scS"] + e + 1)
                ve.tensor_tensor(b3[:], f2[:], f1[:], ALU.mult).then_inc(SM["chn"], 1)
                ve.drain()
                ve.tensor_tensor(b6[:], b3[:], b2[:], ALU.mult).then_inc(SM["chn"], 1)
                ve.drain()
                ve.memset(b1[:, 0:1], 0.0)
                ve.drain()
                ve.tensor_tensor(b1[:, 1:S], b4[:, 1:S], b6[:, 0:S - 1], ALU.subtract
                                 ).then_inc(SM["chn"], 1)
                ve.drain()
                ve.wait_ge(SM["pool"], C["pool"] + e * PL + 2)
                ve.tensor_tensor_scan(b5[:], b4[:], b4[:], 0.0, ALU.add, ALU.bypass
                                      ).then_inc(SM["scn"], 1)
                ve.drain()
                ve.memset(b1[:, 0:1], 0.0)
                ve.drain()
                ve.scalar_tensor_tensor(b1[:, 1:S], b5[:, 1:S], C2, b2[:, 0:S - 1],
                                        ALU.mult, ALU.mult)
                ve.drain()
                ve.tensor_tensor(b1[:], b1[:], b6[:], ALU.add).then_inc(SM["chn"], 1)
                ve.drain()

        for s_, n_ in (("lnc", 16 * DCH), ("lnr0", 16 * 64), ("lnr1", 16 * 64),
                       ("wldA", 16 * 12), ("wldB", 16 * 12), ("wtr", 24), ("wev", 24), ("prj", 3 * 128),
                       ("evS", 32 * DCH), ("evV", 16 * DCH), ("scK2", DCH),
                       ("scS", DCH), ("scn", 3 * DCH), ("chn", CH * DCH),
                       ("pool", PL * DCH), ("ost", 16 * DCH)):
            p.bump(s_, n_)

        JN = emit_join(p, blk, waits=[
            ("sync", [("lnc", p.cnt("lnc")), ("lnr0", p.cnt("lnr0")),
                      ("lnr1", p.cnt("lnr1")), ("ost", p.cnt("ost"))]),
            ("gpsimd", [("wldA", p.cnt("wldA")), ("wldB", p.cnt("wldB"))]),
            ("vector", [("chn", p.cnt("chn"))])])
        scopeB.close()

        scopeC = ExitStack()
        WpT = p.sbs(scopeC, f"p1_WpT_{b}", [128, 64 * 128], BF16)
        gB = p.sbs(scopeC, f"p1_gB_{b}", [128, D], F32)
        bB = p.sbs(scopeC, f"p1_bB_{b}", [128, D], F32)
        oct_ = [p.sbs(scopeC, f"p1_oct{i}_{b}", [128, DCH * 512], BF16) for i in range(2)]
        attn = p.sbs(scopeC, f"p1_attn_{b}", [128, DCH * 512], F32)
        atsq = p.sbs(scopeC, f"p1_atsq_{b}", [128, DCH * 512], BF16)
        murow = p.sbs(scopeC, f"p1_murow_{b}", [1, 512], F32)
        varow = p.sbs(scopeC, f"p1_varow_{b}", [1, 3 * 512], F32)
        mucol = p.sbs(scopeC, f"p1_mucol_{b}", [128, 4], F32)
        rstcol = p.sbs(scopeC, f"p1_rstcol_{b}", [128, 4], F32)
        ytm = p.sbs(scopeC, f"p1_ytm_{b}", [128, D], F32)
        x0e = [p.sbs(scopeC, f"p1_x0e{i}_{b}", [128, D], F32) for i in range(2)]
        x1t = [p.sbs(scopeC, f"p1_x1t{i}_{b}", [128, D], F32) for i in range(2)]
        ln2f = p.sbs(scopeC, f"p1_ln2f_{b}", [128, D], F32)
        bnsc2 = p.sbs(scopeC, f"p1_bn2_{b}", [128, 16], F32)
        normf = p.sbs(scopeC, f"p1_normf_{b}", [128, D], F32)
        nfm = p.sbs(scopeC, f"p1_nfm_{b}", [128, D], F32)
        scrow = p.sbs(scopeC, f"p1_scrow_{b}", [128, 8], F32)

        W_ = dict(wp=p.cnt("wp"), wtr0=p.cnt("wtr0"), wev0=p.cnt("wev0"))

        @blk.sync
        def _(sync, JN=JN):
            sync.wait_ge(SM[JN[0]], JN[1])
            sync.dma_start(out=gB[:], in_=io["gB"][:, :]).then_inc(SM["ini2"], 16)
            sync.dma_start(out=bB[:], in_=io["bB"][:, :]).then_inc(SM["ini2"], 16)
        p.bump("ini2", 32)
        INI2 = p.cnt("ini2")

        @blk.gpsimd
        def _(gp, JN=JN, W_=W_):
            gp.wait_ge(SM[JN[0]], JN[1])
            for i in range(8):
                if i >= 1:
                    gp.wait_ge(SM["wtr0"], W_["wtr0"] + i)
                gp.dma_start(out=wtmp[:], in_=io["Wproj"][i * 128:(i + 1) * 128, :]
                             ).then_inc(SM["wp"], 16)

        @blk.tensor
        def _(te, JN=JN, W_=W_):
            te.wait_ge(SM[JN[0]], JN[1])
            for i in range(8):
                te.wait_ge(SM["wp"], W_["wp"] + 16 * (i + 1))
                if i >= 2:
                    te.wait_ge(SM["wev0"], W_["wev0"] + i - 1)
                for j in range(DCH):
                    inst = te.transpose(psTb[i % 2][:, j * 128:(j + 1) * 128],
                                        wtmp[:, j * 128:(j + 1) * 128], identb[:])
                inst.then_inc(SM["wtr0"], 1)

        @blk.vector
        def _(ve, JN=JN, W_=W_):
            ve.wait_ge(SM[JN[0]], JN[1])
            for i in range(8):
                ve.wait_ge(SM["wtr0"], W_["wtr0"] + i + 1)
                ve.tensor_copy(
                    WpT.ap().rearrange("p (j i c) -> p j i c", i=8, c=128)[:, :, i, :],
                    psTb[i % 2].ap().rearrange("p (j c) -> p j c", c=128),
                ).then_inc(SM["wev0"], 1)
            ve.drain()
        p.bump("wp", 8 * 16)
        p.bump("wtr0", 8)
        p.bump("wev0", 8)

        E = dict(old=[p.cnt("old0"), p.cnt("old1")], pm=p.cnt("pm"),
                 evC=p.cnt("evC"), sqC=p.cnt("sqC"), pstA=p.cnt("pstA"),
                 mvA=p.cnt("mvA"), pstB=p.cnt("pstB"), vvC=p.cnt("vvC"),
                 sq2C=p.cnt("sq2C"), mvrC=p.cnt("mvrC"), eT=p.cnt("eT"),
                 eV=p.cnt("eV"), xe=[p.cnt("xe0"), p.cnt("xe1")],
                 x1v=p.cnt("x1v"), x1s=[p.cnt("x1s0"), p.cnt("x1s1")],
                 lnq=p.cnt("lnq"), sq3=p.cnt("sq3"), nrm=p.cnt("nrm"),
                 ntv=p.cnt("ntv"), scm=p.cnt("scm"),
                 sce=p.cnt("sce"), scs=p.cnt("scs"), nst=p.cnt("nst"),
                 ost=p.cnt("ost"), prj=p.cnt("prj"))

        @blk.sync
        def _(sync, b=b, E=E, JN=JN):
            sync.wait_ge(SM[JN[0]], JN[1])
            for ct in range(NCT):
                if ct >= 2:
                    sync.wait_ge(SM["pm"], E["pm"] + (ct - 1) * 8)
                sync.dma_start(
                    out=oct_[ct % 2].ap().rearrange("p (j t) -> p j t", t=512),
                    in_=oh_3d[:, :, ct * 512:(ct + 1) * 512],
                ).then_inc(SM["old0" if ct % 2 == 0 else "old1"], 16)
                if b == 0:
                    for q in range(4):
                        gq = ct * 4 + q
                        sync.wait_ge(SM["x1v"], E["x1v"] + gq + 1)
                        sync.dma_start(
                            out=io["x1out"][ct * 512 + q * 128: ct * 512 + (q + 1) * 128, :],
                            in_=x1t[gq % 2][:],
                        ).then_inc(SM["x1s0" if gq % 2 == 0 else "x1s1"], 16)

        @blk.scalar
        def _(sc, b=b, E=E, xsrc=xsrc, JN=JN):
            sc.wait_ge(SM[JN[0]], JN[1])
            for ct in range(NCT):
                sc.wait_ge(SM["evC"], E["evC"] + ct * 8 + 8)
                sc.activation(out=atsq[:], in_=attn[:], func=AF.Square
                              ).then_inc(SM["sqC"], 1)
                sc.wait_ge(SM["vvC"], E["vvC"] + ct + 1)
                sc.activation(out=varow[0:1, 512:1024], in_=varow[0:1, 0:512],
                              func=AF.Sqrt).then_inc(SM["sq2C"], 1)
                for q in range(4):
                    gq = ct * 4 + q
                    if gq >= 2:
                        sc.wait_ge(SM["x1v"], E["x1v"] + gq - 1)
                    sc.dma_start(out=x0e[gq % 2][:],
                                 in_=xsrc[ct * 512 + q * 128: ct * 512 + (q + 1) * 128, :]
                                 ).then_inc(SM["xe0" if gq % 2 == 0 else "xe1"], 16)
                    sc.wait_ge(SM["lnq"], E["lnq"] + gq + 1)
                    sc.activation(out=bnsc2[:, 15:16], in_=bnsc2[:, 14:15],
                                  func=AF.Sqrt).then_inc(SM["sq3"], 1)

        @blk.gpsimd
        def _(gp, b=b, E=E, JN=JN):
            gp.wait_ge(SM[JN[0]], JN[1])
            for ct in range(NCT):
                for q in range(4):
                    gq = ct * 4 + q
                    if b == 0:
                        gp.wait_ge(SM["nrm"], E["nrm"] + gq + 1)
                        gp.dma_start(
                            out=io["normout"][ct * 512 + q * 128: ct * 512 + (q + 1) * 128, :],
                            in_=normf[:]).then_inc(SM["nst"], 16)
                    gp.wait_ge(SM["sce"], E["sce"] + gq + 1)
                    gp.dma_start(
                        out=io["scout"][b * 64 + gq: b * 64 + gq + 1, :]
                        .rearrange("r (t e) -> (r t) e", e=8),
                        in_=scrow[:],
                    ).then_inc(SM["scs"], 16)

        @blk.tensor
        def _(te, E=E, JN=JN):
            te.wait_ge(SM[JN[0]], JN[1])
            te.wait_ge(SM["wev0"], p.cnt("wev0"))
            for ct in range(NCT):
                te.wait_ge(SM["old0" if ct % 2 == 0 else "old1"],
                           E["old"][ct % 2] + 16 * (ct // 2 + 1))
                for ech in range(8):
                    gm = ct * 8 + ech
                    if gm >= 2:
                        te.wait_ge(SM["evC"], E["evC"] + gm - 1)
                    for dj in range(DCH):
                        inst = te.matmul(
                            psA[gm % 2][:],
                            WpT[:, (dj * 8 + ech) * 128:(dj * 8 + ech + 1) * 128],
                            oct_[ct % 2][:, dj * 512:(dj + 1) * 512],
                            start=(dj == 0), stop=(dj == DCH - 1))
                    inst.then_inc(SM["pm"], 1)
                te.wait_ge(SM["evC"], E["evC"] + ct * 8 + 8)
                if ct >= 1:
                    te.wait_ge(SM["sce"], E["sce"] + ct * 4)
                for dj in range(DCH):
                    inst = te.matmul(psR[0:1, :], onescF[:, 0:1],
                                     attn[:, dj * 512:(dj + 1) * 512],
                                     start=(dj == 0), stop=(dj == DCH - 1))
                inst.then_inc(SM["pstA"], 1)
                te.wait_ge(SM["mvA"], E["mvA"] + ct + 1)
                te.wait_ge(SM["sqC"], E["sqC"] + ct + 1)
                for dj in range(DCH):
                    inst = te.matmul(psR[0:1, :], onescB[:, 0:1],
                                     atsq[:, dj * 512:(dj + 1) * 512],
                                     start=(dj == 0), stop=(dj == DCH - 1))
                inst.then_inc(SM["pstB"], 1)
                te.wait_ge(SM["mvrC"], E["mvrC"] + ct + 1)
                for k in range(8):
                    j = ct * 72 + k
                    bnk = psA[2 + (j % 2)]
                    if j >= 2:
                        te.wait_ge(SM["eV"], E["eV"] + j - 1)
                    q = k // 2
                    if k % 2 == 0:
                        src = murow[0:1, q * 128:(q + 1) * 128]
                    else:
                        src = varow[0:1, 1024 + q * 128: 1024 + (q + 1) * 128]
                    te.transpose(bnk[:, 0:1], src, identf[0:1, 0:1]
                                 ).then_inc(SM["eT"], 1)
                for q in range(4):
                    gq = ct * 4 + q
                    for i in range(8):
                        j = ct * 72 + 8 + 16 * q + i
                        bnk = psA[2 + (j % 2)]
                        te.wait_ge(SM["eV"], E["eV"] + j - 1)
                        te.transpose(bnk[:, 0:128],
                                     attn[:, i * 512 + q * 128: i * 512 + (q + 1) * 128],
                                     identf[:]).then_inc(SM["eT"], 1)
                    te.wait_ge(SM["nrm"], E["nrm"] + gq + 1)
                    for dj in range(DCH):
                        j = ct * 72 + 16 + 16 * q + dj
                        bnk = psA[2 + (j % 2)]
                        te.wait_ge(SM["eV"], E["eV"] + j - 1)
                        te.transpose(bnk[:, 0:128],
                                     normf[:, dj * 128:(dj + 1) * 128], identf[:]
                                     ).then_inc(SM["eT"], 1)
                    te.wait_ge(SM["eV"], E["eV"] + ct * 72 + 16 * q + 24)
                    if gq >= 1:
                        te.wait_ge(SM["sce"], E["sce"] + gq)
                    for dj in range(DCH):
                        inst = te.matmul(psR[0:128, 0:8], nfm[:, dj * 128:(dj + 1) * 128],
                                         centT[:, dj * 8:(dj + 1) * 8],
                                         start=(dj == 0), stop=(dj == DCH - 1))
                    inst.then_inc(SM["scm"], 1)

        @blk.vector
        def _(ve, b=b, E=E, JN=JN, INI2=INI2):
            ve.wait_ge(SM[JN[0]], JN[1])
            ve.wait_ge(SM["ini2"], INI2)
            for ct in range(NCT):
                for ech in range(8):
                    gm = ct * 8 + ech
                    ve.wait_ge(SM["pm"], E["pm"] + gm + 1)
                    if ct >= 1 and ech == 0:
                        ve.wait_ge(SM["eT"], E["eT"] + (ct - 1) * 72 + 64)
                        ve.wait_ge(SM["pstB"], E["pstB"] + ct)
                        ve.wait_ge(SM["sqC"], E["sqC"] + ct)
                    ve.tensor_scalar(attn[:, ech * 512:(ech + 1) * 512],
                                     psA[gm % 2][:], biases[:, 24 + ech:25 + ech],
                                     None, ALU.add).then_inc(SM["evC"], 1)
                ve.drain()
                ve.wait_ge(SM["pstA"], E["pstA"] + ct + 1)
                ve.tensor_scalar(murow[0:1, :], psR[0:1, :], 1.0 / D, None, ALU.mult
                                 ).then_inc(SM["mvA"], 1)
                ve.drain()
                ve.wait_ge(SM["pstB"], E["pstB"] + ct + 1)
                ve.tensor_scalar(varow[0:1, 1024:1536], psR[0:1, :], 1.0 / D, None,
                                 ALU.mult)
                ve.drain()
                ve.tensor_tensor(varow[0:1, 0:512], murow[0:1, :], murow[0:1, :],
                                 ALU.mult)
                ve.drain()
                ve.tensor_tensor(varow[0:1, 0:512], varow[0:1, 1024:1536],
                                 varow[0:1, 0:512], ALU.subtract)
                ve.drain()
                ve.tensor_scalar(varow[0:1, 0:512], varow[0:1, 0:512], float(LN_EPS),
                                 None, ALU.add).then_inc(SM["vvC"], 1)
                ve.wait_ge(SM["sq2C"], E["sq2C"] + ct + 1)
                ve.reciprocal(varow[0:1, 1024:1536], varow[0:1, 512:1024]
                              ).then_inc(SM["mvrC"], 1)
                ve.drain()
                for k in range(8):
                    j = ct * 72 + k
                    ve.wait_ge(SM["eT"], E["eT"] + j + 1)
                    bnk = psA[2 + (j % 2)]
                    q = k // 2
                    dst = mucol if k % 2 == 0 else rstcol
                    ve.tensor_copy(dst[:, q:q + 1], bnk[:, 0:1]).then_inc(SM["eV"], 1)
                ve.drain()
                for q in range(4):
                    gq = ct * 4 + q
                    for i in range(8):
                        j = ct * 72 + 8 + 16 * q + i
                        ve.wait_ge(SM["eT"], E["eT"] + j + 1)
                        bnk = psA[2 + (j % 2)]
                        ve.tensor_scalar(ytm[:, i * 128:(i + 1) * 128], bnk[:, 0:128],
                                         mucol[:, q:q + 1], rstcol[:, q:q + 1],
                                         ALU.subtract, ALU.mult).then_inc(SM["eV"], 1)
                    ve.drain()
                    ve.wait_ge(SM["xe0" if gq % 2 == 0 else "xe1"],
                               E["xe"][gq % 2] + 16 * (gq // 2 + 1))
                    if b == 0 and gq >= 2:
                        ve.wait_ge(SM["x1s0" if gq % 2 == 0 else "x1s1"],
                                   E["x1s"][gq % 2] + 16 * (gq // 2))
                    ve.tensor_tensor(x1t[gq % 2][:], ytm[:], x0e[gq % 2][:],
                                     ALU.add).then_inc(SM["x1v"], 1)
                    ve.drain()
                    ve.bn_stats(bnsc2[:, 0:6], x1t[gq % 2][:, 0:512])
                    ve.bn_stats(bnsc2[:, 6:12], x1t[gq % 2][:, 512:1024])
                    ve.drain()
                    ve.bn_aggr(bnsc2[:, 12:14], bnsc2[:, 0:12])
                    ve.drain()
                    ve.tensor_scalar(bnsc2[:, 14:15], bnsc2[:, 13:14],
                                     float(LN_EPS), None, ALU.add
                                     ).then_inc(SM["lnq"], 1)
                    ve.wait_ge(SM["sq3"], E["sq3"] + gq + 1)
                    ve.reciprocal(bnsc2[:, 14:15], bnsc2[:, 15:16])
                    ve.drain()
                    ve.tensor_scalar(ln2f[:], x1t[gq % 2][:], bnsc2[:, 12:13],
                                     bnsc2[:, 14:15], ALU.subtract, ALU.mult)
                    ve.drain()
                    if b == 0 and gq >= 1:
                        ve.wait_ge(SM["nst"], E["nst"] + 16 * gq)
                    if q >= 1:
                        ve.wait_ge(SM["eT"], E["eT"] + ct * 72 + 16 * q + 8)
                    elif ct >= 1:
                        ve.wait_ge(SM["eT"], E["eT"] + ct * 72)
                    ve.tensor_tensor(normf[:], ln2f[:], gB[:], ALU.mult)
                    ve.drain()
                    ve.tensor_tensor(normf[:], normf[:], bB[:], ALU.add
                                     ).then_inc(SM["nrm"], 1)
                    ve.drain()
                    for dj in range(DCH):
                        j = ct * 72 + 16 + 16 * q + dj
                        ve.wait_ge(SM["eT"], E["eT"] + j + 1)
                        bnk = psA[2 + (j % 2)]
                        ve.tensor_copy(nfm[:, dj * 128:(dj + 1) * 128],
                                       bnk[:, 0:128]).then_inc(SM["eV"], 1)
                    ve.drain()
                    ve.wait_ge(SM["scm"], E["scm"] + gq + 1)
                    if gq >= 1:
                        ve.wait_ge(SM["scs"], E["scs"] + 16 * gq)
                    ve.tensor_copy(scrow[:], psR[0:128, 0:8]).then_inc(SM["sce"], 1)
                    ve.drain()

        for s_, n_ in (("old0", 16 * 8), ("old1", 16 * 8), ("pm", 8 * NCT),
                       ("evC", 8 * NCT), ("sqC", NCT), ("pstA", NCT), ("mvA", NCT),
                       ("pstB", NCT), ("vvC", NCT), ("sq2C", NCT), ("mvrC", NCT),
                       ("eT", 72 * NCT), ("eV", 72 * NCT),
                       ("xe0", 16 * 32), ("xe1", 16 * 32), ("x1v", 64),
                       ("lnq", 64), ("sq3", 64), ("nrm", 64),
                       ("scm", 64), ("sce", 64), ("scs", 16 * 64)):
            p.bump(s_, n_)
        if b == 0:
            p.bump("x1s0", 16 * 32)
            p.bump("x1s1", 16 * 32)
            p.bump("nst", 16 * 64)

        JN = emit_join(p, blk, waits=[
            ("sync", [("old0", p.cnt("old0")), ("old1", p.cnt("old1")),
                      ("x1s0", p.cnt("x1s0")), ("x1s1", p.cnt("x1s1"))]),
            ("scalar", [("xe0", p.cnt("xe0")), ("xe1", p.cnt("xe1"))]),
            ("gpsimd", [("nst", p.cnt("nst")), ("scs", p.cnt("scs"))])])
        scopeC.close()


"""Phase 2: L0_ffn routing (replicated) + my-4-experts FFN + sandwich epilogue.

Routing: greedy balanced assignment over 32768 scores (pair-local batch order,
my batch = big-rows 0..63). Bisection on threshold per expert (22 iters,
DVE + gpsimd partition_all_reduce), exact-capacity band fixup, pos[t] slots.
FFN: capacity 1280/(expert,batch), tiles (512,512,256); w1my/w2my resident bf16.

io: scoresF [512,1024]f32, normH [S,D]bf16, x1own [OWN,D]f32,
w1my [4*D,4096]f32, w2my [4*4096,D]f32, b1c [128,128]f32, b2c [128,32]f32,
gidx [128,40]i32 (=4*slot*1280 + arange(5120) host), ownrows [128,32]i32,
tokid [128,64]i32, identb/identf, S_h [10240,1]i32, posH [S,1]i32,
yH [10240,D]bf16, x2out [OWN,D]f32.
"""

CAP = 2560
CAPC = CAP // 128
TT_E = (512,) * (CAP // 512)
CGLOB = 4096.0
BISECT = 22
BIGF = 10240.0
GELC = 0.044715
GELS = 0.7978845608028654


def emit_p2(p, io, JN=None, nexp=4, dbg=None):
    nc = p.nc
    blk = p.block
    if dbg is not None:
        p.sem("dbgv")
        p.sem("dbgs")

    identb = p.sb("p2_identb", [128, 128], BF16)
    identf = p.sb("p2_identf", [128, 128], F32)
    onescF = p.sb("p2_onescF", [128, 1], F32)
    ones1f = p.sb("p2_ones1f", [1, 128], F32)
    onescB = p.sb("p2_onescB", [128, 1], BF16)

    psA = [p.ps(f"p2_psA{i}", [128, 512]) for i in range(4)]
    psR = p.ps("p2_psR", [128, 512])
    psT = p.ps("p2_psT", [128, 1024], BF16)
    psTf = p.ps("p2_psTf", [128, 512])

    SM = p._sems

    scR = ExitStack()
    score = p.sbs(scR, "p2_score", [128, 2048], F32)
    avail = p.sbs(scR, "p2_avail", [128, 256], F32)
    gwork = p.sbs(scR, "p2_gwork", [128, 256], F32)
    band = p.sbs(scR, "p2_band", [128, 256], F32)
    base_ = p.sbs(scR, "p2_base", [128, 256], F32)
    pscanF = p.sbs(scR, "p2_pscanF", [128, 256], F32)
    selmy = p.sbs(scR, "p2_selmy", [128, 64], F32)
    pscanM = p.sbs(scR, "p2_pscanM", [128, 64], F32)
    posf = p.sbs(scR, "p2_posf", [128, 64], F32)
    posi = p.sbs(scR, "p2_posi", [128, 64], I32)
    toki = p.sbs(scR, "p2_toki", [128, 64], I32)
    sbig = p.sbs(scR, "p2_sbig", [128, (8 * CAP + 128) // 128], I32)
    cntc = p.sbs(scR, "p2_cntc", [128, 4], F32)
    cntR = p.sbs(scR, "p2_cntR", [128, 4], F32)
    masku = p.sbs(scR, "p2_masku", [128, 64], U8)
    sca = p.sbs(scR, "p2_sca", [128, 16], F32)
    predu = p.sbs(scR, "p2_predu", [128, 2], U8)
    rowsc = p.sbs(scR, "p2_rowsc", [1, 256], F32)
    if dbg is not None:
        dbg["bandpre"] = p.sbs(scR, "p2_bandpre", [128, 256], F32)
        dbg["gworkc"] = p.sbs(scR, "p2_gworkc", [128, 256], F32)

    if JN is not None:
        pass

    @blk.sync
    def _(sync):
        if JN is not None:
            sync.wait_ge(SM[JN[0]], JN[1])
        sync.dma_start(out=identb[:], in_=io["identb"][:, :]).then_inc(SM["ini"], 16)
        sync.dma_start(out=identf[:], in_=io["identf"][:, :]).then_inc(SM["ini"], 16)
        sync.dma_start(out=score.ap().rearrange("p (r e) -> p r e", e=8),
                       in_=io["scoresF"].rearrange("r (p e) -> p r e", e=8)
                       ).then_inc(SM["ini"], 16)
        sync.dma_start(out=toki[:], in_=io["tokid"][:, :]).then_inc(SM["ini"], 16)
    p.bump("ini", 4 * 16)

    @blk.gpsimd
    def _(gp):
        if JN is not None:
            gp.wait_ge(SM[JN[0]], JN[1])
        gp.memset(avail[:], 1.0)
        gp.memset(posf[:], BIGF)
        gp.memset(onescF[:], 1.0)
        gp.memset(onescB[:], 1.0)
        gp.memset(sbig[:], 8192)
        gp.memset(ones1f[:], 1.0)
        gp.drain()
        gp.engine_nop().then_inc(SM["ini"], 1)
        gp.dma_start(out=io["S_h"].rearrange("(a b) one -> a (b one)", a=128),
                     in_=sbig[:]).then_inc(SM["sbig"], 16)
        gp.wait_ge(SM["pcast"], 1)
        gp.wait_ge(SM["sbig"], 16)
        for c in range(64):
            if c >= 1:
                gp.wait_ge(SM["spos"], 16 * c)
            gp.indirect_dma_start(
                out=io["S_h"][:, :],
                out_offset=IndirectOffsetOnAxis(ap=posi[:, c:c + 1], axis=0),
                in_=toki[:, c:c + 1], in_offset=None, oob_is_err=False,
            ).then_inc(SM["spos"], 16)
    p.bump("ini", 1)
    p.bump("sbig", 16)
    p.bump("spos", 16 * 64)
    INI = p.cnt("ini")

    @blk.tensor
    def _(te):
        te.wait_ge(SM["ini"], INI)
        for e in range(8):
            cbase = e * (BISECT + 2)
            for it in range(BISECT + 2):
                src = cntc[:, 0:1] if it < BISECT else (
                    cntc[:, 1:2] if it == BISECT else cntc[:, 2:3])
                te.wait_ge(SM["cntv"], cbase + it + 1)
                te.matmul(psR[0:1, 0:1], src, onescF[:, 0:1],
                          start=True, stop=True).then_inc(SM["rpe"], 1)
                te.wait_ge(SM["cv2"], cbase + it + 1)
                te.matmul(psR[0:128, 4:5], ones1f[0:1, :], sca[0:1, 3:4],
                          start=True, stop=True).then_inc(SM["rpe"], 1)
            for d in range(2):
                rb = e * 4 + d * 2
                te.wait_ge(SM["rpv"], rb + 1)
                te.transpose(psTf[0:1, 0:128], sca[:, 9:10], identf[:]
                             ).then_inc(SM["rpt"], 1)
                te.wait_ge(SM["rpv"], rb + 2)
                te.transpose(psTf[0:128, 128:129], rowsc[0:1, 0:128],
                             identf[0:1, 0:1]).then_inc(SM["rpt"], 1)
    p.bump("rpt", 32)
    p.bump("rpe", 8 * 2 * (BISECT + 2))

    @blk.vector
    def _(ve):
        ve.wait_ge(SM["ini"], INI)
        if dbg is not None:
            ve.memset(sca[:], 0.0)
            ve.drain()
        for e in range(8):
            ve.memset(sca[:, 0:1], -64.0)
            ve.memset(sca[:, 1:2], 64.0)
            ve.memset(sca[:, 2:3], 0.0)
            ve.drain()
            sc_e = score.ap().rearrange("p (r e) -> p r e", e=8)[:, :, e]
            pbase = e * (BISECT + 2)
            rbase = e * 2 * (BISECT + 2)
            for it in range(BISECT):
                ve.scalar_tensor_tensor(gwork[:], sc_e, sca[:, 2:3], avail[:],
                                        ALU.is_ge, ALU.mult,
                                        accum_out=cntc[:, 0:1]).then_inc(SM["cntv"], 1)
                ve.wait_ge(SM["rpe"], rbase + 2 * it + 1)
                ve.tensor_copy(sca[0:1, 3:4], psR[0:1, 0:1])
                ve.drain()
                ve.nop().then_inc(SM["cv2"], 1)
                ve.wait_ge(SM["rpe"], rbase + 2 * it + 2)
                ve.tensor_copy(sca[:, 10:11], psR[:, 4:5])
                ve.drain()
                ve.tensor_scalar(predu[:, 0:1], sca[:, 10:11], CGLOB, None, ALU.is_ge)
                ve.drain()
                ve.select(sca[:, 13:14], predu[:, 0:1], sca[:, 2:3], sca[:, 0:1], add_drain=True)
                ve.select(sca[:, 14:15], predu[:, 0:1], sca[:, 1:2], sca[:, 2:3], add_drain=True)
                ve.drain()
                ve.tensor_copy(sca[:, 0:1], sca[:, 13:14])
                ve.tensor_copy(sca[:, 1:2], sca[:, 14:15])
                ve.drain()
                ve.tensor_tensor(sca[:, 2:3], sca[:, 0:1], sca[:, 1:2], ALU.add)
                ve.drain()
                ve.tensor_scalar(sca[:, 2:3], sca[:, 2:3], 0.5, None, ALU.mult)
                ve.drain()
            ve.scalar_tensor_tensor(base_[:], sc_e, sca[:, 0:1], avail[:],
                                    ALU.is_ge, ALU.mult,
                                    accum_out=cntc[:, 1:2]).then_inc(SM["cntv"], 1)
            ve.wait_ge(SM["rpe"], rbase + 2 * BISECT + 1)
            ve.tensor_copy(sca[0:1, 3:4], psR[0:1, 0:1])
            ve.drain()
            ve.nop().then_inc(SM["cv2"], 1)
            ve.wait_ge(SM["rpe"], rbase + 2 * BISECT + 2)
            ve.tensor_copy(sca[:, 11:12], psR[:, 4:5])
            ve.drain()
            ve.scalar_tensor_tensor(band[:], sc_e, sca[:, 1:2], base_[:],
                                    ALU.is_lt, ALU.mult,
                                    accum_out=cntc[:, 2:3]).then_inc(SM["cntv"], 1)
            ve.wait_ge(SM["rpe"], rbase + 2 * BISECT + 3)
            ve.tensor_copy(sca[0:1, 3:4], psR[0:1, 0:1])
            ve.drain()
            ve.nop().then_inc(SM["cv2"], 1)
            ve.wait_ge(SM["rpe"], rbase + 2 * BISECT + 4)
            ve.tensor_copy(sca[:, 12:13], psR[:, 4:5])
            ve.drain()
            ve.tensor_tensor(base_[:], base_[:], band[:], ALU.subtract)
            ve.drain()
            ve.tensor_scalar(sca[:, 5:6], sca[:, 11:12], -1.0, CGLOB, ALU.mult, ALU.add)
            ve.drain()
            ve.tensor_tensor(sca[:, 5:6], sca[:, 5:6], sca[:, 12:13], ALU.add)
            ve.drain()
            ve.tensor_tensor_scan(pscanF[:], band[:], band[:], 0.0, ALU.add, ALU.bypass)
            ve.drain()
            ve.tensor_copy(sca[:, 9:10], pscanF[:, 255:256])
            ve.drain()
            ve.nop().then_inc(SM["rpv"], 1)
            ve.wait_ge(SM["rpt"], e * 4 + 1)
            ve.tensor_copy(rowsc[0:1, 0:128], psTf[0:1, 0:128])
            ve.drain()
            ve.tensor_tensor_scan(rowsc[0:1, 128:256], rowsc[0:1, 0:128],
                                  rowsc[0:1, 0:128], 0.0, ALU.add, ALU.bypass)
            ve.drain()
            ve.tensor_tensor(rowsc[0:1, 0:128], rowsc[0:1, 128:256],
                             rowsc[0:1, 0:128], ALU.subtract)
            ve.drain()
            ve.nop().then_inc(SM["rpv"], 1)
            ve.wait_ge(SM["rpt"], e * 4 + 2)
            ve.tensor_copy(sca[:, 6:7], psTf[:, 128:129])
            ve.drain()
            ve.tensor_scalar(gwork[:], pscanF[:], sca[:, 6:7], None, ALU.add)
            ve.drain()
            ve.tensor_scalar(gwork[:], gwork[:], sca[:, 5:6], None, ALU.is_le)
            ve.drain()
            if dbg is not None:
                ve.tensor_copy(dbg["bandpre"][:], band[:])
                ve.drain()
                ve.tensor_copy(dbg["gworkc"][:], gwork[:])
                ve.drain()
            ve.tensor_tensor(band[:], band[:], gwork[:], ALU.mult)
            ve.drain()
            ve.tensor_tensor(base_[:], base_[:], band[:], ALU.add)
            ve.drain()
            ve.tensor_tensor(avail[:], avail[:], base_[:], ALU.subtract)
            ve.drain()
            ve.tensor_copy(selmy[:], base_[:, 0:64])
            ve.drain()
            ve.tensor_tensor_scan(pscanM[:], selmy[:], selmy[:], 0.0, ALU.add,
                                  ALU.bypass)
            ve.drain()
            ve.tensor_copy(sca[:, 9:10], pscanM[:, 63:64])
            ve.drain()
            ve.nop().then_inc(SM["rpv"], 1)
            ve.wait_ge(SM["rpt"], e * 4 + 3)
            ve.tensor_copy(rowsc[0:1, 0:128], psTf[0:1, 0:128])
            ve.drain()
            ve.tensor_tensor_scan(rowsc[0:1, 128:256], rowsc[0:1, 0:128],
                                  rowsc[0:1, 0:128], 0.0, ALU.add, ALU.bypass)
            ve.drain()
            ve.tensor_tensor(rowsc[0:1, 0:128], rowsc[0:1, 128:256],
                             rowsc[0:1, 0:128], ALU.subtract)
            ve.drain()
            ve.nop().then_inc(SM["rpv"], 1)
            ve.wait_ge(SM["rpt"], e * 4 + 4)
            ve.tensor_copy(sca[:, 7:8], psTf[:, 128:129])
            ve.drain()
            ve.tensor_scalar(pscanM[:], pscanM[:], sca[:, 7:8],
                             -1.0, ALU.add, ALU.add)
            ve.drain()
            ve.tensor_scalar(gwork[:, 0:64], pscanM[:], float(CAP), None,
                             ALU.is_ge)
            ve.drain()
            ve.tensor_scalar(gwork[:, 0:64], gwork[:, 0:64], float(16 * CAP),
                             None, ALU.mult)
            ve.drain()
            ve.tensor_tensor(pscanM[:], pscanM[:], gwork[:, 0:64], ALU.add)
            ve.drain()
            ve.tensor_scalar(pscanM[:], pscanM[:], float(e * CAP), None,
                             ALU.add)
            ve.drain()
            ve.tensor_scalar(pscanM[:], pscanM[:], float(8 * CAP), None,
                             ALU.min)
            ve.drain()
            ve.tensor_scalar(masku[:], selmy[:], 0.5, None, ALU.is_ge)
            ve.drain()
            ve.select(posf[:], masku[:], pscanM[:], posf[:], add_drain=True)
            ve.drain()
            if dbg is not None:
                ve.nop().then_inc(SM["dbgv"], 1)
                ve.wait_ge(SM["dbgs"], (e + 1) * 9 * 16)
        ve.tensor_copy(posi[:], posf[:]).then_inc(SM["pcast"], 1)
        ve.drain()
    p.bump("cntv", 8 * (BISECT + 2))
    p.bump("cv2", 8 * (BISECT + 2))
    p.bump("rpv", 32)
    p.bump("pcast", 1)

    if dbg is not None:
        @blk.sync
        def _(sync):
            for e in range(8):
                sync.wait_ge(SM["dbgv"], e + 1)
                sync.dma_start(out=dbg["dbgA"][e * 128:(e + 1) * 128, :],
                               in_=avail[:]).then_inc(SM["dbgs"], 16)
                sync.dma_start(out=dbg["dbgS"][e * 128:(e + 1) * 128, :],
                               in_=base_[:]).then_inc(SM["dbgs"], 16)
                sync.dma_start(out=dbg["dbgC"][e * 128:(e + 1) * 128, :],
                               in_=sca[:]).then_inc(SM["dbgs"], 16)
                sync.dma_start(out=dbg["dbgPF"][e * 128:(e + 1) * 128, :],
                               in_=pscanF[:]).then_inc(SM["dbgs"], 16)
                sync.dma_start(out=dbg["dbgPM"][e * 128:(e + 1) * 128, :],
                               in_=pscanM[:]).then_inc(SM["dbgs"], 16)
                sync.dma_start(out=dbg["dbgPO"][e * 128:(e + 1) * 128, :],
                               in_=posf[:]).then_inc(SM["dbgs"], 16)
                sync.dma_start(out=dbg["dbgBP"][e * 128:(e + 1) * 128, :],
                               in_=dbg["bandpre"][:]).then_inc(SM["dbgs"], 16)
                sync.dma_start(out=dbg["dbgGW"][e * 128:(e + 1) * 128, :],
                               in_=dbg["gworkc"][:]).then_inc(SM["dbgs"], 16)
                sync.dma_start(out=dbg["dbgBX"][e * 128:(e + 1) * 128, :],
                               in_=band[:]).then_inc(SM["dbgs"], 16)
        p.bump("dbgv", 8)
        p.bump("dbgs", 8 * 9 * 16)

    JN2 = emit_join(p, blk, waits=[
        ("sync", [("ini", INI)]),
        ("gpsimd", [("sbig", 16), ("spos", p.cnt("spos"))])])
    scR.close()

    scF = ExitStack()
    tmy = p.sbs(scF, "p2_tmy", [128, CAPC * nexp], I32)
    gidxs = p.sbs(scF, "p2_gidxs", [128, CAPC * nexp], I32)
    w1j = p.sbs(scF, "p2_w1j", [128, DCH * 4096], BF16)
    w2j = p.sbs(scF, "p2_w2j", [128, 32 * D], BF16)
    b1s = p.sbs(scF, "p2_b1s", [128, 32 * nexp], F32)
    b2s = p.sbs(scF, "p2_b2s", [128, 8 * nexp], F32)
    normg = [p.sbs(scF, f"p2_normg{i}", [128, 4 * D], BF16) for i in range(2)]
    normfm = p.sbs(scF, "p2_normfm", [128, DCH * 512], BF16)
    hpre = [p.sbs(scF, f"p2_hpre{i}", [128, 512], F32) for i in range(2)]
    hsq = [p.sbs(scF, f"p2_hsq{i}", [128, 512], F32) for i in range(2)]
    hth = [p.sbs(scF, f"p2_hth{i}", [128, 512], BF16) for i in range(2)]
    hbuf = p.sbs(scF, "p2_h", [128, 32 * 512], BF16)
    ybuf = p.sbs(scF, "p2_y", [128, 8 * 512], BF16)
    ytm = p.sbs(scF, "p2_ytm", [128, D], BF16)

    TTOT = []
    for j in range(nexp):
        off = 0
        for t, tw in enumerate(TT_E):
            TTOT.append((j, t, tw, j * CAPC + off))
            off += tw // 128

    @blk.sync
    def _(sync, JN2=JN2):
        sync.wait_ge(SM[JN2[0]], JN2[1])
        sync.dma_start(out=b1s[:], in_=io["b1c"][:, :]).then_inc(SM["bld"], 16)
        sync.dma_start(out=b2s[:], in_=io["b2c"][:, :]).then_inc(SM["bld"], 16)
        sync.dma_start(out=gidxs[:], in_=io["gidx"][:, :]).then_inc(SM["gld"], 16)
    p.bump("bld", 32)
    p.bump("gld", 16)

    @blk.gpsimd
    def _(gp, JN2=JN2):
        gp.wait_ge(SM[JN2[0]], JN2[1])
        gp.memset(normg[0][:], 0.0)
        gp.memset(normg[1][:], 0.0)
        gp.drain()
        gp.wait_ge(SM["gld"], 16)
        for c in range(CAPC * nexp):
            gp.indirect_dma_start(
                out=tmy[:, c:c + 1], out_offset=None, in_=io["S_h"][:, :],
                in_offset=IndirectOffsetOnAxis(ap=gidxs[:, c:c + 1], axis=0),
            ).then_inc(SM["tld"], 16)
        for ti, (j, t, tw, toff) in enumerate(TTOT):
            if t == 0:
                if j >= 1:
                    gp.wait_ge(SM["pm1"], j * 96)
                    gp.wait_ge(SM["pm2"], j * 6)
                gp.dma_start(
                    out=w1j.ap().rearrange("p (dj f) -> p dj f", f=4096),
                    in_=io["w1my"][j * D:(j + 1) * D, :].rearrange("(dj p) f -> p dj f", p=128),
                ).then_inc(SM["w1l"], 16)
                gp.dma_start(
                    out=w2j.ap().rearrange("p (fc d) -> p fc d", d=D),
                    in_=io["w2my"][j * 4096:(j + 1) * 4096, :].rearrange("(fc p) d -> p fc d", p=128),
                ).then_inc(SM["w2l"], 16)
            nq = tw // 128
            gidx_t = CAPC * j + 4 * t
            gbase = sum(x[2] // 128 for x in TTOT[:ti])
            if ti == 0:
                gp.wait_ge(SM["tld"], 16 * CAPC * nexp)
            for q in range(nq):
                gi = gbase + q
                if gi >= 1:
                    gp.wait_ge(SM["gno"], 16 * gi)
                if gi >= 8:
                    gp.wait_ge(SM["ntr"], gi - 8 + 1)
                gp.indirect_dma_start(
                    out=normg[(gi // 4) % 2][:, (gi % 4) * D:(gi % 4 + 1) * D],
                    out_offset=None, in_=io["normH"][:, :],
                    in_offset=IndirectOffsetOnAxis(ap=tmy[:, toff + q:toff + q + 1], axis=0),
                ).then_inc(SM["gno"], 16)
            for q in range(nq):
                yq = gbase + q
                gp.wait_ge(SM["yte"], yq + 1)
                if yq >= 1:
                    gp.wait_ge(SM["yst0"], 16 * yq)
                gp.indirect_dma_start(
                    out=io["yH"][:, :],
                    out_offset=IndirectOffsetOnAxis(ap=tmy[:, toff + q:toff + q + 1], axis=0),
                    in_=ytm[:], in_offset=None,
                ).then_inc(SM["yst0"], 16)
    p.bump("tld", 16 * CAPC * nexp)
    ngtot = sum(tw // 128 for (_, _, tw, _) in TTOT)
    p.bump("gno", 16 * CAPC * nexp)
    p.bump("yst0", 16 * ngtot)

    @blk.tensor
    def _(te, JN2=JN2):
        te.wait_ge(SM[JN2[0]], JN2[1])
        ti = 0
        for (j, t, tw, toff) in TTOT:
            nq = tw // 128
            gbase = sum(x[2] // 128 for x in TTOT[:ti])
            te.wait_ge(SM["yte"], gbase)
            for q in range(nq):
                gi = gbase + q
                te.wait_ge(SM["gno"], 16 * (gi + 1))
                if gi >= 1:
                    te.wait_ge(SM["nte"], gi)
                for dj in range(DCH):
                    inst = te.transpose(
                        psT[:, dj * 128:(dj + 1) * 128],
                        normg[(gi // 4) % 2][:, (gi % 4) * D + dj * 128:(gi % 4) * D + (dj + 1) * 128],
                        identb[:])
                inst.then_inc(SM["ntr"], 1)
            te.wait_ge(SM["w1l"], 16 * (j + 1))
            te.wait_ge(SM["nte"], gbase + nq)
            for gb in range(16):
                gm = ti * 32 + gb * 2
                for fi in range(2):
                    fc = gb * 2 + fi
                    ps_ = psA[(gb % 2) * 2 + fi]
                    if gm + fi >= 4:
                        te.wait_ge(SM["ev1"], (ti * 32 + gb * 2 + fi) - 3)
                    for dj in range(DCH):
                        inst = te.matmul(
                            ps_[:, 0:tw],
                            w1j[:, dj * 4096 + fc * 128: dj * 4096 + (fc + 1) * 128],
                            normfm[:, dj * 512: dj * 512 + tw],
                            start=(dj == 0), stop=(dj == DCH - 1))
                    inst.then_inc(SM["pm1"], 1)
            te.wait_ge(SM["w2l"], 16 * (j + 1))
            te.wait_ge(SM["gel"], ti * 32 + 32)
            for half in range(2):
                hm = ti * 2 + half
                if hm >= 1:
                    te.wait_ge(SM["ev2"], hm * 4)
                for fc in range(32):
                    for oc in range(4):
                        inst = te.matmul(
                            psA[oc][:, 0:tw],
                            w2j[:, fc * D + half * 512 + oc * 128: fc * D + half * 512 + (oc + 1) * 128],
                            hbuf[:, fc * 512: fc * 512 + tw],
                            start=(fc == 0), stop=(fc == 31))
                        if fc == 31 and oc == 3:
                            inst.then_inc(SM["pm2"], 1)
            for q in range(nq):
                yq = gbase + q
                te.wait_ge(SM["ev2"], ti * 8 + 8)
                if yq >= 1:
                    te.wait_ge(SM["yte"], yq)
                for dj in range(DCH):
                    inst = te.transpose(psT[:, dj * 128:(dj + 1) * 128],
                                        ybuf[:, dj * 512 + q * 128: dj * 512 + (q + 1) * 128],
                                        identb[:])
                inst.then_inc(SM["ytr"], 1)
            ti += 1
    p.bump("ntr", ngtot)
    p.bump("pm1", 32 * len(TTOT))
    p.bump("pm2", 2 * len(TTOT))
    p.bump("ytr", ngtot)

    @blk.vector
    def _(ve, JN2=JN2):
        ve.wait_ge(SM[JN2[0]], JN2[1])
        ve.wait_ge(SM["bld"], 32)
        ti = 0
        for (j, t, tw, toff) in TTOT:
            nq = tw // 128
            gbase = sum(x[2] // 128 for x in TTOT[:ti])
            if ti >= 1:
                ve.wait_ge(SM["pm1"], ti * 32)
                ve.wait_ge(SM["pm2"], ti * 2)
                ve.wait_ge(SM["ytr"], gbase)
            for q in range(nq):
                gi = gbase + q
                ve.wait_ge(SM["ntr"], gi + 1)
                ve.tensor_copy(
                    normfm.ap().rearrange("p (dj c) -> p dj c", c=512)[:, :, q * 128:(q + 1) * 128],
                    psT.ap().rearrange("p (dj c) -> p dj c", c=128)
                ).then_inc(SM["nte"], 1)
            ve.drain()
            for fc in range(32):
                gm1 = ti * 32 + fc
                ve.wait_ge(SM["pm1"], gm1 + 1)
                if gm1 >= 2:
                    ve.wait_ge(SM["sqp"], gm1 - 1)
                ps_ = psA[((fc // 2) % 2) * 2 + (fc % 2)]
                ve.tensor_scalar(hpre[fc % 2][:, 0:tw], ps_[:, 0:tw],
                                 b1s[:, j * 32 + fc:j * 32 + fc + 1], None,
                                 ALU.add).then_inc(SM["ev1"], 1)
                ve.wait_ge(SM["sqp"], gm1 + 1)
                ve.tensor_scalar(hsq[fc % 2][:, 0:tw], hsq[fc % 2][:, 0:tw],
                                 GELC, 1.0, ALU.mult, ALU.add)
                ve.drain()
                ve.tensor_tensor(hsq[fc % 2][:, 0:tw], hsq[fc % 2][:, 0:tw],
                                 hpre[fc % 2][:, 0:tw], ALU.mult).then_inc(SM["fv"], 1)
                ve.drain()
                ve.wait_ge(SM["sqp2"], gm1 + 1)
                ve.scalar_tensor_tensor(hbuf[:, fc * 512:fc * 512 + tw],
                                        hth[fc % 2][:, 0:tw], 1.0,
                                        hpre[fc % 2][:, 0:tw], ALU.add, ALU.mult
                                        ).then_inc(SM["gel"], 1)
            for half in range(2):
                ve.wait_ge(SM["pm2"], ti * 2 + half + 1)
                for oc in range(4):
                    ve.tensor_scalar(ybuf[:, (half * 4 + oc) * 512:(half * 4 + oc) * 512 + tw],
                                     psA[oc][:, 0:tw], 0.5,
                                     b2s[:, j * 8 + half * 4 + oc:j * 8 + half * 4 + oc + 1],
                                     ALU.mult, ALU.add).then_inc(SM["ev2"], 1)
            ve.drain()
            for q in range(nq):
                yq = gbase + q
                ve.wait_ge(SM["ytr"], yq + 1)
                if yq >= 1:
                    ve.wait_ge(SM["yst0"], 16 * yq)
                ve.tensor_copy(ytm[:], psT[:, 0:1024]).then_inc(SM["yte"], 1)
                ve.drain()
            ti += 1
    p.bump("nte", ngtot)
    p.bump("ev1", 32 * len(TTOT))
    p.bump("gel", 32 * len(TTOT))
    p.bump("fv", 32 * len(TTOT))
    p.bump("ev2", 8 * len(TTOT))
    p.bump("yte", ngtot)

    @blk.scalar
    def _(sc, JN2=JN2):
        sc.wait_ge(SM[JN2[0]], JN2[1])
        ti = 0
        for (j, t, tw, toff) in TTOT:
            for fc in range(32):
                gm1 = ti * 32 + fc
                sc.wait_ge(SM["ev1"], gm1 + 1)
                if gm1 >= 2:
                    sc.wait_ge(SM["fv"], gm1 - 1)
                    sc.wait_ge(SM["gel"], gm1 - 1)
                sc.activation(out=hsq[fc % 2][:, 0:tw], in_=hpre[fc % 2][:, 0:tw],
                              func=AF.Square).then_inc(SM["sqp"], 1)
                sc.wait_ge(SM["fv"], gm1 + 1)
                sc.activation(out=hth[fc % 2][:, 0:tw], in_=hsq[fc % 2][:, 0:tw],
                              func=AF.Tanh, scale=GELS).then_inc(SM["sqp2"], 1)
            ti += 1
    p.bump("sqp", 32 * len(TTOT))
    p.bump("sqp2", 32 * len(TTOT))

    JN3 = emit_join(p, blk, waits=[
        ("gpsimd", [("gno", p.cnt("gno")),
                    ("yst0", p.cnt("yst0")), ("tld", p.cnt("tld")), ("gld", p.cnt("gld"))]),
        ("sync", [("bld", p.cnt("bld"))])])
    scF.close()

    scO = ExitStack()
    ownr = p.sbs(scO, "p2_ownr", [128, 32], I32)
    x1t = [p.sbs(scO, f"p2_x1t{i}", [128, D], F32) for i in range(2)]
    yg = [p.sbs(scO, f"p2_yg{i}", [128, D], BF16) for i in range(2)]
    feat = p.sbs(scO, "p2_feat", [128, D], F32)
    sfull = p.sbs(scO, "p2_sfull", [128, D], F32)
    x2t = [p.sbs(scO, f"p2_x2t{i}", [128, D], F32) for i in range(2)]
    bnp = p.sbs(scO, "p2_bnp", [128, 16], F32)
    bnp2 = p.sbs(scO, "p2_bnp2", [128, 16], F32)

    @blk.sync
    def _(sync, JN3=JN3):
        sync.wait_ge(SM[JN3[0]], JN3[1])
        sync.dma_start(out=ownr[:], in_=io["ownrows"][:, :]).then_inc(SM["opg"], 16)
        for i in range(32):
            sync.wait_ge(SM["x2v"], i + 1)
            sync.dma_start(out=io["x2out"][i * 128:(i + 1) * 128, :],
                           in_=x2t[i % 2][:]
                           ).then_inc(SM["x2s0" if i % 2 == 0 else "x2s1"], 16)
        sync.wait_ge(SM["x2s0"], p.cnt("x2s0") + 16 * 16)
        sync.wait_ge(SM["x2s1"], p.cnt("x2s1") + 16 * 16)
    p.bump("opg", 16)
    p.bump("x2s0", 16 * 16)
    p.bump("x2s1", 16 * 16)

    @blk.gpsimd
    def _(gp, JN3=JN3):
        gp.wait_ge(SM[JN3[0]], JN3[1])
        gp.wait_ge(SM["opg"], 16)
        for i in range(32):
            if i >= 2:
                gp.wait_ge(SM["sv"], i - 1)
            gp.indirect_dma_start(
                out=yg[i % 2][:], out_offset=None, in_=io["yH"][:, :],
                in_offset=IndirectOffsetOnAxis(ap=ownr[:, i:i + 1], axis=0),
            ).then_inc(SM["ygt0" if i % 2 == 0 else "ygt1"], 16)
    p.bump("ygt0", 16 * 16)
    p.bump("ygt1", 16 * 16)

    @blk.scalar
    def _(sc, JN3=JN3):
        sc.wait_ge(SM[JN3[0]], JN3[1])
        for i in range(32):
            if i >= 2:
                sc.wait_ge(SM["x2v"], i - 1)
            sc.dma_start(out=x1t[i % 2][:],
                         in_=io["x1own"][i * 128:(i + 1) * 128, :]
                         ).then_inc(SM["xld0" if i % 2 == 0 else "xld1"], 16)
            sc.wait_ge(SM["lnp"], i + 1)
            sc.activation(out=bnp[:, 15:16], in_=bnp[:, 14:15], func=AF.Sqrt
                          ).then_inc(SM["sqp"], 1)
            sc.wait_ge(SM["lnp2"], i + 1)
            sc.activation(out=bnp2[:, 15:16], in_=bnp2[:, 14:15], func=AF.Sqrt
                          ).then_inc(SM["sqp2"], 1)
    p.bump("xld0", 16 * 16)
    p.bump("xld1", 16 * 16)

    @blk.vector
    def _(ve, JN3=JN3, SQ=p.cnt("sqp"), SQ2=p.cnt("sqp2")):
        ve.wait_ge(SM[JN3[0]], JN3[1])
        for i in range(32):
            ve.wait_ge(SM["xld0" if i % 2 == 0 else "xld1"], 16 * (i // 2 + 1))
            ve.bn_stats(bnp[:, 0:6], x1t[i % 2][:, 0:512])
            ve.bn_stats(bnp[:, 6:12], x1t[i % 2][:, 512:1024])
            ve.drain()
            ve.bn_aggr(bnp[:, 12:14], bnp[:, 0:12])
            ve.drain()
            ve.tensor_scalar(bnp[:, 14:15], bnp[:, 13:14], float(LN_EPS), None,
                             ALU.add).then_inc(SM["lnp"], 1)
            ve.wait_ge(SM["sqp"], SQ + i + 1)
            ve.reciprocal(bnp[:, 14:15], bnp[:, 15:16])
            ve.drain()
            ve.tensor_scalar(feat[:], x1t[i % 2][:], bnp[:, 12:13], bnp[:, 14:15],
                             ALU.subtract, ALU.mult)
            ve.drain()
            ve.wait_ge(SM["ygt0" if i % 2 == 0 else "ygt1"], 16 * (i // 2 + 1))
            ve.tensor_tensor(sfull[:], feat[:], yg[i % 2][:], ALU.add
                             ).then_inc(SM["sv"], 1)
            ve.drain()
            ve.bn_stats(bnp2[:, 0:6], sfull[:, 0:512])
            ve.bn_stats(bnp2[:, 6:12], sfull[:, 512:1024])
            ve.drain()
            ve.bn_aggr(bnp2[:, 12:14], bnp2[:, 0:12])
            ve.drain()
            ve.tensor_scalar(bnp2[:, 14:15], bnp2[:, 13:14], float(LN_EPS), None,
                             ALU.add).then_inc(SM["lnp2"], 1)
            ve.wait_ge(SM["sqp2"], SQ2 + i + 1)
            ve.reciprocal(bnp2[:, 14:15], bnp2[:, 15:16])
            ve.drain()
            if i >= 2:
                ve.wait_ge(SM["x2s0" if i % 2 == 0 else "x2s1"],
                           16 * (i // 2))
            ve.tensor_scalar(sfull[:], sfull[:], bnp2[:, 12:13], bnp2[:, 14:15],
                             ALU.subtract, ALU.mult)
            ve.drain()
            ve.tensor_tensor(x2t[i % 2][:], x1t[i % 2][:], sfull[:], ALU.add
                             ).then_inc(SM["x2v"], 1)
            ve.drain()
    p.bump("lnp", 32)
    p.bump("lnp2", 32)
    p.bump("sqp", 32)
    p.bump("sqp2", 32)
    p.bump("sv", 32)
    p.bump("x2v", 32)

    JNF = emit_join(p, blk, waits=[
        ("sync", [("x2s0", p.cnt("x2s0")), ("x2s1", p.cnt("x2s1"))]),
        ("gpsimd", [("ygt0", p.cnt("ygt0")), ("ygt1", p.cnt("ygt1")),
                    ("opg", p.cnt("opg"))]),
        ("scalar", [("xld0", p.cnt("xld0")), ("xld1", p.cnt("xld1"))])])
    scO.close()
    return JNF


def emit_p2_owner(p, io, JN=None):
    blk = p.block
    SM = p._sems

    scO = ExitStack()
    ownr = p.sbs(scO, "o_ownr", [128, 32], I32)
    x1t = [p.sbs(scO, f"o_x1t{i}", [128, D], F32) for i in range(2)]
    yg = [p.sbs(scO, f"o_yg{i}", [128, D], BF16) for i in range(2)]
    feat = p.sbs(scO, "o_feat", [128, D], F32)
    sfull = p.sbs(scO, "o_sfull", [128, D], F32)
    x2t = [p.sbs(scO, f"o_x2t{i}", [128, D], F32) for i in range(2)]
    bnp = p.sbs(scO, "o_bnp", [128, 16], F32)
    bnp2 = p.sbs(scO, "o_bnp2", [128, 16], F32)

    @blk.sync
    def _(sync):
        if JN is not None:
            sync.wait_ge(SM[JN[0]], JN[1])
        sync.dma_start(out=ownr[:], in_=io["ownrows"][:, :]).then_inc(SM["opg"], 16)
        for i in range(32):
            sync.wait_ge(SM["x2v"], i + 1)
            sync.dma_start(out=io["x2out"][i * 128:(i + 1) * 128, :],
                           in_=x2t[i % 2][:]
                           ).then_inc(SM["x2s0" if i % 2 == 0 else "x2s1"], 16)
        sync.wait_ge(SM["x2s0"], 16 * 16)
        sync.wait_ge(SM["x2s1"], 16 * 16)
    p.bump("opg", 16)
    p.bump("x2s0", 16 * 16)
    p.bump("x2s1", 16 * 16)

    @blk.gpsimd
    def _(gp):
        if JN is not None:
            gp.wait_ge(SM[JN[0]], JN[1])
        gp.wait_ge(SM["opg"], 16)
        for i in range(32):
            if i >= 2:
                gp.wait_ge(SM["sv"], i - 1)
            gp.indirect_dma_start(
                out=yg[i % 2][:], out_offset=None, in_=io["yH"][:, :],
                in_offset=IndirectOffsetOnAxis(ap=ownr[:, i:i + 1], axis=0),
            ).then_inc(SM["ygt0" if i % 2 == 0 else "ygt1"], 16)
    p.bump("ygt0", 16 * 16)
    p.bump("ygt1", 16 * 16)

    @blk.scalar
    def _(sc):
        if JN is not None:
            sc.wait_ge(SM[JN[0]], JN[1])
        for i in range(32):
            if i >= 2:
                sc.wait_ge(SM["x2v"], i - 1)
            sc.dma_start(out=x1t[i % 2][:],
                         in_=io["x1own"][i * 128:(i + 1) * 128, :]
                         ).then_inc(SM["xld0" if i % 2 == 0 else "xld1"], 16)
            sc.wait_ge(SM["lnp"], i + 1)
            sc.activation(out=bnp[:, 15:16], in_=bnp[:, 14:15], func=AF.Sqrt
                          ).then_inc(SM["sqp"], 1)
            sc.wait_ge(SM["lnp2"], i + 1)
            sc.activation(out=bnp2[:, 15:16], in_=bnp2[:, 14:15], func=AF.Sqrt
                          ).then_inc(SM["sqp2"], 1)
    p.bump("xld0", 16 * 16)
    p.bump("xld1", 16 * 16)

    @blk.vector
    def _(ve):
        if JN is not None:
            ve.wait_ge(SM[JN[0]], JN[1])
        for i in range(32):
            ve.wait_ge(SM["xld0" if i % 2 == 0 else "xld1"], 16 * (i // 2 + 1))
            ve.bn_stats(bnp[:, 0:6], x1t[i % 2][:, 0:512])
            ve.bn_stats(bnp[:, 6:12], x1t[i % 2][:, 512:1024])
            ve.drain()
            ve.bn_aggr(bnp[:, 12:14], bnp[:, 0:12])
            ve.drain()
            ve.tensor_scalar(bnp[:, 14:15], bnp[:, 13:14], float(LN_EPS), None,
                             ALU.add).then_inc(SM["lnp"], 1)
            ve.wait_ge(SM["sqp"], i + 1)
            ve.reciprocal(bnp[:, 14:15], bnp[:, 15:16])
            ve.drain()
            ve.tensor_scalar(feat[:], x1t[i % 2][:], bnp[:, 12:13], bnp[:, 14:15],
                             ALU.subtract, ALU.mult)
            ve.drain()
            ve.wait_ge(SM["ygt0" if i % 2 == 0 else "ygt1"], 16 * (i // 2 + 1))
            ve.tensor_tensor(sfull[:], feat[:], yg[i % 2][:], ALU.add
                             ).then_inc(SM["sv"], 1)
            ve.drain()
            ve.bn_stats(bnp2[:, 0:6], sfull[:, 0:512])
            ve.bn_stats(bnp2[:, 6:12], sfull[:, 512:1024])
            ve.drain()
            ve.bn_aggr(bnp2[:, 12:14], bnp2[:, 0:12])
            ve.drain()
            ve.tensor_scalar(bnp2[:, 14:15], bnp2[:, 13:14], float(LN_EPS), None,
                             ALU.add).then_inc(SM["lnp2"], 1)
            ve.wait_ge(SM["sqp2"], i + 1)
            ve.reciprocal(bnp2[:, 14:15], bnp2[:, 15:16])
            ve.drain()
            if i >= 2:
                ve.wait_ge(SM["x2s0" if i % 2 == 0 else "x2s1"], 16 * (i // 2))
            ve.tensor_scalar(sfull[:], sfull[:], bnp2[:, 12:13], bnp2[:, 14:15],
                             ALU.subtract, ALU.mult)
            ve.drain()
            ve.tensor_tensor(x2t[i % 2][:], x1t[i % 2][:], sfull[:], ALU.add
                             ).then_inc(SM["x2v"], 1)
            ve.drain()
    p.bump("lnp", 32)
    p.bump("lnp2", 32)
    p.bump("sqp", 32)
    p.bump("sqp2", 32)
    p.bump("sv", 32)
    p.bump("x2v", 32)

    ret = emit_join(p, blk, waits=[
        ("sync", [("x2s0", p.cnt("x2s0")), ("x2s1", p.cnt("x2s1"))]),
        ("gpsimd", [("ygt0", p.cnt("ygt0")), ("ygt1", p.cnt("ygt1"))]),
        ("scalar", [("xld0", p.cnt("xld0")), ("xld1", p.cnt("xld1"))])])
    scO.close()
    return ret


"""Phase 3: L1_attn — causal relu attention, 8 heads, one batch of 8192 tokens.

Per-core: own 4096 rows as 16 tiles of 256 (tile t' = global 256-block
2*t'+slot); keys = whole batch, segment s = keys [512s, 512s+512).
Causal schedule: group (s, t') exists for t' >= s, 4 key-chunks each;
masked (per-core maskTab) iff t' == s.  SPMD-uniform: slot enters only
through input data (x2own rows, maskTab contents).

out[n,v] = sum_h P_h @ V_h with P = relu(Q K^T)/8192 causal;
Q/K/V projections of ln3 = LN(x2) built per (head, seg) from an FM copy
of ln3 staged in HBM by the prologue.

io: x2own [4096,1024]f32, x2full [8192,1024]f32,
    WkTh/WqTh [128, 8*8*128]bf16  (col (h*8+dc)*128+e  = W[h,e,dc*128+p]),
    WvTh      [128, 8*8*1024]bf16 (col (h*8+dc)*1024+v = Wv[h,v,dc*128+p]),
    maskTab [128, 4*256]bf16, identb [128,128]bf16,
    ln3fH dint [128, 8*8192]bf16, ln3oH dint [128, 8*4096]bf16,
    x3own out [4096,1024]f32.
"""

NH = 8
NSEG = 16
TP = 16
ISC = 1.0 / 8192.0


def emit_p3(p, io, JN=None):
    nc = p.nc
    blk = p.block
    SM = p._sems

    sc3 = ExitStack()
    identb = p.sbs(sc3, "p3_identb", [128, 128], BF16)
    outacc = p.sbs(sc3, "p3_outacc", [128, 32 * 1024], F32)


    @blk.sync
    def _(sync):
        if JN is not None:
            sync.wait_ge(SM[JN[0]], JN[1])
        sync.dma_start(out=identb[:], in_=io["identb"][:, :]).then_inc(SM["ini3"], 16)
    p.bump("ini3", 16)
    INI = p.cnt("ini3")

    ln3fH3 = io["ln3fH"].rearrange("p (j c) -> p j c", c=8192)
    ln3oH3 = io["ln3oH"].rearrange("p (j c) -> p j c", c=4096)

    scP = ExitStack()
    x2t = [p.sbs(scP, f"p3_x2t{i}", [128, D], F32) for i in range(2)]
    lnt = [p.sbs(scP, f"p3_lnt{i}", [128, D], BF16) for i in range(2)]
    bnsc = [p.sbs(scP, f"p3_bn{i}", [128, 16], F32) for i in range(2)]
    fs = [p.sbs(scP, f"p3_fs{i}", [128, DCH * 128], BF16) for i in range(2)]
    psTb = [p.pss(scP, f"p3_psTb{i}", [128, 1024], BF16) for i in range(2)]
    NTP = 64 + 32

    @blk.scalar
    def _(sc):
        if JN is not None:
            sc.wait_ge(SM[JN[0]], JN[1])
        for i in range(NTP):
            src = (io["x2full"][i * 128:(i + 1) * 128, :] if i < 64 else
                   io["x2own"][(i - 64) * 128:(i - 63) * 128, :])
            if i >= 2:
                sc.wait_ge(SM["pv2"], i - 1)
            sc.dma_start(out=x2t[i % 2][:], in_=src
                         ).then_inc(SM["xf0" if i % 2 == 0 else "xf1"], 16)
            sc.wait_ge(SM["pv1"], i + 1)
            sc.activation(out=bnsc[i % 2][:, 15:16], in_=bnsc[i % 2][:, 14:15],
                          func=AF.Sqrt).then_inc(SM["ps1"], 1)
            dst = (ln3fH3[:, :, i * 128:(i + 1) * 128] if i < 64 else
                   ln3oH3[:, :, (i - 64) * 128:(i - 63) * 128])
            sc.wait_ge(SM["ple"], i + 1)
            sc.dma_start(out=dst,
                         in_=fs[i % 2].ap().rearrange("p (j c) -> p j c", c=128)
                         ).then_inc(SM["lfs0" if i % 2 == 0 else "lfs1"], 16)

    @blk.tensor
    def _(te):
        te.wait_ge(SM["ini3"], INI)
        for i in range(NTP):
            te.wait_ge(SM["pv2"], i + 1)
            if i >= 2:
                te.wait_ge(SM["ple"], i - 1)
            for j in range(DCH):
                inst = te.transpose(psTb[i % 2][:, j * 128:(j + 1) * 128],
                                    lnt[i % 2][:, j * 128:(j + 1) * 128],
                                    identb[:])
            inst.then_inc(SM["ptl"], 1)

    @blk.vector
    def _(ve):
        for i in range(NTP):
            ve.wait_ge(SM["xf0" if i % 2 == 0 else "xf1"], 16 * (i // 2 + 1))
            sc_ = bnsc[i % 2]
            ve.bn_stats(sc_[:, 0:6], x2t[i % 2][:, 0:512])
            ve.bn_stats(sc_[:, 6:12], x2t[i % 2][:, 512:1024])
            ve.drain()
            ve.bn_aggr(sc_[:, 12:14], sc_[:, 0:12])
            ve.drain()
            ve.tensor_scalar(sc_[:, 14:15], sc_[:, 13:14], float(LN_EPS), None,
                             ALU.add).then_inc(SM["pv1"], 1)
            ve.wait_ge(SM["ps1"], i + 1)
            ve.reciprocal(sc_[:, 14:15], sc_[:, 15:16])
            ve.drain()
            if i >= 2:
                ve.wait_ge(SM["ptl"], i - 1)
            ve.tensor_scalar(lnt[i % 2][:], x2t[i % 2][:], sc_[:, 12:13],
                             sc_[:, 14:15], ALU.subtract, ALU.mult
                             ).then_inc(SM["pv2"], 1)
            ve.wait_ge(SM["ptl"], i + 1)
            if i >= 2:
                ve.wait_ge(SM["lfs0" if i % 2 == 0 else "lfs1"], 16 * (i // 2))
            ve.tensor_copy(fs[i % 2][:], psTb[i % 2][:, 0:DCH * 128]
                           ).then_inc(SM["ple"], 1)
        ve.drain()

    for s_, n_ in (("ps1", NTP), ("pv1", NTP), ("pv2", NTP), ("ptl", NTP),
                   ("ple", NTP), ("xf0", 16 * NTP // 2), ("xf1", 16 * NTP // 2),
                   ("lfs0", 16 * NTP // 2), ("lfs1", 16 * NTP // 2)):
        p.bump(s_, n_)

    JN2 = emit_join(p, blk, waits=[
        ("scalar", [("lfs0", p.cnt("lfs0")), ("lfs1", p.cnt("lfs1"))]),
        ("vector", [("ple", p.cnt("ple"))])])
    scP.close()

    scM = ExitStack()
    WvT = p.sbs(scM, "p3_WvT", [128, DCH * 1024], BF16)
    WkT = p.sbs(scM, "p3_WkT", [128, DCH * 128], BF16)
    WqT = p.sbs(scM, "p3_WqT", [128, DCH * 128], BF16)
    QhT = p.sbs(scM, "p3_QhT", [128, 4096], BF16)
    KhS = p.sbs(scM, "p3_KhS", [128, 512], BF16)
    VhS = p.sbs(scM, "p3_VhS", [128, 4 * 1024], BF16)
    stf = p.sbs(scM, "p3_stf", [128, DCH * 512], BF16)
    qtf = [p.sbs(scM, f"p3_qtf{i}", [128, DCH * 256], BF16) for i in range(2)]
    maskT = p.sbs(scM, "p3_maskT", [128, 4 * 256], BF16)
    STb = [p.sbs(scM, f"p3_STb{i}", [128, 256], BF16) for i in range(4)]
    STm = [p.sbs(scM, f"p3_STm{i}", [128, 256], BF16) for i in range(4)]
    ps2 = [p.pss(scM, f"p3_ps2_{i}", [128, 512]) for i in range(4)]
    psS = p.pss(scM, "p3_psS", [128, 512])
    psKQ = p.pss(scM, "p3_psKQ", [128, 512])
    psV = [p.pss(scM, f"p3_psV{i}", [128, 512]) for i in range(2)]

    GROUPS = [(s, tp) for s in range(NSEG) for tp in range(s, TP)]
    NG = len(GROUPS)

    @blk.sync
    def _(sync):
        sync.wait_ge(SM[JN2[0]], JN2[1])
        sync.dma_start(out=maskT[:], in_=io["maskTab"][:, :]).then_inc(SM["mld"], 16)
        for h in range(NH):
            if h >= 1:
                sync.wait_ge(SM["oa"], 136 * h)
            sync.dma_start(out=WvT[:],
                           in_=io["WvTh"][:, h * 8192:(h + 1) * 8192]
                           ).then_inc(SM["wld"], 16)
            sync.dma_start(out=WkT[:],
                           in_=io["WkTh"][:, h * 1024:(h + 1) * 1024]
                           ).then_inc(SM["wld"], 16)
            sync.dma_start(out=WqT[:],
                           in_=io["WqTh"][:, h * 1024:(h + 1) * 1024]
                           ).then_inc(SM["wld"], 16)
            for tp in range(TP):
                if h * TP + tp >= 2:
                    sync.wait_ge(SM["qpe"], h * TP + tp - 1)
                sync.dma_start(
                    out=qtf[tp % 2].ap().rearrange("p (j c) -> p j c", c=256),
                    in_=ln3oH3[:, :, tp * 256:(tp + 1) * 256]
                    ).then_inc(SM["qld0" if tp % 2 == 0 else "qld1"], 16)
            for s in range(NSEG):
                g = h * NSEG + s
                if g >= 1:
                    sync.wait_ge(SM["vpe"], 8 * g)
                sync.dma_start(
                    out=stf.ap().rearrange("p (j c) -> p j c", c=512),
                    in_=ln3fH3[:, :, s * 512:(s + 1) * 512]
                    ).then_inc(SM["sld"], 16)
    p.bump("wld", NH * 48)
    p.bump("mld", 16)
    p.bump("qld0", NH * TP // 2 * 16)
    p.bump("qld1", NH * TP // 2 * 16)
    p.bump("sld", NH * NSEG * 16)

    @blk.tensor
    def _(te):
        te.wait_ge(SM[JN2[0]], JN2[1])
        nchunk = 0
        ngrp = 0
        nvv = 0
        for h in range(NH):
            te.wait_ge(SM["wld"], (h + 1) * 48)
            if h >= 1:
                te.wait_ge(SM["kev"], h * NSEG)
            for tp in range(TP):
                it = h * TP + tp
                te.wait_ge(SM["qld0" if tp % 2 == 0 else "qld1"],
                           16 * (it // 2 + 1))
                if it >= 1:
                    te.wait_ge(SM["qev"], it)
                half = (tp % 2) * 256
                for dc in range(DCH):
                    inst = te.matmul(psKQ[:, half:half + 256],
                                     WqT[:, dc * 128:(dc + 1) * 128],
                                     qtf[tp % 2][:, dc * 256:(dc + 1) * 256],
                                     start=(dc == 0), stop=(dc == DCH - 1))
                inst.then_inc(SM["qpe"], 1)
            for s in range(NSEG):
                gseg = h * NSEG + s
                te.wait_ge(SM["sld"], 16 * (gseg + 1))
                te.wait_ge(SM["qev"], (h + 1) * TP)
                if gseg >= 1:
                    te.wait_ge(SM["kev"], gseg)
                for dc in range(DCH):
                    inst = te.matmul(psKQ[:, 0:512],
                                     WkT[:, dc * 128:(dc + 1) * 128],
                                     stf[:, dc * 512:(dc + 1) * 512],
                                     start=(dc == 0), stop=(dc == DCH - 1))
                inst.then_inc(SM["kpe"], 1)
                for mch in range(4):
                    for vh in range(2):
                        if nvv >= 2:
                            te.wait_ge(SM["vev"], nvv - 1)
                        for dc in range(DCH):
                            inst = te.matmul(
                                psV[nvv % 2][:, 0:512],
                                stf[:, dc * 512 + mch * 128:dc * 512 + mch * 128 + 128],
                                WvT[:, dc * 1024 + vh * 512:dc * 1024 + vh * 512 + 512],
                                start=(dc == 0), stop=(dc == DCH - 1))
                        nvv += 1
                        inst.then_inc(SM["vpe"], 1)
                te.wait_ge(SM["kev"], gseg + 1)
                te.wait_ge(SM["vev"], 8 * (gseg + 1))
                for tp in range(s, TP):
                    masked = (tp == s)
                    chunk_of = [nchunk + j for j in range(4)]
                    emitted_st = 0
                    for j in range(4):
                        while emitted_st <= j + 1 and emitted_st < 4:
                            jj = emitted_st
                            c = chunk_of[jj]
                            if c >= 2:
                                te.wait_ge(SM["sse"], c - 1)
                            half = (c % 2) * 256
                            te.matmul(psS[:, half:half + 256],
                                      KhS[:, jj * 128:(jj + 1) * 128],
                                      QhT[:, tp * 256:(tp + 1) * 256],
                                      start=True, stop=True
                                      ).then_inc(SM["spe"], 1)
                            emitted_st += 1
                        c = chunk_of[j]
                        if masked:
                            te.wait_ge(SM["ssm"], (h * NSEG + s) * 4 + j + 1)
                        else:
                            te.wait_ge(SM["sse"], c + 1)
                        if j == 0 and ngrp >= 1:
                            te.wait_ge(SM["oa"], ngrp)
                        src = STm[c % 4] if masked else STb[c % 4]
                        for nb in range(2):
                            for vh in range(2):
                                inst = te.matmul(
                                    ps2[nb * 2 + vh][:, 0:512],
                                    src[:, nb * 128:(nb + 1) * 128],
                                    VhS[:, j * 1024 + vh * 512:j * 1024 + vh * 512 + 512],
                                    start=(j == 0), stop=(j == 3))
                        if j == 3:
                            inst.then_inc(SM["g2"], 1)
                    nchunk += 4
                    ngrp += 1
    NCH = NG * 4 * NH
    p.bump("qpe", NH * TP)
    p.bump("kpe", NH * NSEG)
    p.bump("vpe", NH * NSEG * 8)
    p.bump("spe", NCH)
    p.bump("g2", NG * NH)

    @blk.scalar
    def _(sc):
        sc.wait_ge(SM[JN2[0]], JN2[1])
        nchunk = 0
        ngrp = 0
        for h in range(NH):
            for s in range(NSEG):
                for tp in range(s, TP):
                    for j in range(4):
                        c = nchunk + j
                        sc.wait_ge(SM["spe"], c + 1)
                        if ngrp >= 1:
                            sc.wait_ge(SM["g2"], ngrp)
                        half = (c % 2) * 256
                        sc.activation(out=STb[c % 4][:], in_=psS[:, half:half + 256],
                                      func=AF.Relu, scale=ISC
                                      ).then_inc(SM["sse"], 1)
                    nchunk += 4
                    ngrp += 1
    p.bump("sse", NCH)

    @blk.vector
    def _(ve):
        ve.wait_ge(SM[JN2[0]], JN2[1])
        ve.wait_ge(SM["mld"], 16)
        nchunk = 0
        ngrp = 0
        nvv = 0
        for h in range(NH):
            for tp in range(TP):
                it = h * TP + tp
                ve.wait_ge(SM["qpe"], it + 1)
                half = (tp % 2) * 256
                ve.tensor_copy(QhT[:, tp * 256:(tp + 1) * 256],
                               psKQ[:, half:half + 256]).then_inc(SM["qev"], 1)
            for s in range(NSEG):
                gseg = h * NSEG + s
                ve.wait_ge(SM["kpe"], gseg + 1)
                ve.tensor_copy(KhS[:], psKQ[:, 0:512]).then_inc(SM["kev"], 1)
                for i8 in range(8):
                    ve.wait_ge(SM["vpe"], nvv + 1)
                    mch, vh = i8 // 2, i8 % 2
                    ve.tensor_copy(
                        VhS[:, mch * 1024 + vh * 512:mch * 1024 + vh * 512 + 512],
                        psV[nvv % 2][:, 0:512]).then_inc(SM["vev"], 1)
                    nvv += 1
                for tp in range(s, TP):
                    masked = (tp == s)
                    if masked:
                        for j in range(4):
                            c = nchunk + j
                            ve.wait_ge(SM["sse"], c + 1)
                            if ngrp >= 1:
                                ve.wait_ge(SM["g2"], ngrp)
                            ve.tensor_tensor(STm[c % 4][:], STb[c % 4][:],
                                             maskT[:, j * 256:(j + 1) * 256],
                                             ALU.mult).then_inc(SM["ssm"], 1)
                    ve.wait_ge(SM["g2"], ngrp + 1)
                    first = (h == 0 and s == 0)
                    for nb in range(2):
                        for vh in range(2):
                            col = (tp * 2 + nb) * 1024 + vh * 512
                            if first:
                                inst = ve.tensor_copy(
                                    outacc[:, col:col + 512],
                                    ps2[nb * 2 + vh][:, 0:512])
                            else:
                                inst = ve.tensor_tensor(
                                    outacc[:, col:col + 512],
                                    ps2[nb * 2 + vh][:, 0:512],
                                    outacc[:, col:col + 512], ALU.add)
                    inst.then_inc(SM["oa"], 1)
                    nchunk += 4
                    ngrp += 1
        ve.drain()
    p.bump("qev", NH * TP)
    p.bump("kev", NH * NSEG)
    p.bump("vev", NH * NSEG * 8)
    p.bump("ssm", NH * NSEG * 4)
    p.bump("oa", NG * NH)

    JN3 = emit_join(p, blk, waits=[
        ("vector", [("oa", p.cnt("oa"))])])
    scM.close()

    scE = ExitStack()
    xe = [p.sbs(scE, f"p3_xe{i}", [128, D], F32) for i in range(2)]
    lnE = p.sbs(scE, "p3_lnE", [128, D], F32)
    aR = [p.sbs(scE, f"p3_aR{i}", [128, D], F32) for i in range(2)]
    x3t = [p.sbs(scE, f"p3_x3t{i}", [128, D], F32) for i in range(2)]
    bnE = [p.sbs(scE, f"p3_bnE{i}", [128, 16], F32) for i in range(2)]
    bnE2 = [p.sbs(scE, f"p3_bnE2{i}", [128, 16], F32) for i in range(2)]

    @blk.scalar
    def _(sc):
        sc.wait_ge(SM[JN3[0]], JN3[1])
        for i in range(32):
            if i >= 2:
                sc.wait_ge(SM["pe4"], i - 1)
            sc.dma_start(out=xe[i % 2][:],
                         in_=io["x2own"][i * 128:(i + 1) * 128, :]
                         ).then_inc(SM["xe0" if i % 2 == 0 else "xe1"], 16)
            sc.wait_ge(SM["pe1"], 2 * i + 1)
            sc.activation(out=bnE[i % 2][:, 15:16], in_=bnE[i % 2][:, 14:15],
                          func=AF.Sqrt).then_inc(SM["ps1"], 1)
            sc.wait_ge(SM["pe1"], 2 * i + 2)
            sc.activation(out=bnE2[i % 2][:, 15:16], in_=bnE2[i % 2][:, 14:15],
                          func=AF.Sqrt).then_inc(SM["ps1"], 1)
            sc.wait_ge(SM["pe4"], i + 1)
            sc.dma_start(out=io["x3own"][i * 128:(i + 1) * 128, :],
                         in_=x3t[i % 2][:]
                         ).then_inc(SM["est0" if i % 2 == 0 else "est1"], 16)
        sc.wait_ge(SM["est0"], 16 * 16)
        sc.wait_ge(SM["est1"], 16 * 16)

    @blk.vector
    def _(ve):
        ve.wait_ge(SM[JN3[0]], JN3[1])
        ps1base = p.cnt("ps1")
        for i in range(32):
            ve.wait_ge(SM["xe0" if i % 2 == 0 else "xe1"], 16 * (i // 2 + 1))
            b1 = bnE[i % 2]
            ve.bn_stats(b1[:, 0:6], xe[i % 2][:, 0:512])
            ve.bn_stats(b1[:, 6:12], xe[i % 2][:, 512:1024])
            ve.drain()
            ve.bn_aggr(b1[:, 12:14], b1[:, 0:12])
            ve.drain()
            ve.tensor_scalar(b1[:, 14:15], b1[:, 13:14], float(LN_EPS), None,
                             ALU.add).then_inc(SM["pe1"], 1)
            ve.wait_ge(SM["ps1"], ps1base + 2 * i + 1)
            ve.reciprocal(b1[:, 14:15], b1[:, 15:16])
            ve.drain()
            ve.tensor_scalar(lnE[:], xe[i % 2][:], b1[:, 12:13],
                             b1[:, 14:15], ALU.subtract, ALU.mult)
            ve.drain()
            if i >= 2:
                ve.wait_ge(SM["pe4"], i - 1)
            ve.tensor_tensor(aR[i % 2][:], lnE[:],
                             outacc[:, i * 1024:(i + 1) * 1024], ALU.add)
            ve.drain()
            b2 = bnE2[i % 2]
            ve.bn_stats(b2[:, 0:6], aR[i % 2][:, 0:512])
            ve.bn_stats(b2[:, 6:12], aR[i % 2][:, 512:1024])
            ve.drain()
            ve.bn_aggr(b2[:, 12:14], b2[:, 0:12])
            ve.drain()
            ve.tensor_scalar(b2[:, 14:15], b2[:, 13:14], float(LN_EPS), None,
                             ALU.add).then_inc(SM["pe1"], 1)
            ve.wait_ge(SM["ps1"], ps1base + 2 * i + 2)
            ve.reciprocal(b2[:, 14:15], b2[:, 15:16])
            ve.drain()
            ve.tensor_scalar(lnE[:], aR[i % 2][:], b2[:, 12:13],
                             b2[:, 14:15], ALU.subtract, ALU.mult)
            ve.drain()
            if i >= 2:
                ve.wait_ge(SM["est0" if i % 2 == 0 else "est1"], 16 * (i // 2))
            ve.tensor_tensor(x3t[i % 2][:], lnE[:], xe[i % 2][:], ALU.add
                             ).then_inc(SM["pe4"], 1)
        ve.drain()
    p.bump("pe1", 64)
    p.bump("ps1", 64)
    p.bump("pe4", 32)
    p.bump("xe0", 16 * 16)
    p.bump("xe1", 16 * 16)
    p.bump("est0", 16 * 16)
    p.bump("est1", 16 * 16)

    ret = emit_join(p, blk, waits=[
        ("scalar", [("est0", p.cnt("est0")), ("est1", p.cnt("est1"))]),
        ("vector", [("pe4", p.cnt("pe4"))])])
    scE.close()
    sc3.close()
    return ret


def p3_host_meta(slot):
    m = np.arange(128)[:, None]
    n = np.arange(256)[None, :]
    A = (m <= n).astype(np.float32)
    Bm = (m <= n - 128).astype(np.float32)
    one = np.ones((128, 256), np.float32)
    zero = np.zeros((128, 256), np.float32)
    if slot == 0:
        tab = np.concatenate([A, Bm, zero, zero], 1)
    else:
        tab = np.concatenate([one, one, A, Bm], 1)
    return tab.astype(bf16)


def p3_wk_layout(W):
    H = W.shape[0]
    blocks = []
    for h in range(H):
        WT = W[h].T
        for dc in range(8):
            blocks.append(WT[dc * 128:(dc + 1) * 128, :])
    return np.ascontiguousarray(np.concatenate(blocks, 1)).astype(bf16)


def p3_wv_layout(W):
    H = W.shape[0]
    blocks = []
    for h in range(H):
        WT = W[h].T
        for dc in range(8):
            blocks.append(WT[dc * 128:(dc + 1) * 128, :])
    return np.ascontiguousarray(np.concatenate(blocks, 1)).astype(bf16)


def p3_own_rows(slot):
    return np.concatenate([np.arange(512 * t + 256 * slot, 512 * t + 256 * slot + 256)
                           for t in range(16)])


"""Phase 4: L1_ffn sandwich stage.
x4 = x3 + LN( ffn(LN(x3)) ), ffn = 3-layer {0,1}-conn-dense net.
Per core: 4096 own token rows (local order), weights replicated.
"""

TW = 512
NTT = OWN // TW
NTILE = OWN // 128


def emit_p4(p, io, JN=None):
    nc = p.nc
    blk = p.block
    sc4 = ExitStack()

    ln4s = [p.sbs(sc4, f"p4_ln4s{i}", [128, DCH * 128], BF16) for i in range(2)]
    ln4tt = [p.sbs(sc4, f"p4_ln4tt{i}", [128, DCH * TW], BF16) for i in range(2)]
    x3t = [p.sbs(sc4, f"p4_x3t{i}", [128, D], F32) for i in range(2)]
    lnt = [p.sbs(sc4, f"p4_lnt{i}", [128, D], BF16) for i in range(2)]
    bnsc = [p.sbs(sc4, f"p4_bn{i}", [128, 16], F32) for i in range(2)]
    k0t = [p.sbs(sc4, f"p4_k0t{i}", [128, 8 * 256], U8) for i in range(2)]
    k0b = [p.sbs(sc4, f"p4_k0b{i}", [128, 8 * 256], U8) for i in range(2)]
    k0d = [p.sbs(sc4, f"p4_k0d{i}", [128, 8 * 256], BF16) for i in range(2)]
    k1b = [p.sbs(sc4, f"p4_k1b{i}", [128, 32 * 256], BF16) for i in range(2)]
    h1 = p.sbs(sc4, "p4_h1", [128, 32 * TW], BF16)
    h2 = p.sbs(sc4, "p4_h2", [128, 32 * TW], BF16)
    y = p.sbs(sc4, "p4_y", [128, 8 * TW], F32)
    ysq = p.sbs(sc4, "p4_ysq", [128, 8 * TW], BF16)
    rs_f = p.sbs(sc4, "p4_rsf", [1, 3 * TW], F32)
    rs1h = p.sbs(sc4, "p4_rs1h", [1, 2 * TW], BF16)
    rs2h = p.sbs(sc4, "p4_rs2h", [1, 2 * TW], BF16)
    onescF = p.sbs(sc4, "p4_onescF", [128, 1], F32)
    ones1 = p.sbs(sc4, "p4_ones1", [1, 128], BF16)
    onescB = p.sbs(sc4, "p4_onescB", [128, 1], BF16)
    murow = p.sbs(sc4, "p4_murow", [1, TW], F32)
    varow = p.sbs(sc4, "p4_varow", [1, 3 * TW], F32)
    mucol = p.sbs(sc4, "p4_mucol", [128, 4], F32)
    rstcol = p.sbs(sc4, "p4_rstcol", [128, 4], F32)
    ytm = [p.sbs(sc4, "p4_ytm0", [128, D], F32)] * 2
    x3e = [p.sbs(sc4, f"p4_x3e{i}", [128, D], F32) for i in range(2)]
    x4t = [p.sbs(sc4, f"p4_x4t{i}", [128, D], F32) for i in range(2)]
    sv4 = p.sbs(sc4, "p4_sv4", [128, 3], F32)
    identb = p.sbs(sc4, "p4_identb", [128, 128], BF16)
    identf = p.sbs(sc4, "p4_identf", [128, 128], F32)

    psTb = [p.pss(sc4, f"p4_psTb{i}", [128, 1024], BF16) for i in range(2)]
    psA = [p.pss(sc4, f"p4_psA{i}", [128, TW]) for i in range(4)]
    psR = p.pss(sc4, "p4_psR", [128, TW])

    SM = p._sems

    def kidx(tt, i):
        return tt * 20 + i

    ln4h = io["ln4h"]


    @blk.sync
    def _(sync):
        if JN is not None:
            sync.wait_ge(SM[JN[0]], JN[1])
        def store(gq):
            tt, q = gq // 4, gq % 4
            sync.wait_ge(SM["f4x4v"], gq + 1)
            sync.dma_start(
                out=io["x4"][tt * TW + q * 128: tt * TW + (q + 1) * 128, :],
                in_=x4t[gq % 2][:],
            ).then_inc(SM["f4stA" if gq % 2 == 0 else "f4stB"], 16)

        nst = 0
        for tt in range(NTT):
            if tt >= 1:
                for q in range(4):
                    store((tt - 1) * 4 + q)
                    nst += 1
            sync.wait_ge(SM["f4lnsA"], 16 * (NTILE // 2))
            sync.wait_ge(SM["f4lnsB"], 16 * (NTILE // 2))
            if tt >= 2:
                sync.wait_ge(SM["f4p1"], tt * 16 - 16)
            sync.dma_start(
                out=ln4tt[tt % 2].ap().rearrange("p (j t) -> p j t", t=TW),
                in_=ln4h.rearrange("p (j t) -> p j t", t=OWN)[:, :, tt * TW:(tt + 1) * TW],
            ).then_inc(SM["f4lktA" if tt % 2 == 0 else "f4lktB"], 16)
            for i in range(16):
                g = tt * 16 + i
                if g >= 2:
                    sync.wait_ge(SM["f4p1"], g - 1)
                c0 = i * 256
                sync.dma_start(
                    out=k0t[g % 2].ap().rearrange("p (j c) -> p j c", c=256),
                    in_=io["k0"][0:1024, c0:c0 + 256].rearrange("(j p) c -> p j c", p=128),
                ).then_inc(SM["f4lk0A" if g % 2 == 0 else "f4lk0B"], 16)
                sync.dma_start(
                    out=k0b[g % 2].ap().rearrange("p (j c) -> p j c", c=256),
                    in_=io["k0"][1024:2048, c0:c0 + 256].rearrange("(j p) c -> p j c", p=128),
                ).then_inc(SM["f4lk0A" if g % 2 == 0 else "f4lk0B"], 16)
        for q in range(4):
            store((NTT - 1) * 4 + q)
            nst += 1
        sync.wait_ge(SM["f4stA"], 16 * (nst - nst // 2))
        sync.wait_ge(SM["f4stB"], 16 * (nst // 2))

    @blk.scalar
    def _(sc):
        if JN is not None:
            sc.wait_ge(SM[JN[0]], JN[1])
        sc.dma_start(out=identb[:], in_=io["identb"][:, :]).then_inc(SM["f4ini"], 16)
        sc.dma_start(out=identf[:], in_=io["identf"][:, :]).then_inc(SM["f4ini"], 16)
        sc.dma_start(out=sv4[:, 0:1], in_=io["s0v"][:, :]).then_inc(SM["f4ini"], 16)
        sc.dma_start(out=sv4[:, 1:2], in_=io["s1v"][:, :]).then_inc(SM["f4ini"], 16)
        sc.dma_start(out=sv4[:, 2:3], in_=io["s2v"][:, :]).then_inc(SM["f4ini"], 16)
        for i in range(NTILE):
            if i >= 2:
                sc.wait_ge(SM["f4pv2"], i - 1)
            sc.dma_start(out=x3t[i % 2][:],
                         in_=io["x3"][i * 128:(i + 1) * 128, :]
                         ).then_inc(SM["f4xtA" if i % 2 == 0 else "f4xtB"], 16)
            sc.wait_ge(SM["f4pv1"], i + 1)
            sc.activation(out=bnsc[i % 2][:, 15:16], in_=bnsc[i % 2][:, 14:15],
                          func=AF.Sqrt).then_inc(SM["f4ps1"], 1)
            sc.wait_ge(SM["f4ple"], i + 1)
            sc.dma_start(
                out=ln4h.rearrange("p (j t) -> p j t", t=OWN)[:, :, i * 128:(i + 1) * 128],
                in_=ln4s[i % 2].ap().rearrange("p (j c) -> p j c", c=128),
            ).then_inc(SM["f4lnsA" if i % 2 == 0 else "f4lnsB"], 16)
        sc.wait_ge(SM["f4svr"], 1)
        for tt in range(NTT):
            for i in range(16):
                g = tt * 16 + i
                sc.wait_ge(SM["f4p1"], g + 1)
                if tt >= 1 and i == 0:
                    sc.wait_ge(SM["f4p2"], tt * 16)
                pa = psA[0] if i % 2 == 0 else psA[2]
                pb = psA[1] if i % 2 == 0 else psA[3]
                sc.activation(out=h1[:, (2 * i) * TW:(2 * i + 1) * TW], in_=pa[:],
                              func=AF.Relu, scale=sv4[:, 0:1])
                sc.activation(out=h1[:, (2 * i + 1) * TW:(2 * i + 2) * TW], in_=pb[:],
                              func=AF.Relu, scale=sv4[:, 0:1]).then_inc(SM["f4ev1"], 1)
            for i in range(16):
                g = tt * 16 + i
                sc.wait_ge(SM["f4p2"], g + 1)
                if tt >= 1 and i == 0:
                    sc.wait_ge(SM["f4p3"], tt * 4)
                pa = psA[0] if i % 2 == 0 else psA[2]
                pb = psA[1] if i % 2 == 0 else psA[3]
                sc.activation(out=h2[:, (2 * i) * TW:(2 * i + 1) * TW], in_=pa[:],
                              func=AF.Relu, scale=sv4[:, 1:2])
                sc.activation(out=h2[:, (2 * i + 1) * TW:(2 * i + 2) * TW], in_=pb[:],
                              func=AF.Relu, scale=sv4[:, 1:2]).then_inc(SM["f4ev2"], 1)
            sc.wait_ge(SM["f4ev3"], tt * 4 + 4)
            sc.activation(out=ysq[:], in_=y[:], func=AF.Square).then_inc(SM["f4sq8"], 1)
            sc.wait_ge(SM["f4vv"], tt + 1)
            sc.activation(out=varow[0:1, TW:2 * TW], in_=varow[0:1, 0:TW],
                          func=AF.Sqrt).then_inc(SM["f4sq2"], 1)
            for q in range(4):
                gq = tt * 4 + q
                if gq >= 2:
                    sc.wait_ge(SM["f4x4v"], gq - 1)
                sc.dma_start(out=x3e[gq % 2][:],
                             in_=io["x3"][tt * TW + q * 128: tt * TW + (q + 1) * 128, :]
                             ).then_inc(SM["f4xeA" if gq % 2 == 0 else "f4xeB"], 16)

    @blk.gpsimd
    def _(gp):
        if JN is not None:
            gp.wait_ge(SM[JN[0]], JN[1])
        gp.memset(onescF[:], 1.0)
        gp.memset(ones1[:], 1.0)
        gp.memset(onescB[:], 1.0).then_inc(SM["f4ini"], 1)
        for tt in range(NTT):
            for i in range(16):
                g = tt * 16 + i
                gp.wait_ge(SM["f4lk0A" if g % 2 == 0 else "f4lk0B"], 32 * (g // 2 + 1))
                if g >= 2:
                    gp.wait_ge(SM["f4p1"], g - 1)
                gp.tensor_tensor(k0d[g % 2][:], k0t[g % 2][:], k0b[g % 2][:],
                                 ALU.subtract).then_inc(SM["f4kd"], 1)
            for i in range(20):
                g = kidx(tt, i)
                if g >= 2:
                    gp2 = g - 2
                    t2, i2 = gp2 // 20, gp2 % 20
                    if i2 < 16:
                        gp.wait_ge(SM["f4p2"], t2 * 16 + i2 + 1)
                    else:
                        gp.wait_ge(SM["f4p3"], t2 * 4 + (i2 - 16) + 1)
                if i < 16:
                    c0 = i * 256
                    src = io["k1"][:, c0:c0 + 256].rearrange("(j p) c -> p j c", p=128)
                else:
                    c0 = (i - 16) * 256
                    src = io["k2"][:, c0:c0 + 256].rearrange("(j p) c -> p j c", p=128)
                gp.dma_start(out=k1b[g % 2].ap().rearrange("p (j c) -> p j c", c=256),
                             in_=src).then_inc(SM["f4lk1A" if g % 2 == 0 else "f4lk1B"], 16)

    @blk.tensor
    def _(te):
        te.wait_ge(SM["f4ini"], 5 * 16 + 1)
        for i in range(NTILE):
            te.wait_ge(SM["f4pv2"], i + 1)
            if i >= 2:
                te.wait_ge(SM["f4ple"], i - 1)
            for j in range(DCH):
                inst = te.transpose(psTb[i % 2][:, j * 128:(j + 1) * 128],
                                    lnt[i % 2][:, j * 128:(j + 1) * 128], identb[:])
            inst.then_inc(SM["f4ptl"], 1)
        for tt in range(NTT):
            for i in range(16):
                g = tt * 16 + i
                te.wait_ge(SM["f4kd"], g + 1)
                if g >= 2:
                    te.wait_ge(SM["f4ev1"], g - 1)
                pa = psA[0] if i % 2 == 0 else psA[2]
                pb = psA[1] if i % 2 == 0 else psA[3]
                if i == 0:
                    te.wait_ge(SM["f4lktA" if tt % 2 == 0 else "f4lktB"], 16 * (tt // 2 + 1))
                for fi, ps_ in ((0, pa), (1, pb)):
                    for dj in range(DCH):
                        inst = te.matmul(
                            ps_[:],
                            k0d[g % 2][:, dj * 256 + fi * 128: dj * 256 + (fi + 1) * 128],
                            ln4tt[tt % 2][:, dj * TW: dj * TW + TW],
                            start=(dj == 0), stop=(dj == DCH - 1))
                inst.then_inc(SM["f4p1"], 1)
            te.wait_ge(SM["f4ev1"], tt * 16 + 16)
            if tt >= 1:
                te.wait_ge(SM["f4vv"], tt)
            for f in range(32):
                inst = te.matmul(psR[0:1, :], onescB[:, 0:1],
                                 h1[:, f * TW:(f + 1) * TW],
                                 start=(f == 0), stop=(f == 31))
            inst.then_inc(SM["f4pr1"], 1)
            for i in range(16):
                g = tt * 16 + i
                gk = kidx(tt, i)
                te.wait_ge(SM["f4lk1A" if gk % 2 == 0 else "f4lk1B"], 16 * (gk // 2 + 1))
                te.wait_ge(SM["f4rs1"], tt + 1)
                if g >= 2:
                    te.wait_ge(SM["f4ev2"], g - 1)
                pa = psA[0] if i % 2 == 0 else psA[2]
                pb = psA[1] if i % 2 == 0 else psA[3]
                for fi, ps_ in ((0, pa), (1, pb)):
                    for f in range(32):
                        te.matmul(
                            ps_[:],
                            k1b[gk % 2][:, f * 256 + fi * 128: f * 256 + (fi + 1) * 128],
                            h1[:, f * TW:(f + 1) * TW],
                            start=(f == 0), stop=False)
                    te.matmul(ps_[:], ones1[0:1, :], rs1h[0:1, 0:TW], start=False, stop=False)
                    inst = te.matmul(ps_[:], ones1[0:1, :], rs1h[0:1, TW:2 * TW], start=False, stop=True)
                inst.then_inc(SM["f4p2"], 1)
            te.wait_ge(SM["f4ev2"], tt * 16 + 16)
            te.wait_ge(SM["f4rs1"], tt + 1)
            for f in range(32):
                inst = te.matmul(psR[0:1, :], onescB[:, 0:1],
                                 h2[:, f * TW:(f + 1) * TW],
                                 start=(f == 0), stop=(f == 31))
            inst.then_inc(SM["f4pr2"], 1)
            for i in range(4):
                g = tt * 4 + i
                gk = kidx(tt, 16 + i)
                te.wait_ge(SM["f4lk1A" if gk % 2 == 0 else "f4lk1B"], 16 * (gk // 2 + 1))
                te.wait_ge(SM["f4rs2"], tt + 1)
                if g >= 2:
                    te.wait_ge(SM["f4ev3"], g - 1)
                pa = psA[0] if i % 2 == 0 else psA[2]
                pb = psA[1] if i % 2 == 0 else psA[3]
                for fi, ps_ in ((0, pa), (1, pb)):
                    for f in range(32):
                        te.matmul(
                            ps_[:],
                            k1b[gk % 2][:, f * 256 + fi * 128: f * 256 + (fi + 1) * 128],
                            h2[:, f * TW:(f + 1) * TW],
                            start=(f == 0), stop=False)
                    te.matmul(ps_[:], ones1[0:1, :], rs2h[0:1, 0:TW], start=False, stop=False)
                    inst = te.matmul(ps_[:], ones1[0:1, :], rs2h[0:1, TW:2 * TW], start=False, stop=True)
                inst.then_inc(SM["f4p3"], 1)
            te.wait_ge(SM["f4ev3"], tt * 4 + 4)
            te.wait_ge(SM["f4rs2"], tt + 1)
            for dj in range(DCH):
                inst = te.matmul(psR[0:1, :], onescF[:, 0:1], y[:, dj * TW:(dj + 1) * TW],
                                 start=(dj == 0), stop=(dj == DCH - 1))
            inst.then_inc(SM["f4pstA"], 1)
            te.wait_ge(SM["f4mvA"], tt + 1)
            te.wait_ge(SM["f4sq8"], tt + 1)
            for dj in range(DCH):
                inst = te.matmul(psR[0:1, :], onescB[:, 0:1], ysq[:, dj * TW:(dj + 1) * TW],
                                 start=(dj == 0), stop=(dj == DCH - 1))
            inst.then_inc(SM["f4pstB"], 1)
            te.wait_ge(SM["f4mvr"], tt + 1)
            for k in range(40):
                b = psA[k % 4]
                if k >= 4:
                    te.wait_ge(SM["f4eev"], 40 * tt + k - 3)
                if k < 8:
                    q = k // 2
                    if k % 2 == 0:
                        src = murow[0:1, q * 128:(q + 1) * 128]
                    else:
                        src = varow[0:1, 2 * TW + q * 128: 2 * TW + (q + 1) * 128]
                    te.transpose(b[:, 0:1], src, identf[0:1, 0:1]).then_inc(SM["f4eet"], 1)
                else:
                    q, dj = (k - 8) // 8, (k - 8) % 8
                    te.transpose(b[:, 0:128],
                                 y[:, dj * TW + q * 128: dj * TW + (q + 1) * 128],
                                 identf[:]).then_inc(SM["f4eet"], 1)

    @blk.vector
    def _(ve):
        ve.wait_ge(SM["f4ini"], 5 * 16 + 1)
        ve.tensor_scalar(sv4[:, 0:1], sv4[:, 0:1], 4.0, None, ALU.mult)
        ve.tensor_scalar(sv4[:, 1:2], sv4[:, 1:2], 4.0, None, ALU.mult)
        ve.tensor_scalar(sv4[:, 2:3], sv4[:, 2:3], 2.0, None, ALU.mult).then_inc(SM["f4svr"], 1)
        for i in range(NTILE):
            ve.wait_ge(SM["f4xtA" if i % 2 == 0 else "f4xtB"], 16 * (i // 2 + 1))
            sc_ = bnsc[i % 2]
            ve.bn_stats(sc_[:, 0:6], x3t[i % 2][:, 0:512])
            ve.bn_stats(sc_[:, 6:12], x3t[i % 2][:, 512:1024])
            ve.drain()
            ve.bn_aggr(sc_[:, 12:14], sc_[:, 0:12])
            ve.drain()
            ve.tensor_scalar(sc_[:, 14:15], sc_[:, 13:14], float(LN_EPS), None,
                             ALU.add).then_inc(SM["f4pv1"], 1)
            ve.wait_ge(SM["f4ps1"], i + 1)
            ve.reciprocal(sc_[:, 14:15], sc_[:, 15:16])
            ve.drain()
            if i >= 2:
                ve.wait_ge(SM["f4ptl"], i - 1)
            ve.tensor_scalar(lnt[i % 2][:], x3t[i % 2][:], sc_[:, 12:13], sc_[:, 14:15],
                             ALU.subtract, ALU.mult).then_inc(SM["f4pv2"], 1)
            ve.wait_ge(SM["f4ptl"], i + 1)
            if i >= 2:
                ve.wait_ge(SM["f4lnsA" if i % 2 == 0 else "f4lnsB"], 16 * (i // 2))
            ve.tensor_copy(ln4s[i % 2][:], psTb[i % 2][:, 0:DCH * 128]).then_inc(SM["f4ple"], 1)
        for tt in range(NTT):
            ve.wait_ge(SM["f4pr1"], tt + 1)
            ve.tensor_scalar(rs_f[0:1, 0:TW], psR[0:1, :], -0.5, None, ALU.mult)
            ve.drain()
            ve.tensor_copy(rs1h[0:1, 0:TW], rs_f[0:1, 0:TW])
            ve.drain()
            ve.tensor_tensor(rs_f[0:1, TW:2 * TW], rs_f[0:1, 0:TW], rs1h[0:1, 0:TW],
                             ALU.subtract)
            ve.drain()
            ve.tensor_copy(rs1h[0:1, TW:2 * TW], rs_f[0:1, TW:2 * TW]).then_inc(SM["f4rs1"], 1)
            ve.wait_ge(SM["f4pr2"], tt + 1)
            ve.tensor_scalar(rs_f[0:1, 0:TW], psR[0:1, :], -0.5, None, ALU.mult)
            ve.drain()
            ve.tensor_copy(rs2h[0:1, 0:TW], rs_f[0:1, 0:TW])
            ve.drain()
            ve.tensor_tensor(rs_f[0:1, TW:2 * TW], rs_f[0:1, 0:TW], rs2h[0:1, 0:TW],
                             ALU.subtract)
            ve.drain()
            ve.tensor_copy(rs2h[0:1, TW:2 * TW], rs_f[0:1, TW:2 * TW]).then_inc(SM["f4rs2"], 1)
            for i in range(4):
                ve.wait_ge(SM["f4p3"], tt * 4 + i + 1)
                if tt >= 1 and i == 0:
                    ve.wait_ge(SM["f4eet"], 40 * tt)
                pa = psA[0] if i % 2 == 0 else psA[2]
                pb = psA[1] if i % 2 == 0 else psA[3]
                ve.tensor_scalar(y[:, (2 * i) * TW:(2 * i + 1) * TW], pa[:],
                                 sv4[:, 2:3], None, ALU.mult)
                ve.tensor_scalar(y[:, (2 * i + 1) * TW:(2 * i + 2) * TW], pb[:],
                                 sv4[:, 2:3], None, ALU.mult).then_inc(SM["f4ev3"], 1)
            ve.wait_ge(SM["f4pstA"], tt + 1)
            ve.tensor_scalar(murow[0:1, :], psR[0:1, :], 1.0 / D, None,
                             ALU.mult).then_inc(SM["f4mvA"], 1)
            ve.wait_ge(SM["f4pstB"], tt + 1)
            ve.tensor_scalar(rs_f[0:1, 2 * TW:3 * TW], psR[0:1, :], 1.0 / D, None,
                             ALU.mult)
            ve.drain()
            ve.tensor_tensor(varow[0:1, 0:TW], murow[0:1, :], murow[0:1, :], ALU.mult)
            ve.drain()
            ve.tensor_tensor(varow[0:1, 0:TW], rs_f[0:1, 2 * TW:3 * TW],
                             varow[0:1, 0:TW], ALU.subtract)
            ve.drain()
            ve.tensor_scalar(varow[0:1, 0:TW], varow[0:1, 0:TW], float(LN_EPS), None,
                             ALU.add).then_inc(SM["f4vv"], 1)
            ve.wait_ge(SM["f4sq2"], tt + 1)
            ve.reciprocal(varow[0:1, 2 * TW:3 * TW], varow[0:1, TW:2 * TW]).then_inc(SM["f4mvr"], 1)
            for k in range(40):
                ve.wait_ge(SM["f4eet"], 40 * tt + k + 1)
                b = psA[k % 4]
                if k < 8:
                    q = k // 2
                    dst = mucol if k % 2 == 0 else rstcol
                    ve.tensor_copy(dst[:, q:q + 1], b[:, 0:1]).then_inc(SM["f4eev"], 1)
                    if k == 7:
                        ve.drain()
                else:
                    q, dj = (k - 8) // 8, (k - 8) % 8
                    gq = tt * 4 + q
                    ve.tensor_scalar(ytm[gq % 2][:, dj * 128:(dj + 1) * 128],
                                     b[:, 0:128], mucol[:, q:q + 1], rstcol[:, q:q + 1],
                                     ALU.subtract, ALU.mult).then_inc(SM["f4eev"], 1)
                if k >= 8 and (k - 8) % 8 == 7:
                    q = (k - 8) // 8
                    gq = tt * 4 + q
                    ve.drain()
                    ve.wait_ge(SM["f4xeA" if gq % 2 == 0 else "f4xeB"], 16 * (gq // 2 + 1))
                    if gq >= 2:
                        ve.wait_ge(SM["f4stA" if gq % 2 == 0 else "f4stB"], 16 * (gq // 2))
                    ve.tensor_tensor(x4t[gq % 2][:], ytm[gq % 2][:], x3e[gq % 2][:],
                                     ALU.add).then_inc(SM["f4x4v"], 1)
                    ve.drain()

    ret = emit_join(p, blk, waits=[
        ("sync", [("f4stA", p.cnt("f4stA")), ("f4stB", p.cnt("f4stB"))])])
    sc4.close()
    return ret




def _own_idx(slot):
    return np.concatenate(
        [np.arange(512 * t + 256 * slot, 512 * t + 256 * slot + 256)
         for t in range(16)])


def _ownrows_tab(slot):
    r = _own_idx(slot).astype(np.int32)
    return np.ascontiguousarray(r.reshape(32, 128).T)


def kernel(_dump=None, **inputs):
    cc = np.ascontiguousarray
    f32 = np.float32
    x0 = np.asarray(inputs["L0_attn_x"], f32)
    identb_np = np.eye(128, dtype=bf16)
    identf_np = np.eye(128, dtype=f32)

    p = Prog()
    io = {}
    io["xs1"] = p.din("xs1", [2 * S, D], F32)
    for nm in ("Wk1", "Wq1", "Wk2", "Wproj"):
        io[nm] = p.din(nm, [D, D], F32)
    for nm in ("bk1c", "bq1c", "bk2c", "bprojc"):
        io[nm] = p.din(nm, [128, 8], F32)
    io["centTc"] = p.din("centTc", [128, 64], F32)
    io["gB"] = p.din("gB", [128, D], F32)
    io["bB"] = p.din("bB", [128, D], F32)
    io["identb"] = p.din("identb", [128, 128], BF16)
    io["identf"] = p.din("identf", [128, 128], F32)
    io["ln1h"] = p.dint("ln1h", [128, DCH * S], BF16)
    io["oh"] = p.dint("oh", [128, DCH * S], BF16)
    io["x1out"] = p.dout("x1out", [S, D], F32)
    io["normout"] = p.dout("normout", [S, D], BF16)
    io["scout"] = p.dout("scout", [128, 1024], F32)
    p.start_block()
    emit_p1(p, io)
    p.es.close()

    cent = np.asarray(inputs["L0_ffn_centroids"], f32)
    shared1 = {
        "Wk1": np.asarray(inputs["L0_attn_Wk1"], f32),
        "Wq1": np.asarray(inputs["L0_attn_Wq1"], f32),
        "Wk2": np.asarray(inputs["L0_attn_Wk2"], f32),
        "Wproj": np.asarray(inputs["L0_attn_Wproj"], f32),
        "bk1c": cc(np.asarray(inputs["L0_attn_bk1"], f32).reshape(8, 128).T),
        "bq1c": cc(np.asarray(inputs["L0_attn_bq1"], f32).reshape(8, 128).T),
        "bk2c": cc(np.asarray(inputs["L0_attn_bk2"], f32).reshape(8, 128).T),
        "bprojc": cc(np.asarray(inputs["L0_attn_bproj"], f32).reshape(8, 128).T),
        "centTc": cc(cent.T.reshape(8, 128, 8).transpose(1, 0, 2).reshape(128, 64)),
        "gB": cc(np.broadcast_to(np.asarray(inputs["L0_ffn_ln_g"], f32), (128, D))),
        "bB": cc(np.broadcast_to(np.asarray(inputs["L0_ffn_ln_b"], f32), (128, D))),
        "identb": identb_np, "identf": identf_np,
    }
    in_maps = []
    for c in range(NCORES):
        pair, slot = c // 2, c % 2
        bA, bB_ = ((pair, (pair + 1) % 4) if slot == 0 else
                   ((pair + 2) % 4, (pair + 3) % 4))
        im = dict(shared1)
        im["xs1"] = cc(np.concatenate([x0[bA], x0[bB_]], 0))
        in_maps.append(im)
    res1 = run_bass_kernel_spmd(p.nc, in_maps, core_ids=list(range(NCORES)))
    x1b, normb, scoresb = {}, {}, {}
    for b in range(4):
        r = res1.results[2 * b]
        x1b[b] = np.asarray(r["x1out"], f32)
        normb[b] = np.asarray(r["normout"])
        scoresb[b] = np.asarray(r["scout"], f32)[0:64].reshape(S, 8)

    nexp = 4
    p = Prog()
    io = {}
    io["scoresF"] = p.din("scoresF", [256, 1024], F32)
    io["normH"] = p.din("normH", [S + 128, D], BF16)
    io["x1own"] = p.din("x1own", [OWN, D], F32)
    io["w1my"] = p.din("w1my", [nexp * D, 4096], F32)
    io["w2my"] = p.din("w2my", [nexp * 4096, D], F32)
    io["b1c"] = p.din("b1c", [128, 32 * nexp], F32)
    io["b2c"] = p.din("b2c", [128, 8 * nexp], F32)
    io["gidx"] = p.din("gidx", [128, (CAP // 128) * nexp], I32)
    io["ownrows"] = p.din("ownrows", [128, 32], I32)
    io["tokid"] = p.din("tokid", [128, 64], I32)
    io["identb"] = p.din("identb", [128, 128], BF16)
    io["identf"] = p.din("identf", [128, 128], F32)
    io["S_h"] = p.dout("S_h", [8 * CAP + 128, 1], I32)
    io["yH"] = p.dout("yH", [S + 128, D], BF16)
    io["x2out"] = p.dout("x2out", [OWN, D], F32)
    p.start_block()
    emit_p2(p, io, nexp=nexp)
    p.es.close()

    w1 = np.asarray(inputs["L0_ffn_w1"], f32)
    w2 = np.asarray(inputs["L0_ffn_w2"], f32)
    b1 = np.asarray(inputs["L0_ffn_b1"], f32)
    b2 = np.asarray(inputs["L0_ffn_b2"], f32)
    tokid_np = (np.arange(64)[None, :] * 128
                + np.arange(128)[:, None]).astype(np.int32)
    in_maps = []
    for c in range(NCORES):
        pair, slot = c // 2, c % 2
        rot = [(pair + k) % 4 for k in range(4)]
        scoresL = np.concatenate([scoresb[b] for b in rot], 0)
        ee = range(4 * slot, 4 * slot + 4)
        im = {
            "scoresF": cc(scoresL.reshape(256, 1024)),
            "normH": cc(np.concatenate(
                [normb[pair], np.zeros((128, D), bf16)], 0)),
            "x1own": cc(x1b[pair][_own_idx(slot)]),
            "w1my": cc(w1[4 * slot:4 * slot + 4].reshape(nexp * D, 4096)),
            "w2my": cc(w2[4 * slot:4 * slot + 4].reshape(nexp * 4096, D)),
            "b1c": cc(np.concatenate([b1[j].reshape(32, 128).T for j in ee], 1)),
            "b2c": cc(np.concatenate([b2[j].reshape(8, 128).T for j in ee], 1)),
            "gidx": cc((slot * 4 * CAP
                        + np.arange(4 * CAP, dtype=np.int32)).reshape(-1, 128).T),
            "ownrows": _ownrows_tab(slot),
            "tokid": tokid_np,
            "identb": identb_np, "identf": identf_np,
        }
        in_maps.append(im)
    res2 = run_bass_kernel_spmd(p.nc, in_maps, core_ids=list(range(NCORES)))

    if _dump is not None:
        _dump["x1b"] = x1b; _dump["normb"] = normb; _dump["scoresb"] = scoresb
        _dump["S_h"] = {pr: np.asarray(res2.results[2 * pr]["S_h"]).reshape(-1)
                        for pr in range(4)}
    yHm = {}
    for pair in range(4):
        S0 = np.asarray(res2.results[2 * pair]["S_h"]).reshape(-1)
        y0 = np.asarray(res2.results[2 * pair]["yH"])
        y1 = np.asarray(res2.results[2 * pair + 1]["yH"])
        ym = np.zeros((S + 128, D), bf16)
        for e in range(8):
            tk = S0[e * CAP:(e + 1) * CAP]
            tk = tk[(tk >= 0) & (tk < S)]
            ym[tk] = (y0 if e < 4 else y1)[tk]
        yHm[pair] = ym

    p = Prog()
    io = {}
    io["ownrows"] = p.din("ownrows", [128, 32], I32)
    io["x1own"] = p.din("x1own", [OWN, D], F32)
    io["yH"] = p.din("yH", [S + 128, D], BF16)
    io["x2out"] = p.dout("x2out", [OWN, D], F32)
    p.start_block()
    emit_p2_owner(p, io)
    p.es.close()
    in_maps = []
    for c in range(NCORES):
        pair, slot = c // 2, c % 2
        in_maps.append({
            "ownrows": _ownrows_tab(slot),
            "x1own": cc(x1b[pair][_own_idx(slot)]),
            "yH": yHm[pair],
        })
    res3 = run_bass_kernel_spmd(p.nc, in_maps, core_ids=list(range(NCORES)))

    if _dump is not None:
        _dump["x2"] = {c: np.asarray(res3.results[c]["x2out"], f32)
                       for c in range(NCORES)}
        _dump["yHm"] = yHm
    p = Prog()
    io = {}
    io["x2own"] = p.din("x2own", [4096, D], F32)
    io["x2full"] = p.din("x2full", [S, D], F32)
    io["WkTh"] = p.din("WkTh", [128, 8 * 8 * 128], BF16)
    io["WqTh"] = p.din("WqTh", [128, 8 * 8 * 128], BF16)
    io["WvTh"] = p.din("WvTh", [128, 8 * 8 * 1024], BF16)
    io["maskTab"] = p.din("maskTab", [128, 4 * 256], BF16)
    io["identb"] = p.din("identb", [128, 128], BF16)
    io["ln3fH"] = p.dint("ln3fH", [128, DCH * S], BF16)
    io["ln3oH"] = p.dint("ln3oH", [128, DCH * 4096], BF16)
    io["x3own"] = p.dout("x3own", [4096, D], F32)
    io4 = {}
    io4["x3"] = io["x3own"]
    io4["k0"] = p.din("k0", [2 * D, 4096], U8)
    io4["k1"] = p.din("k1", [4096, 4096], U8)
    io4["k2"] = p.din("k2", [4096, D], U8)
    for nm in ("s0v", "s1v", "s2v"):
        io4[nm] = p.din(nm, [128, 1], F32)
    io4["identb"] = io["identb"]
    io4["identf"] = p.din("identf", [128, 128], F32)
    io4["x4"] = p.dout("x4", [OWN, D], F32)
    io4["ln4h"] = p.dint("ln4h", [128, DCH * OWN], BF16)
    p.start_block()
    JN3 = emit_p3(p, io)
    emit_p4(p, io4, JN=JN3)
    p.es.close()

    WkTh = p3_wk_layout(np.asarray(inputs["L1_attn_Wk"], f32))
    WqTh = p3_wk_layout(np.asarray(inputs["L1_attn_Wq"], f32))
    WvTh = p3_wv_layout(np.asarray(inputs["L1_attn_Wv"], f32))
    x2full = {}
    for pair in range(4):
        xf = np.empty((S, D), f32)
        xf[_own_idx(0)] = np.asarray(res3.results[2 * pair]["x2out"], f32)
        xf[_own_idx(1)] = np.asarray(res3.results[2 * pair + 1]["x2out"], f32)
        x2full[pair] = xf
    base5 = {
        "k0": np.asarray(inputs["L1_ffn_k0"]).astype(np.uint8),
        "k1": np.asarray(inputs["L1_ffn_k1"]).astype(np.uint8),
        "k2": np.asarray(inputs["L1_ffn_k2"]).astype(np.uint8),
        "s0v": np.full((128, 1), np.asarray(inputs["L1_ffn_s0"], f32), f32),
        "s1v": np.full((128, 1), np.asarray(inputs["L1_ffn_s1"], f32), f32),
        "s2v": np.full((128, 1), np.asarray(inputs["L1_ffn_s2"], f32), f32),
        "identf": identf_np,
    }
    in_maps = []
    for c in range(NCORES):
        pair, slot = c // 2, c % 2
        in_maps.append(dict(base5, **{
            "x2own": cc(np.asarray(res3.results[c]["x2out"], f32)),
            "x2full": x2full[pair],
            "WkTh": WkTh, "WqTh": WqTh, "WvTh": WvTh,
            "maskTab": p3_host_meta(slot),
            "identb": identb_np,
        }))
    res4 = run_bass_kernel_spmd(p.nc, in_maps, core_ids=list(range(NCORES)))

    if _dump is not None:
        _dump["x3"] = {c: np.asarray(res4.results[c]["x3own"], f32)
                       for c in range(NCORES)}

    out = np.empty((4, S, D), f32)
    for c in range(NCORES):
        pair, slot = c // 2, c % 2
        out[pair][_own_idx(slot)] = np.asarray(res4.results[c]["x4"], f32)
    return out
